# version 02
# baseline (speedup 1.0000x reference)
import numpy as np
import os
from contextlib import ExitStack

import concourse.bass as bass
import concourse.mybir as mybir
from concourse.bass_utils import run_bass_kernel_spmd

F32 = mybir.dt.float32
BF16 = mybir.dt.bfloat16
AF = mybir.ActivationFunctionType
ALU = mybir.AluOpType
AX = mybir.AxisListType

C = 1024
T = 4096
PT = 8
TC = 512
NCH = T // TC
N_CORES = 8
LN_EPS = 1e-5
DEDUP_WAITS = True



class SemTrack:

    def __init__(self, nc, name):
        self.sem = nc.alloc_semaphore(name)
        self.name = name
        self.n = 0


N_DMA_SLOTS = int(os.environ.get('MK_DMA_SLOTS', 10))


class Stream:

    def __init__(self, bk, name):
        self.bk = bk
        self.name = name
        self.sem = bk.nc.alloc_semaphore(f"sem_{name}")
        self.n = 0
        self.code = []
        self.wmax = {}
        self.dma_pool = [SemTrack(bk.nc, f"dma_{name}{i}")
                         for i in range(N_DMA_SLOTS)]
        self.dma_i = 0
        self.mark_slots = ([SemTrack(bk.nc, f"mk_{name}{i}") for i in range(8)]
                           if name == 't' else None)
        self.mark_i = 0

    def force_wait(self, tok):
        st, v = tok
        self.code.append(('w', st.sem, v))
        self.wmax[st.name] = max(self.wmax.get(st.name, 0), v)

    def wait(self, tok):
        if tok is None:
            return
        st, v = tok
        if st is self and v > self.n:
            raise RuntimeError(f"wait on future own token {v} > {self.n}")
        if self.mark_slots is not None and st in self.mark_slots:
            if v > st.n:
                raise RuntimeError("wait on future slot token")
        if self.wmax.get(st.name, 0) >= v:
            if not DEDUP_WAITS:
                self.code.append(('w', st.sem, v))
            return
        self.wmax[st.name] = v
        self.code.append(('w', st.sem, v))

    def op(self, fn, after=(), mark=False, dma=False, drain=False):
        for t in after:
            self.wait(t)
        if drain and mark:
            self.code.append(('i', fn, None, 0))
            fn = lambda e: e.drain()
        elif self.mark_slots is not None and mark:
            self.code.append(('i', fn, None, 0))
            fn = lambda e: e.drain()
        if dma:
            tr = self.dma_pool[self.dma_i % N_DMA_SLOTS]
            self.dma_i += 1
            tr.n += 16
            self.code.append(('i', fn, tr.sem, 16))
            return (tr, tr.n)
        if mark:
            if self.mark_slots is not None:
                tr = self.mark_slots[self.mark_i % len(self.mark_slots)]
                self.mark_i += 1
                if tr.n:
                    self.wait((tr, tr.n))
                tr.n += 1
                self.code.append(('i', fn, tr.sem, 1))
                return (tr, tr.n)
            self.n += 1
            self.code.append(('i', fn, self.sem, 1))
            return (self, self.n)
        self.code.append(('i', fn, None, 0))
        return None


class Buf:

    __slots__ = ('ap', 'w', 'rs', 'name')

    def __init__(self, ap=None, name=""):
        self.ap = ap
        self.w = None
        self.rs = []
        self.name = name

    def dep_read(self):
        return [self.w]

    def dep_write(self):
        return self.rs + [self.w]


class BK:

    def __init__(self):
        self.nc = bass.Bass(
            "TRN2", target_bir_lowering=False,
            detect_race_conditions=bool(os.environ.get("MK_DETECT_RACES")))
        self.T = Stream(self, 't')
        self.V = Stream(self, 'v')
        self.S = Stream(self, 's')
        self.G = Stream(self, 'g')
        self.Y = Stream(self, 'y')
        self.streams = [self.T, self.V, self.S, self.G, self.Y]
        self.stack = None
        self.gstack = ExitStack()
        self._consts = {}
        self._cn = 0

    def scope(self):
        if self.stack is None:
            self.stack = ExitStack()
            self._scopes = [self.stack]
        else:
            self._scopes.append(ExitStack())
            self.stack = self._scopes[-1]

    def end_scope(self, barrier=True):
        if barrier:
            self.barrier()
        self._scopes.pop().close()
        self.stack = self._scopes[-1] if self._scopes else None

    def sbuf(self, name, shape, dtype, glob=False):
        if os.environ.get('MK_NOSCOPE'):
            return self.nc.alloc_sbuf_tensor(
                name, shape, dtype, side="right" if glob else "left").ap()
        stk = self.gstack if glob else self.stack
        h = stk.enter_context(self.nc.sbuf_tensor(
            name, shape, dtype, side="right" if glob else "left"))
        return h.ap()

    def const_col(self, value):
        if value not in self._consts:
            self._cn += 1
            b = Buf(self.sbuf(f"cc{self._cn}", [128, 1], F32, glob=True),
                    f"cc{self._cn}")
            self.do(self.V, lambda e, b=b, value=value: e.memset(b.ap[:],
                                                                 value), w=[b])
            self._consts[value] = b
        return self._consts[value]

    def psum(self, name, shape, dtype=F32):
        h = self.stack.enter_context(self.nc.psum_tensor(name, shape, dtype))
        return h.ap()

    def buf(self, name, shape, dtype):
        return Buf(self.sbuf(name, shape, dtype), name)

    def pbuf(self, name, shape, dtype=F32):
        return Buf(self.psum(name, shape, dtype), name)

    def do(self, st: Stream, fn, r=(), w=(), after=(), mark=True, dma=False,
           drain=None):
        if drain is None:
            drain = st is self.S or st is self.V
        deps = list(after)
        for b in r:
            deps += b.dep_read()
        for b in w:
            deps += b.dep_write()
        tok = st.op(fn, after=deps, mark=mark, dma=dma, drain=drain)
        if tok is not None:
            for b in r:
                b.rs.append(tok)
            for b in w:
                b.w = tok
                b.rs = []
        return tok

    def mm_group(self, fns, r=(), w=(), after=()):
        deps = list(after)
        for b in r:
            deps += b.dep_read()
        for b in w:
            deps += b.dep_write()
        for i, fn in enumerate(fns):
            last = (i == len(fns) - 1)
            tok = self.T.op(fn, after=deps if i == 0 else (), mark=last)
        for b in r:
            b.rs.append(tok)
        for b in w:
            b.w = tok
            b.rs = []
        return tok

    def barrier(self):
        for st in self.streams:
            for tr in st.dma_pool:
                if tr.n:
                    st.wait((tr, tr.n))
        toks = [st.op(lambda e: e.nop(), mark=True) for st in self.streams]
        for st in self.streams:
            for t in toks:
                st.wait(t)
        return toks

    def finalize(self):
        with self.nc.Block() as block:
            regs = ((self.T, block.tensor), (self.V, block.vector),
                    (self.S, block.scalar), (self.G, block.gpsimd),
                    (self.Y, block.sync))
            for st, reg in regs:
                def make(code):
                    def emit(eng):
                        for item in code:
                            if item[0] == 'w':
                                eng.wait_ge(item[1], item[2])
                            else:
                                inst = item[1](eng)
                                if item[2] is not None:
                                    inst.then_inc(item[2], item[3])
                    return emit
                reg(make(st.code))
        return self.nc


def chunk_view(hbm_ap, c0, tc):
    return hbm_ap.rearrange("(ft p) t -> p ft t", p=128)[:, :, c0:c0 + tc]


def sb3(ap, tc=TC):
    return ap.rearrange("p (ft t) -> p ft t", t=tc)



class LNScratch:

    def __init__(self, bk: BK, tag=""):
        self.sq = bk.buf(f"ln_sq{tag}", [128, PT * TC], BF16)
        self.mu = bk.buf(f"ln_mu{tag}", [128, TC], F32)
        self.msq = bk.buf(f"ln_msq{tag}", [128, TC], F32)
        self.var = bk.buf(f"ln_var{tag}", [128, TC], F32)
        self.r = bk.buf(f"ln_r{tag}", [128, TC], F32)
        self.b2 = bk.buf(f"ln_b2{tag}", [128, TC], F32)
        self.psA = bk.pbuf(f"ln_psA{tag}", [128, TC], F32)
        self.psB = bk.pbuf(f"ln_psB{tag}", [128, TC], F32)


def emit_ln_stats(bk: BK, ln: LNScratch, ones_b: Buf, src: Buf, tc=TC, nf=PT,
                  src_tiles=None):
    inv_n = 1.0 / (128 * nf)
    tiles = src_tiles or [src.ap[:, ft * tc:(ft + 1) * tc] for ft in range(nf)]
    for ft in range(nf):
        bk.do(bk.S, lambda e, ft=ft, t=tiles[ft]: e.activation(
            ln.sq.ap[:, ft * tc: (ft + 1) * tc], t, AF.Square),
            r=[src], w=[ln.sq] if ft == nf - 1 else (),
            after=ln.sq.dep_write() if ft == 0 else ())
    bk.mm_group([(lambda e, ft=ft, t=tiles[ft]: e.matmul(
        ln.psA.ap[:, 0:tc], ones_b.ap, t,
        start=(ft == 0), stop=(ft == nf - 1))) for ft in range(nf)],
        r=[src, ones_b], w=[ln.psA])
    bk.mm_group([(lambda e, ft=ft: e.matmul(
        ln.psB.ap[:, 0:tc], ones_b.ap,
        ln.sq.ap[:, ft * tc:(ft + 1) * tc],
        start=(ft == 0), stop=(ft == nf - 1))) for ft in range(nf)],
        r=[ln.sq, ones_b], w=[ln.psB])
    bk.do(bk.V, lambda e: e.tensor_scalar_mul(
        ln.mu.ap[:, 0:tc], ln.psA.ap[:, 0:tc], inv_n), r=[ln.psA], w=[ln.mu])
    bk.do(bk.V, lambda e: e.tensor_scalar_mul(
        ln.msq.ap[:, 0:tc], ln.psB.ap[:, 0:tc], inv_n), r=[ln.psB], w=[ln.msq])
    bk.do(bk.V, lambda e: e.tensor_tensor(
        ln.var.ap[:, 0:tc], ln.mu.ap[:, 0:tc], ln.mu.ap[:, 0:tc], ALU.mult),
        r=[ln.mu], w=[ln.var])
    bk.do(bk.V, lambda e: e.tensor_tensor(
        ln.var.ap[:, 0:tc], ln.msq.ap[:, 0:tc], ln.var.ap[:, 0:tc],
        ALU.subtract), r=[ln.msq, ln.var], w=[ln.var])
    epsb = bk.const_col(LN_EPS)
    bk.do(bk.S, lambda e: e.activation(
        ln.var.ap[:, 0:tc], ln.var.ap[:, 0:tc], AF.Ln, bias=epsb.ap[:, 0:1]),
        r=[epsb], w=[ln.var])
    bk.do(bk.S, lambda e: e.activation(
        ln.r.ap[:, 0:tc], ln.var.ap[:, 0:tc], AF.Exp, scale=-0.5),
        r=[ln.var], w=[ln.r], drain=True)
    bk.do(bk.V, lambda e: e.scalar_tensor_tensor(
        ln.b2.ap[:, 0:tc], ln.mu.ap[:, 0:tc], -1.0, ln.r.ap[:, 0:tc],
        ALU.mult, ALU.mult), r=[ln.mu, ln.r], w=[ln.b2])


def emit_ln_apply(bk: BK, ln: LNScratch, src: Buf, tmp: Buf, dst: Buf,
                  tc=TC, nf=PT, res: Buf = None, streams=None):
    sa, sbt, scs = streams or (bk.V, bk.V, bk.V)
    for ft in range(nf):
        sl = (slice(None), slice(ft * tc, (ft + 1) * tc))
        bk.do(sa, lambda e, sl=sl: e.tensor_tensor(
            tmp.ap[sl], src.ap[sl], ln.r.ap[:, 0:tc], ALU.mult),
            r=[src, ln.r], w=[tmp] if ft == nf - 1 else (),
            after=tmp.dep_write() if ft == 0 else ())
    if res is None:
        for ft in range(nf):
            sl = (slice(None), slice(ft * tc, (ft + 1) * tc))
            bk.do(sbt, lambda e, sl=sl: e.tensor_tensor(
                dst.ap[sl], tmp.ap[sl], ln.b2.ap[:, 0:tc], ALU.add),
                r=[tmp, ln.b2], w=[dst] if ft == nf - 1 else (),
                after=dst.dep_write() if ft == 0 else ())
    else:
        for ft in range(nf):
            sl = (slice(None), slice(ft * tc, (ft + 1) * tc))
            bk.do(sbt, lambda e, sl=sl: e.tensor_tensor(
                tmp.ap[sl], tmp.ap[sl], ln.b2.ap[:, 0:tc], ALU.add),
                r=[ln.b2, tmp], w=[tmp] if ft == nf - 1 else ())
        for ft in range(nf):
            sl = (slice(None), slice(ft * tc, (ft + 1) * tc))
            bk.do(scs, lambda e, sl=sl: e.tensor_tensor(
                dst.ap[sl], tmp.ap[sl], res.ap[sl], ALU.add),
                r=[tmp, res], w=[dst] if ft == nf - 1 else (),
                after=dst.dep_write() if ft == 0 else ())



def emit_l1_ffn(bk: BK, W, consts, x_in, x_out):
    nc = bk.nc
    bk.scope()
    ln = LNScratch(bk)
    ones_b = bk.buf("ones_b", [128, 128], BF16)
    bk.do(bk.V, lambda e: e.memset(ones_b.ap[:], 1.0), w=[ones_b])
    ident = bk.buf("identb", [128, 128], BF16)
    bk.do(bk.G, lambda e: e.dma_start(out=ident.ap[:], in_=consts['ident'][:]),
          w=[ident], dma=True)

    g1r = bk.buf("g1r", [4, 1024], BF16)
    g2r = bk.buf("g2r", [4, 256], BF16)
    g1i = bk.buf("g1i", [4, 1024], BF16)
    g2i = bk.buf("g2i", [4, 256], BF16)
    for b, key, rr in ((g1r, 'L1_ffn_G1_R', 1), (g2r, 'L1_ffn_G2_R', 2),
                       (g1i, 'L1_ffn_G1_I', 1), (g2i, 'L1_ffn_G2_I', 2)):
        if rr == 1:
            srcv = W[key][0].rearrange("b c d -> c b d")
            dstv = b.ap.rearrange("c (b d) -> c b d", d=32)
        else:
            srcv = W[key][:, :, 0, :]
            dstv = b.ap.rearrange("c (e g) -> c e g", g=16)
        bk.do(bk.G, lambda e, srcv=srcv, dstv=dstv: e.dma_start(
            out=dstv, in_=srcv), w=[b], dma=True)

    vhbm = {k: Buf(nc.dram_tensor(f"l1f_v_{k}", [1024, 256], BF16,
                                  kind="Internal").ap(), f"vhbm{k}")
            for k in 'ri'}
    vps = bk.pbuf("l1f_vps", [128, 256], F32)
    vsb = bk.buf("l1f_vsb", [128, 256], BF16)
    w_sb = {'r': bk.buf("l1f_wr", [128, 4 * 512], BF16),
            'i': bk.buf("l1f_wi", [128, 4 * 512], BF16)}
    wt = {'r': bk.buf("l1f_wtr", [128, 4 * 512], BF16),
          'i': bk.buf("l1f_wti", [128, 4 * 512], BF16),
          'in': bk.buf("l1f_wtin", [128, 4 * 512], BF16)}
    tps = bk.pbuf("l1f_tps", [128, 128], BF16)

    for key, g1, g2 in (('r', g1r, g2r), ('i', g1i, g2i)):
        for mc in range(8):
            bk.mm_group([lambda e, g1=g1, mc=mc, g2=g2: e.matmul(
                vps.ap[:], g1.ap[:, mc * 128:(mc + 1) * 128], g2.ap[:],
                start=True, stop=True)], r=[g1, g2], w=[vps])
            bk.do(bk.V, lambda e: e.tensor_copy(vsb.ap[:], vps.ap[:]),
                  r=[vps], w=[vsb])
            bk.do(bk.Y, lambda e, key=key, mc=mc: e.dma_start(
                out=vhbm[key].ap[mc * 128:(mc + 1) * 128, :], in_=vsb.ap[:]),
                r=[vsb], w=[vhbm[key]] if mc == 7 else (), dma=True,
                after=vhbm[key].dep_write() if mc == 0 else ())
    for key in 'ri':
        for p4 in range(4):
            for bl in range(8):
                b = p4 * 8 + bl
                src = vhbm[key].ap[b * 32:(b + 1) * 32, :].rearrange(
                    "d (e g) -> e d g", g=16)
                bk.do(bk.Y, lambda e, key=key, p4=p4, bl=bl, src=src:
                      e.dma_start(out=w_sb[key].ap[
                          16 * bl:16 * (bl + 1),
                          p4 * 512:(p4 + 1) * 512].rearrange(
                          "p (d g) -> p d g", g=16),
                          in_=src),
                      r=[vhbm[key]], w=[w_sb[key]] if (p4, bl) == (3, 7) else (),
                      dma=True,
                      after=w_sb[key].dep_write() if (p4, bl) == (0, 0) else ())
    for key in 'ri':
        for i in range(4):
            for j in range(4):
                bk.mm_group([lambda e, key=key, i=i, j=j: e.transpose(
                    tps.ap[:], w_sb[key].ap[:, i * 512 + j * 128:
                                            i * 512 + (j + 1) * 128],
                    ident.ap[:])], r=[w_sb[key], ident], w=[tps])
                bk.do(bk.V, lambda e, key=key, i=i, j=j: e.tensor_copy(
                    wt[key].ap[:, j * 512 + i * 128: j * 512 + (i + 1) * 128],
                    tps.ap[:]), r=[tps],
                    w=[wt[key]] if (i, j) == (3, 3) else (),
                    after=wt[key].dep_write() if (i, j) == (0, 0) else ())
    bk.do(bk.V, lambda e: e.tensor_scalar_mul(wt['in'].ap[:], wt['i'].ap[:],
                                              -1.0), r=[wt['i']], w=[wt['in']])

    biasc = bk.buf("l1f_bias", [128, 8], F32)
    for rt in range(4):
        for cc, key in ((rt, 'L1_ffn_bias_real'), (4 + rt, 'L1_ffn_bias_imag')):
            bk.do(bk.Y, lambda e, cc=cc, key=key, rt=rt: e.dma_start(
                out=biasc.ap[:, cc:cc + 1],
                in_=W[key][rt * 128:(rt + 1) * 128].unsqueeze(1)),
                w=[biasc] if cc == 7 else (), dma=True,
                after=biasc.dep_write() if cc == 0 else ())

    xc = bk.buf("l1f_xc", [128, PT * TC], F32)
    xcb = bk.buf("l1f_xcb", [128, PT * TC], BF16)
    xnb = bk.buf("l1f_xnb", [128, PT * TC], BF16)
    bo_f = bk.buf("l1f_bof", [128, PT * TC], F32)
    bo_b = bk.buf("l1f_bob", [128, PT * TC], BF16)
    tmp = bk.buf("l1f_tmp", [128, PT * TC], F32)
    mm_ps = [bk.pbuf(f"l1f_mm{i}", [128, TC], F32) for i in range(2)]

    out_toks = []
    for ch in range(NCH):
        c0 = ch * TC
        bk.do(bk.Y, lambda e, c0=c0: e.dma_start(
            out=sb3(xc.ap), in_=chunk_view(x_in, c0, TC)), w=[xc], dma=True)
        for ft in range(PT):
            sl = (slice(None), slice(ft * TC, (ft + 1) * TC))
            bk.do(bk.V, lambda e, sl=sl: e.tensor_copy(xcb.ap[sl], xc.ap[sl]),
                  r=[xc], w=[xcb] if ft == PT - 1 else (),
                  after=xcb.dep_write() if ft == 0 else ())
        emit_ln_stats(bk, ln, ones_b, xcb)
        emit_ln_apply(bk, ln, xc, tmp, xnb)
        for rt in range(8):
            ps = mm_ps[rt % 2]
            fns = []
            if rt < 4:
                ka, kb, sl_r = wt['r'], wt['in'], rt
            else:
                ka, kb, sl_r = wt['i'], wt['r'], rt - 4
            for kt in range(4):
                fns.append(lambda e, kt=kt, ka=ka, sl_r=sl_r, ps=ps: e.matmul(
                    ps.ap[:], ka.ap[:, kt * 512 + sl_r * 128:
                                    kt * 512 + (sl_r + 1) * 128],
                    xnb.ap[:, kt * TC:(kt + 1) * TC],
                    start=(kt == 0), stop=False))
            for kt in range(4):
                fns.append(lambda e, kt=kt, kb=kb, sl_r=sl_r, ps=ps: e.matmul(
                    ps.ap[:], kb.ap[:, kt * 512 + sl_r * 128:
                                    kt * 512 + (sl_r + 1) * 128],
                    xnb.ap[:, (4 + kt) * TC:(5 + kt) * TC],
                    start=False, stop=(kt == 3)))
            bk.mm_group(fns, r=[wt['r'], wt['in'], wt['i'], xnb], w=[ps])
            sl = (slice(None), slice(rt * TC, (rt + 1) * TC))
            bk.do(bk.V, lambda e, sl=sl, rt=rt, ps=ps: e.tensor_scalar_add(
                bo_f.ap[sl], ps.ap[:], biasc.ap[:, rt:rt + 1]),
                r=[ps, biasc], w=[bo_f] if rt == 7 else (),
                after=bo_f.dep_write() if rt == 0 else ())
        for ft in range(PT):
            sl = (slice(None), slice(ft * TC, (ft + 1) * TC))
            bk.do(bk.V, lambda e, sl=sl: e.tensor_copy(bo_b.ap[sl],
                                                       bo_f.ap[sl]),
                  r=[bo_f], w=[bo_b] if ft == PT - 1 else (),
                  after=bo_b.dep_write() if ft == 0 else ())
        emit_ln_stats(bk, ln, ones_b, bo_b)
        emit_ln_apply(bk, ln, bo_f, tmp, tmp, res=xc,
                      streams=(bk.V, bk.V, bk.V))
        out_toks.append(bk.do(bk.Y, lambda e, c0=c0: e.dma_start(
            out=chunk_view(x_out, c0, TC), in_=sb3(tmp.ap)),
            r=[tmp, xc], dma=True))
    bk.end_scope()
    return out_toks





def load_cols(bk: BK, name, vec_hbm, n, q=None):
    dst = bk.buf(name, [128, n], F32)
    q = q or bk.Y
    for i in range(n):
        bk.do(q, lambda e, i=i: e.dma_start(
            out=dst.ap[:, i:i + 1],
            in_=vec_hbm[i * 128:(i + 1) * 128].unsqueeze(1)),
            w=[dst] if i == n - 1 else (), dma=True,
            after=dst.dep_write() if i == 0 else ())
    return dst


def load_conv_cols(bk: BK, name, w_hbm, b_hbm):
    dst = bk.buf(name, [128, 4 * PT], F32)
    first = True
    for k in range(3):
        for ft in range(PT):
            bk.do(bk.Y, lambda e, k=k, ft=ft: e.dma_start(
                out=dst.ap[:, k * PT + ft:k * PT + ft + 1],
                in_=w_hbm[ft * 128:(ft + 1) * 128, 0, k].unsqueeze(1),
                allow_slow_non_contiguous=True),
                dma=True, after=dst.dep_write() if first else ())
            first = False
    for ft in range(PT):
        bk.do(bk.Y, lambda e, ft=ft: e.dma_start(
            out=dst.ap[:, 3 * PT + ft:3 * PT + ft + 1],
            in_=b_hbm[ft * 128:(ft + 1) * 128].unsqueeze(1)),
            w=[dst] if ft == PT - 1 else (), dma=True)
    return dst


def load_lhsT(bk: BK, name, w_hbm, kt_n, n_cols):
    dst = bk.buf(name, [128, kt_n * n_cols], BF16)
    for i in range(kt_n):
        bk.do(bk.G, lambda e, i=i: e.dma_start(
            out=dst.ap[:, i * n_cols:(i + 1) * n_cols],
            in_=w_hbm[i * 128:(i + 1) * 128, :]),
            w=[dst] if i == kt_n - 1 else (), dma=True,
            after=dst.dep_write() if i == 0 else ())
    return dst


def load_halo(bk: BK, dst: Buf, src_hbm, c0, tcx):
    W2 = tcx + 2
    d3 = dst.ap.rearrange("p (ft t) -> p ft t", t=W2)
    a, b = max(c0 - 1, 0), min(c0 + tcx + 1, T)
    lo = 1 - (c0 - a)
    if c0 == 0:
        bk.do(bk.V, lambda e: e.memset(d3[:, :, 0:1], 0.0),
              w=(), after=dst.dep_write())
    if c0 + tcx == T:
        bk.do(bk.V, lambda e: e.memset(d3[:, :, W2 - 1:W2], 0.0),
              w=(), after=dst.dep_write())
    return bk.do(bk.Y, lambda e: e.dma_start(
        out=d3[:, :, lo:lo + (b - a)], in_=chunk_view(src_hbm, a, b - a)),
        w=[dst], dma=True)


def emit_l0_attn(bk: BK, W, consts, x_in, x_out, dbg=None):
    nc = bk.nc
    TCB = 512
    NCB = T // TCB
    H2 = TCB + 2
    bk.scope()
    ln = LNScratch(bk)
    ones_b = bk.buf("ones_m", [128, 128], BF16)
    bk.do(bk.V, lambda e: e.memset(ones_b.ap[:], 1.0), w=[ones_b])
    ident = bk.buf("identm", [128, 128], BF16)
    bk.do(bk.G, lambda e: e.dma_start(out=ident.ap[:], in_=consts['ident'][:]),
          w=[ident], dma=True)
    ind64 = bk.buf("ind64b", [128, 128], BF16)
    bk.do(bk.G, lambda e: e.dma_start(out=ind64.ap[:], in_=consts['ind64'][:]),
          w=[ind64], dma=True)
    kv_b = bk.buf("m_kvb", [128, 8 * 128], BF16)
    m2 = bk.buf("m_m2", [128, PT * 128], BF16)
    kmean = bk.buf("m_kmean", [128, PT], F32)
    def hb(n, dt=F32, key=None):
        if dbg and key in dbg:
            return Buf(dbg[key], n)
        return Buf(nc.dram_tensor(n, [C, T], dt, kind="Internal").ap(), n)
    xn_h, y_h = hb("m_xnh", key='xn'), hb("m_yh", key='y')
    xi0_h = hb("m_xi0h", key='xi0')
    act_h = hb("m_acth", BF16, key='act')
    q_h = Buf(xn_h.ap.bitcast(BF16)[:, 0:T], "m_qh")
    y2_h = Buf(xi0_h.ap, "m_y2h")
    if dbg and 'y2' in dbg:
        y2_h = Buf(dbg['y2'], "m_y2h")

    def conv_tile(dstb, dsl, srcb, sb, cw, ft, tcx, w1p=None, bias=True,
                  last=False):
        w1c = (w1p.ap[:, ft:ft + 1] if w1p is not None
               else cw.ap[:, PT + ft:PT + ft + 1])
        bc = cw.ap[:, 3 * PT + ft:3 * PT + ft + 1]
        rd = [srcb, cw] + ([w1p] if w1p is not None else [])
        bk.do(bk.V, lambda e: e.tensor_scalar(
            dstb.ap[:, dsl:dsl + tcx], srcb.ap[:, sb + 1:sb + 1 + tcx],
            w1c, bc, ALU.mult, ALU.add), r=rd,
            w=(), after=dstb.dep_write() if dsl == 0 else ())
        bk.do(bk.V, lambda e: e.scalar_tensor_tensor(
            dstb.ap[:, dsl:dsl + tcx], srcb.ap[:, sb:sb + tcx],
            cw.ap[:, 0 * PT + ft:0 * PT + ft + 1],
            dstb.ap[:, dsl:dsl + tcx], ALU.mult, ALU.add), r=rd)
        bk.do(bk.V, lambda e: e.scalar_tensor_tensor(
            dstb.ap[:, dsl:dsl + tcx], srcb.ap[:, sb + 2:sb + 2 + tcx],
            cw.ap[:, 2 * PT + ft:2 * PT + ft + 1],
            dstb.ap[:, dsl:dsl + tcx], ALU.mult, ALU.add), r=rd,
            w=[dstb] if last else ())

    bk.scope()
    xi_b = bk.buf("m_xib", [128, PT * (T + 2)], BF16)
    x3 = xi_b.ap.rearrange("p (ft t) -> p ft t", t=T + 2)
    bk.do(bk.V, lambda e: e.memset(x3[:, :, 0:1], 0.0), w=())
    bk.do(bk.V, lambda e: e.memset(x3[:, :, T + 1:T + 2], 0.0), w=())

    TCA, H2A = 256, 258
    NCA = T // TCA
    bk.scope()
    cpe1c = load_conv_cols(bk, "m_cpe1", W['L0_attn_cpe1_w'],
                           W['L0_attn_cpe1_b'])
    cpe1p = bk.buf("m_cpe1p", [128, PT], F32)
    bk.do(bk.V, lambda e: e.tensor_scalar_add(
        cpe1p.ap[:], cpe1c.ap[:, PT:2 * PT], 1.0), r=[cpe1c], w=[cpe1p])
    dwcc = load_conv_cols(bk, "m_dwc", W['L0_attn_dwc_w'], W['L0_attn_dwc_b'])
    n1g = load_cols(bk, "m_n1g", W['L0_attn_norm1_g'], PT)
    n1b = load_cols(bk, "m_n1b", W['L0_attn_norm1_b'], PT)
    actw = load_lhsT(bk, "m_actw", W['L0_attn_actp_w'], PT, 1024)
    actb = load_cols(bk, "m_actb", W['L0_attn_actp_b'], PT)
    inw = load_lhsT(bk, "m_inw", W['L0_attn_in_w'], PT, 1024)
    inb = load_cols(bk, "m_inb", W['L0_attn_in_b'], PT)
    xc = bk.buf("m_xc", [128, PT * TCA], F32)
    xcb = bk.buf("m_xcb", [128, PT * TCA], BF16)
    tmpA = bk.buf("m_tmpA", [128, PT * TCA], F32)
    xnf = bk.buf("m_xnf", [128, PT * TCA], F32)
    mmA = [bk.pbuf(f"m_mmA{i}", [128, TCA], F32) for i in range(2)]
    for ch in range(NCA):
        c0 = ch * TCA
        bk.do(bk.Y, lambda e, c0=c0: e.dma_start(
            out=sb3(xc.ap), in_=chunk_view(x_in, c0, TCA)), w=[xc], dma=True)
        for ft in range(PT):
            sl = (slice(None), slice(ft * TCA, (ft + 1) * TCA))
            bk.do(bk.V, lambda e, sl=sl: e.tensor_copy(xcb.ap[sl], xc.ap[sl]),
                  r=[xc], w=[xcb] if ft == PT - 1 else (),
                  after=xcb.dep_write() if ft == 0 else ())
        emit_ln_stats(bk, ln, ones_b, xcb, tc=TCA)
        emit_ln_apply(bk, ln, xc, tmpA, xnf, tc=TCA)
        bk.do(bk.Y, lambda e, c0=c0: e.dma_start(
            out=chunk_view(xn_h.ap, c0, TCA), in_=sb3(xnf.ap)),
            r=[xnf], w=[xn_h] if ch == NCA - 1 else (), dma=True,
            after=xn_h.dep_write() if ch == 0 else ())
    xh = bk.buf("m_xh", [128, PT * H2A], F32)
    yA = bk.buf("m_yA", [128, PT * TCA], F32)
    yb = bk.buf("m_yb", [128, PT * TCA], BF16)
    ynb = bk.buf("m_ynb", [128, PT * TCA], BF16)
    acb = bk.buf("m_acb", [128, PT * TCA], BF16)
    for ch in range(NCA):
        c0 = ch * TCA
        load_halo(bk, xh, xn_h.ap, c0, TCA)
        for ft in range(PT):
            conv_tile(yA, ft * TCA, xh, ft * H2A, cpe1c, ft, TCA,
                      w1p=cpe1p, last=(ft == PT - 1))
        bk.do(bk.Y, lambda e, c0=c0: e.dma_start(
            out=chunk_view(y_h.ap, c0, TCA), in_=sb3(yA.ap)),
            r=[yA], w=[y_h] if ch == NCA - 1 else (), dma=True,
            after=y_h.dep_write() if ch == 0 else ())
        for ft in range(PT):
            sl = (slice(None), slice(ft * TCA, (ft + 1) * TCA))
            bk.do(bk.V, lambda e, sl=sl: e.tensor_copy(yb.ap[sl], yA.ap[sl]),
                  r=[yA], w=[yb] if ft == PT - 1 else (),
                  after=yb.dep_write() if ft == 0 else ())
        emit_ln_stats(bk, ln, ones_b, yb, tc=TCA)
        for ft in range(PT):
            sl = (slice(None), slice(ft * TCA, (ft + 1) * TCA))
            bk.do(bk.V, lambda e, sl=sl: e.tensor_tensor(
                tmpA.ap[sl], yA.ap[sl], ln.r.ap[:, 0:TCA], ALU.mult),
                r=[yA, ln.r], w=[tmpA] if ft == PT - 1 else (),
                after=tmpA.dep_write() if ft == 0 else ())
        for ft in range(PT):
            sl = (slice(None), slice(ft * TCA, (ft + 1) * TCA))
            bk.do(bk.V, lambda e, sl=sl: e.tensor_tensor(
                tmpA.ap[sl], tmpA.ap[sl], ln.b2.ap[:, 0:TCA], ALU.add),
                r=[ln.b2, tmpA], w=[tmpA] if ft == PT - 1 else ())
        for ft in range(PT):
            sl = (slice(None), slice(ft * TCA, (ft + 1) * TCA))
            bk.do(bk.V, lambda e, sl=sl, ft=ft: e.tensor_scalar(
                ynb.ap[sl], tmpA.ap[sl], n1g.ap[:, ft:ft + 1],
                n1b.ap[:, ft:ft + 1], ALU.mult, ALU.add),
                r=[tmpA, n1g, n1b], w=[ynb] if ft == PT - 1 else (),
                after=ynb.dep_write() if ft == 0 else ())
        for fo in range(PT):
            ps = mmA[fo % 2]
            bk.mm_group([(lambda e, fo=fo, kt=kt, ps=ps: e.matmul(
                ps.ap[:], actw.ap[:, kt * 1024 + fo * 128:
                                 kt * 1024 + (fo + 1) * 128],
                ynb.ap[:, kt * TCA:(kt + 1) * TCA],
                start=(kt == 0), stop=(kt == PT - 1))) for kt in range(PT)],
                r=[actw, ynb], w=[ps])
            bk.do(bk.S, lambda e, fo=fo, ps=ps: e.activation(
                acb.ap[:, fo * TCA:(fo + 1) * TCA], ps.ap[:], AF.Silu,
                bias=actb.ap[:, fo:fo + 1]),
                r=[ps, actb], w=[acb] if fo == PT - 1 else (),
                after=acb.dep_write() if fo == 0 else ())
        bk.do(bk.Y, lambda e, c0=c0: e.dma_start(
            out=chunk_view(act_h.ap, c0, TCA), in_=sb3(acb.ap)),
            r=[acb], w=[act_h] if ch == NCA - 1 else (), dma=True,
            after=act_h.dep_write() if ch == 0 else ())
        for fo in range(PT):
            ps = mmA[fo % 2]
            bk.mm_group([(lambda e, fo=fo, kt=kt, ps=ps: e.matmul(
                ps.ap[:], inw.ap[:, kt * 1024 + fo * 128:
                                kt * 1024 + (fo + 1) * 128],
                ynb.ap[:, kt * TCA:(kt + 1) * TCA],
                start=(kt == 0), stop=(kt == PT - 1))) for kt in range(PT)],
                r=[inw, ynb], w=[ps])
            bk.do(bk.V, lambda e, fo=fo, ps=ps: e.tensor_scalar_add(
                xnf.ap[:, fo * TCA:(fo + 1) * TCA], ps.ap[:],
                inb.ap[:, fo:fo + 1]),
                r=[ps, inb], w=[xnf] if fo == PT - 1 else (),
                after=xnf.dep_write() if fo == 0 else ())
        bk.do(bk.Y, lambda e, c0=c0: e.dma_start(
            out=chunk_view(xi0_h.ap, c0, TCA), in_=sb3(xnf.ap)),
            r=[xnf], w=[xi0_h] if ch == NCA - 1 else (), dma=True,
            after=xi0_h.dep_write() if ch == 0 else ())
    for ch in range(NCA):
        c0 = ch * TCA
        load_halo(bk, xh, xi0_h.ap, c0, TCA)
        for ft in range(PT):
            conv_tile(tmpA, ft * TCA, xh, ft * H2A, dwcc, ft, TCA,
                      last=(ft == PT - 1))
        for ft in range(PT):
            bk.do(bk.S, lambda e, ft=ft, c0=c0: e.activation(
                xi_b.ap[:, ft * (T + 2) + 1 + c0:
                        ft * (T + 2) + 1 + c0 + TCA],
                tmpA.ap[:, ft * TCA:(ft + 1) * TCA], AF.Silu),
                r=[tmpA], w=[xi_b] if (ch, ft) == (NCA - 1, PT - 1) else (),
                after=xi_b.dep_write() if (ch, ft) == (0, 0) else ())
    bk.end_scope()

    bk.scope()
    qkw = load_lhsT(bk, "m_qkw", W['L0_attn_qk_w'], PT, 2048)
    qkb = load_cols(bk, "m_qkb", W['L0_attn_qk_b'], 16)
    q_ch = bk.buf("m_qch", [128, PT * TCB], BF16)
    k_ch = bk.buf("m_kch", [128, PT * TCB], BF16)
    e1 = bk.buf("m_e1", [128, TCB], F32)
    e2 = bk.buf("m_e2", [128, TCB], F32)
    e3 = bk.buf("m_e3", [128, TCB], F32)
    red = bk.buf("m_red", [128, 1], F32)
    ktm = bk.buf("m_ktm", [128, 4 * 1024], BF16)
    xitm = bk.buf("m_xitm", [128, 4 * 1024], BF16)
    mmB = [bk.pbuf(f"m_mmB{i}", [128, TCB], F32) for i in range(2)]
    kvps = bk.pbuf("m_kvps", [128, 512], F32)
    kvacc = bk.buf("m_kvacc", [128, 512], F32)
    tpsb = bk.pbuf("m_tpsb", [128, 128], BF16)
    bk.do(bk.V, lambda e: e.memset(kmean.ap[:], 0.0), w=[kmean])
    for ch in range(NCB):
        c0 = ch * TCB
        for fo2 in range(16):
            ps = mmB[fo2 % 2]
            bk.mm_group([(lambda e, fo2=fo2, kt=kt, ps=ps, c0=c0: e.matmul(
                ps.ap[:], qkw.ap[:, kt * 2048 + fo2 * 128:
                                kt * 2048 + (fo2 + 1) * 128],
                xi_b.ap[:, kt * (T + 2) + 1 + c0:
                        kt * (T + 2) + 1 + c0 + TCB],
                start=(kt == 0), stop=(kt == PT - 1))) for kt in range(PT)],
                r=[qkw, xi_b], w=[ps])
            bk.do(bk.V, lambda e, fo2=fo2, ps=ps: e.tensor_scalar(
                e1.ap[:], ps.ap[:], qkb.ap[:, fo2:fo2 + 1], 0.0,
                ALU.add, ALU.min), r=[ps, qkb], w=[e1])
            bk.do(bk.S, lambda e: e.activation(e2.ap[:], e1.ap[:], AF.Exp),
                  r=[e1], w=[e2])
            bk.do(bk.V, lambda e, fo2=fo2, ps=ps: e.tensor_scalar(
                e3.ap[:], ps.ap[:], qkb.ap[:, fo2:fo2 + 1], 0.0,
                ALU.add, ALU.max), r=[ps, qkb], w=[e3])
            dst, dc = (q_ch, fo2) if fo2 < 8 else (k_ch, fo2 - 8)
            bk.do(bk.V, lambda e, dst=dst, dc=dc: e.tensor_tensor(
                dst.ap[:, dc * TCB:(dc + 1) * TCB], e2.ap[:], e3.ap[:],
                ALU.add), r=[e2, e3], w=[dst] if dc == PT - 1 else (),
                after=dst.dep_write() if dc == 0 else ())
        bk.do(bk.Y, lambda e, c0=c0: e.dma_start(
            out=chunk_view(q_h.ap, c0, TCB), in_=sb3(q_ch.ap)),
            r=[q_ch], w=[q_h] if ch == NCB - 1 else (), dma=True,
            after=q_h.dep_write() if ch == 0 else ())
        if dbg and 'q' in dbg:
            bk.do(bk.Y, lambda e, c0=c0: e.dma_start(
                out=chunk_view(dbg['q'], c0, TCB), in_=sb3(q_ch.ap)),
                r=[q_ch], dma=True)
        for ft in range(PT):
            bk.do(bk.V, lambda e, ft=ft: e.reduce_sum(
                red.ap[:], k_ch.ap[:, ft * TCB:(ft + 1) * TCB], axis=AX.X),
                r=[k_ch], w=[red])
            bk.do(bk.V, lambda e, ft=ft: e.tensor_tensor(
                kmean.ap[:, ft:ft + 1], kmean.ap[:, ft:ft + 1], red.ap[:],
                ALU.add), r=[red], w=[kmean])
        for src, dst, ha in ((k_ch, ktm, False), (None, xitm, True)):
            for ft in range(PT):
                for tt in range(4):
                    if ha:
                        sap = xi_b.ap[:, ft * (T + 2) + 1 + c0 + tt * 128:
                                      ft * (T + 2) + 1 + c0 + (tt + 1) * 128]
                        rsrc = xi_b
                    else:
                        sap = src.ap[:, ft * TCB + tt * 128:
                                     ft * TCB + (tt + 1) * 128]
                        rsrc = src
                    bk.mm_group([lambda e, sap=sap: e.transpose(
                        tpsb.ap[:], sap, ident.ap[:])],
                        r=[rsrc, ident], w=[tpsb])
                    bk.do(bk.V, lambda e, ft=ft, tt=tt, dst=dst: e.tensor_copy(
                        dst.ap[:, tt * 1024 + ft * 128:
                               tt * 1024 + (ft + 1) * 128], tpsb.ap[:]),
                        r=[tpsb], w=[dst] if (ft, tt) == (PT - 1, 3) else (),
                        after=dst.dep_write() if (ft, tt) == (0, 0) else ())
        fns = []
        for h in range(16):
            pb, cb = 64 * (h % 2), 64 * (h // 2)
            for tt in range(4):
                fns.append(lambda e, h=h, tt=tt, pb=pb, cb=cb: e.matmul(
                    kvps.ap[pb:pb + 64, cb:cb + 64],
                    ktm.ap[:, tt * 1024 + h * 64:tt * 1024 + (h + 1) * 64],
                    xitm.ap[:, tt * 1024 + h * 64:tt * 1024 + (h + 1) * 64],
                    start=(h <= 1 and tt == 0), stop=(h == 15 and tt == 3),
                    skip_group_check=True))
        bk.mm_group(fns, r=[ktm, xitm], w=[kvps])
        if ch == 0:
            bk.do(bk.V, lambda e: e.tensor_copy(kvacc.ap[:], kvps.ap[:]),
                  r=[kvps], w=[kvacc])
        else:
            bk.do(bk.V, lambda e: e.tensor_tensor(
                kvacc.ap[:], kvacc.ap[:], kvps.ap[:], ALU.add),
                r=[kvps], w=[kvacc])
    bk.do(bk.V, lambda e: e.memset(kv_b.ap[:], 0.0), w=[kv_b])
    for h in range(16):
        pt, par = divmod(h, 2)
        pb, cb = 64 * par, 64 * (h // 2)
        db = pt * 128 + 64 * par
        bk.do(bk.V, lambda e, pb=pb, cb=cb, db=db: e.tensor_scalar_mul(
            kv_b.ap[pb:pb + 64, db:db + 64],
            kvacc.ap[pb:pb + 64, cb:cb + 64], 1.0 / T),
            r=[kvacc], w=[kv_b] if h == 15 else ())
    bk.do(bk.V, lambda e: e.tensor_scalar_mul(kmean.ap[:], kmean.ap[:],
                                              1.0 / T), w=[kmean])
    for ft in range(PT):
        bk.do(bk.V, lambda e, ft=ft: e.tensor_scalar_mul(
            m2.ap[:, ft * 128:(ft + 1) * 128], ind64.ap[:],
            kmean.ap[:, ft:ft + 1]), r=[ind64, kmean],
            w=[m2] if ft == PT - 1 else (),
            after=m2.dep_write() if ft == 0 else ())
    bk.end_scope()

    bk.scope()
    outw = load_lhsT(bk, "m_outw", W['L0_attn_out_w'], PT, 1024)
    outb = load_cols(bk, "m_outb", W['L0_attn_out_b'], PT)
    lepc = load_conv_cols(bk, "m_lep", W['L0_attn_lepe_w'],
                          W['L0_attn_lepe_b'])
    qC = bk.buf("m_qch2", [128, PT * TCB], BF16)
    a_ch = bk.buf("m_ach2", [128, PT * TCB], BF16)
    y_ch = bk.buf("m_ych", [128, PT * TCB], F32)
    zr = bk.buf("m_zr", [128, PT * TCB], F32)
    zden = bk.buf("m_zden", [128, TCB], F32)
    lt = bk.buf("m_lt", [128, TCB], F32)
    af = bk.buf("m_af", [128, TCB], F32)
    af0 = bk.buf("m_af0", [128, TCB], F32)
    gat = bk.buf("m_gat", [128, PT * TCB], BF16)
    y2 = bk.buf("m_y2", [128, PT * TCB], F32)
    mmC = [bk.pbuf(f"m_mmC{i}", [128, TCB], F32) for i in range(2)]
    for ch in range(NCB):
        c0 = ch * TCB
        bk.do(bk.Y, lambda e, c0=c0: e.dma_start(
            out=sb3(qC.ap), in_=chunk_view(q_h.ap, c0, TCB)),
            r=[q_h], w=[qC], dma=True)
        bk.do(bk.Y, lambda e, c0=c0: e.dma_start(
            out=sb3(a_ch.ap), in_=chunk_view(act_h.ap, c0, TCB)),
            r=[act_h], w=[a_ch], dma=True)
        bk.do(bk.Y, lambda e, c0=c0: e.dma_start(
            out=sb3(y_ch.ap), in_=chunk_view(y_h.ap, c0, TCB)),
            r=[y_h], w=[y_ch], dma=True)
        for ft in range(PT):
            ps = mmC[ft % 2]
            bk.mm_group([lambda e, ft=ft, ps=ps: e.matmul(
                ps.ap[:], m2.ap[:, ft * 128:(ft + 1) * 128],
                qC.ap[:, ft * TCB:(ft + 1) * TCB],
                start=True, stop=True)], r=[m2, qC], w=[ps])
            bk.do(bk.V, lambda e, ps=ps: e.tensor_scalar_add(
                zden.ap[:], ps.ap[:], 1e-6), r=[ps], w=[zden], drain=True)
            bk.do(bk.V, lambda e, ft=ft: e.reciprocal(
                zr.ap[:, ft * TCB:(ft + 1) * TCB], zden.ap[:]),
                r=[zden], w=[zr] if ft == PT - 1 else (), drain=True,
                after=zr.dep_write() if ft == 0 else ())
        bk.V.force_wait(zr.w)
        for ft in range(PT):
            ps = mmC[ft % 2]
            bk.mm_group([lambda e, ft=ft, ps=ps: e.matmul(
                ps.ap[:], kv_b.ap[:, ft * 128:(ft + 1) * 128],
                qC.ap[:, ft * TCB:(ft + 1) * TCB],
                start=True, stop=True)], r=[kv_b, qC], w=[ps])
            sb = ft * (T + 2) + c0
            bk.do(bk.V, lambda e, ft=ft, sb=sb: e.tensor_scalar(
                lt.ap[:], xi_b.ap[:, sb + 1:sb + 1 + TCB],
                lepc.ap[:, PT + ft:PT + ft + 1],
                lepc.ap[:, 3 * PT + ft:3 * PT + ft + 1], ALU.mult, ALU.add),
                r=[xi_b, lepc], w=[lt])
            bk.do(bk.V, lambda e, ft=ft, sb=sb: e.scalar_tensor_tensor(
                lt.ap[:], xi_b.ap[:, sb:sb + TCB],
                lepc.ap[:, ft:ft + 1], lt.ap[:], ALU.mult, ALU.add),
                r=[xi_b, lepc, lt], w=[lt])
            bk.do(bk.V, lambda e, ft=ft, sb=sb: e.scalar_tensor_tensor(
                lt.ap[:], xi_b.ap[:, sb + 2:sb + 2 + TCB],
                lepc.ap[:, 2 * PT + ft:2 * PT + ft + 1], lt.ap[:],
                ALU.mult, ALU.add), r=[xi_b, lepc, lt], w=[lt])
            bk.do(bk.S, lambda e, ps=ps: e.copy(af0.ap[:], ps.ap[:]),
                  r=[ps], w=[af0])
            bk.do(bk.V, lambda e, ft=ft: e.tensor_tensor(
                af.ap[:], af0.ap[:], zr.ap[:, ft * TCB:(ft + 1) * TCB],
                ALU.mult), r=[af0, zr], w=[af])
            bk.do(bk.V, lambda e: e.tensor_tensor(
                af.ap[:], af.ap[:], lt.ap[:], ALU.add), r=[lt, af], w=[af])
            if dbg and 'afd' in dbg:
                bk.do(bk.Y, lambda e, ch=ch, ft=ft: e.dma_start(
                    out=dbg['afd'][:, (ch * 8 + ft) * 512:
                                   (ch * 8 + ft + 1) * 512], in_=af.ap[:]),
                    r=[af], dma=True)
                bk.do(bk.Y, lambda e, ch=ch, ft=ft: e.dma_start(
                    out=dbg['af0d'][:, (ch * 8 + ft) * 512:
                                    (ch * 8 + ft + 1) * 512], in_=af0.ap[:]),
                    r=[af0], dma=True)
                bk.do(bk.Y, lambda e, ch=ch, ft=ft: e.dma_start(
                    out=dbg['ltd'][:, (ch * 8 + ft) * 512:
                                   (ch * 8 + ft + 1) * 512], in_=lt.ap[:]),
                    r=[lt], dma=True)
            bk.do(bk.V, lambda e, ft=ft: e.tensor_tensor(
                gat.ap[:, ft * TCB:(ft + 1) * TCB], af.ap[:],
                a_ch.ap[:, ft * TCB:(ft + 1) * TCB], ALU.mult),
                r=[af, a_ch], w=[gat] if ft == PT - 1 else (),
                after=gat.dep_write() if ft == 0 else ())
            if os.environ.get('MK_SERC3'):
                bk.barrier()
        for fo in range(PT):
            ps = mmC[fo % 2]
            bk.mm_group([(lambda e, fo=fo, kt=kt, ps=ps: e.matmul(
                ps.ap[:], outw.ap[:, kt * 1024 + fo * 128:
                                 kt * 1024 + (fo + 1) * 128],
                gat.ap[:, kt * TCB:(kt + 1) * TCB],
                start=(kt == 0), stop=(kt == PT - 1))) for kt in range(PT)],
                r=[outw, gat], w=[ps])
            bk.do(bk.V, lambda e, fo=fo, ps=ps: e.scalar_tensor_tensor(
                y2.ap[:, fo * TCB:(fo + 1) * TCB], ps.ap[:],
                outb.ap[:, fo:fo + 1], y_ch.ap[:, fo * TCB:(fo + 1) * TCB],
                ALU.add, ALU.add), r=[ps, outb, y_ch],
                w=[y2] if fo == PT - 1 else (),
                after=y2.dep_write() if fo == 0 else ())
        bk.do(bk.Y, lambda e, c0=c0: e.dma_start(
            out=chunk_view(y2_h.ap, c0, TCB), in_=sb3(y2.ap)),
            r=[y2], w=[y2_h] if ch == NCB - 1 else (), dma=True,
            after=y2_h.dep_write() if ch == 0 else ())
        if os.environ.get('MK_SERC'):
            bk.barrier()
        if dbg and 'zrd' in dbg:
            bk.do(bk.Y, lambda e, ch=ch: e.dma_start(
                out=dbg['zrd'][:, ch * 4096:(ch + 1) * 4096], in_=zr.ap[:]),
                r=[zr], dma=True)
        if dbg and 'gat' in dbg:
            bk.do(bk.Y, lambda e, ch=ch: e.dma_start(
                out=dbg['gat'][:, ch * 4096:(ch + 1) * 4096], in_=gat.ap[:]),
                r=[gat], dma=True)
    bk.end_scope()
    bk.end_scope()

    x_i = hb("m_xint")
    bk.scope()
    xcp = bk.buf("m_xcp", [128, PT * 512], F32)
    for ch in range(8):
        c0 = ch * 512
        bk.do(bk.Y, lambda e, c0=c0: e.dma_start(
            out=sb3(xcp.ap, 512), in_=chunk_view(x_in, c0, 512)),
            w=[xcp], dma=True)
        bk.do(bk.Y, lambda e, c0=c0: e.dma_start(
            out=chunk_view(x_i.ap, c0, 512), in_=sb3(xcp.ap, 512)),
            r=[xcp], w=[x_i] if ch == 7 else (), dma=True,
            after=x_i.dep_write() if ch == 0 else ())
    bk.end_scope(barrier=True)
    bk.scope()
    TCD = 256
    NCD = T // TCD
    H2D = TCD + 2
    fc1w = load_lhsT(bk, "m_fc1w", W['L0_attn_fc1_w'], PT, 4096)
    fc2w = load_lhsT(bk, "m_fc2w", W['L0_attn_fc2_w'], 32, 1024)
    fc1b = load_cols(bk, "m_fc1b", W['L0_attn_fc1_b'], 32)
    fc2b = load_cols(bk, "m_fc2b", W['L0_attn_fc2_b'], PT)
    cpe2c = load_conv_cols(bk, "m_cpe2", W['L0_attn_cpe2_w'],
                           W['L0_attn_cpe2_b'])
    cpe2p = bk.buf("m_cpe2p", [128, PT], F32)
    bk.do(bk.V, lambda e: e.tensor_scalar_add(
        cpe2p.ap[:], cpe2c.ap[:, PT:2 * PT], 1.0), r=[cpe2c], w=[cpe2p])
    n2g = load_cols(bk, "m_n2g", W['L0_attn_norm2_g'], PT)
    n2b = load_cols(bk, "m_n2b", W['L0_attn_norm2_b'], PT)
    y2h = bk.buf("m_y2hh", [128, PT * H2D], F32)
    y2c = bk.buf("m_y2c", [128, PT * TCD], F32)
    y2b = bk.buf("m_y2b", [128, PT * TCD], BF16)
    yn2 = bk.buf("m_yn2", [128, PT * TCD], BF16)
    hbuf = bk.buf("m_hbuf", [128, 32 * TCD], BF16)
    xcD = bk.buf("m_xcD", [128, PT * TCD], F32)
    tmpD = bk.buf("m_tmpD", [128, PT * TCD], F32)
    mmD = [bk.pbuf(f"m_mmD{i}", [128, TCD], F32) for i in range(2)]
    out_toks = []
    for ch in range(NCD):
        c0 = ch * TCD
        load_halo(bk, y2h, y2_h.ap, c0, TCD)
        for ft in range(PT):
            conv_tile(y2c, ft * TCD, y2h, ft * H2D, cpe2c, ft, TCD,
                      w1p=cpe2p, last=(ft == PT - 1))
        for ft in range(PT):
            sl = (slice(None), slice(ft * TCD, (ft + 1) * TCD))
            bk.do(bk.V, lambda e, sl=sl: e.tensor_copy(y2b.ap[sl],
                                                       y2c.ap[sl]),
                  r=[y2c], w=[y2b] if ft == PT - 1 else (),
                  after=y2b.dep_write() if ft == 0 else ())
        emit_ln_stats(bk, ln, ones_b, y2b, tc=TCD)
        for ft in range(PT):
            sl = (slice(None), slice(ft * TCD, (ft + 1) * TCD))
            bk.do(bk.V, lambda e, sl=sl: e.tensor_tensor(
                tmpD.ap[sl], y2c.ap[sl], ln.r.ap[:, 0:TCD], ALU.mult),
                r=[y2c, ln.r], w=[tmpD] if ft == PT - 1 else (),
                after=tmpD.dep_write() if ft == 0 else ())
        for ft in range(PT):
            sl = (slice(None), slice(ft * TCD, (ft + 1) * TCD))
            bk.do(bk.V, lambda e, sl=sl: e.tensor_tensor(
                tmpD.ap[sl], tmpD.ap[sl], ln.b2.ap[:, 0:TCD], ALU.add),
                r=[ln.b2, tmpD], w=[tmpD] if ft == PT - 1 else ())
        for ft in range(PT):
            sl = (slice(None), slice(ft * TCD, (ft + 1) * TCD))
            bk.do(bk.V, lambda e, sl=sl, ft=ft: e.tensor_scalar(
                yn2.ap[sl], tmpD.ap[sl], n2g.ap[:, ft:ft + 1],
                n2b.ap[:, ft:ft + 1], ALU.mult, ALU.add),
                r=[tmpD, n2g, n2b], w=[yn2] if ft == PT - 1 else (),
                after=yn2.dep_write() if ft == 0 else ())
        for fo in range(32):
            ps = mmD[fo % 2]
            bk.mm_group([(lambda e, fo=fo, kt=kt, ps=ps: e.matmul(
                ps.ap[:], fc1w.ap[:, kt * 4096 + fo * 128:
                                 kt * 4096 + (fo + 1) * 128],
                yn2.ap[:, kt * TCD:(kt + 1) * TCD],
                start=(kt == 0), stop=(kt == PT - 1))) for kt in range(PT)],
                r=[fc1w, yn2], w=[ps])
            bk.do(bk.S, lambda e, fo=fo, ps=ps: e.activation(
                hbuf.ap[:, fo * TCD:(fo + 1) * TCD], ps.ap[:], AF.Gelu,
                bias=fc1b.ap[:, fo:fo + 1]),
                r=[ps, fc1b], w=[hbuf] if fo == 31 else (),
                after=hbuf.dep_write() if fo == 0 else ())
        for fo in range(PT):
            ps = mmD[fo % 2]
            bk.mm_group([(lambda e, fo=fo, kt=kt, ps=ps: e.matmul(
                ps.ap[:], fc2w.ap[:, kt * 1024 + fo * 128:
                                 kt * 1024 + (fo + 1) * 128],
                hbuf.ap[:, kt * TCD:(kt + 1) * TCD],
                start=(kt == 0), stop=(kt == 31))) for kt in range(32)],
                r=[fc2w, hbuf], w=[ps])
            bk.do(bk.V, lambda e, fo=fo, ps=ps: e.scalar_tensor_tensor(
                y2c.ap[:, fo * TCD:(fo + 1) * TCD], ps.ap[:],
                fc2b.ap[:, fo:fo + 1], y2c.ap[:, fo * TCD:(fo + 1) * TCD],
                ALU.add, ALU.add), r=[ps, fc2b, y2c],
                w=[y2c] if fo == PT - 1 else ())
        for ft in range(PT):
            sl = (slice(None), slice(ft * TCD, (ft + 1) * TCD))
            bk.do(bk.V, lambda e, sl=sl: e.tensor_copy(y2b.ap[sl],
                                                       y2c.ap[sl]),
                  r=[y2c], w=[y2b] if ft == PT - 1 else (),
                  after=y2b.dep_write() if ft == 0 else ())
        emit_ln_stats(bk, ln, ones_b, y2b, tc=TCD)
        bk.do(bk.Y, lambda e, c0=c0: e.dma_start(
            out=sb3(xcD.ap, TCD), in_=chunk_view(x_i.ap, c0, TCD)),
            r=[x_i], w=[xcD], dma=True)
        emit_ln_apply(bk, ln, y2c, tmpD, tmpD, tc=TCD, res=xcD)
        out_toks.append(bk.do(bk.Y, lambda e, c0=c0: e.dma_start(
            out=chunk_view(x_out, c0, TCD), in_=sb3(tmpD.ap, TCD)),
            r=[tmpD, xcD], dma=True))
    bk.end_scope(barrier=bool(dbg and 'qq' in dbg))
    if dbg and 'qq' in dbg:
        bk.scope()
        bq = bk.buf("m_dbgq", [128, PT * 256], BF16)
        for ch2 in range(16):
            c02 = ch2 * 256
            bk.do(bk.Y, lambda e, c02=c02: e.dma_start(
                out=sb3(bq.ap, 256), in_=chunk_view(q_h.ap, c02, 256)),
                r=[q_h], w=[bq], dma=True)
            bk.do(bk.Y, lambda e, c02=c02: e.dma_start(
                out=dbg['qq'].rearrange("(ft p) t -> p ft t", p=128)
                [:, :, c02:c02 + 256], in_=sb3(bq.ap, 256)),
                r=[bq], dma=True)
        bk.end_scope(barrier=False)
    bk.end_scope(barrier=False)
    return out_toks




def emit_l0_ffn(bk: BK, W, consts, x_in, x_out, dbg=None):
    nc = bk.nc
    TCB = 512
    NCB = T // TCB
    bk.scope()
    ones_b = bk.buf("ones_f", [128, 128], BF16)
    bk.do(bk.V, lambda e: e.memset(ones_b.ap[:], 1.0), w=[ones_b])
    e8 = bk.buf("e8b", [8, 1024], BF16)
    bk.do(bk.G, lambda e: e.dma_start(out=e8.ap[:], in_=consts['e8'][:]),
          w=[e8], dma=True)
    gnb = bk.buf("f_gnb", [8, T], BF16)
    hb = lambda n, dt, shape: Buf(nc.dram_tensor(n, shape, dt,
                                                 kind="Internal").ap(), n)
    acc_h = hb("f_acch", F32, [C, T])
    acc_p = [Buf(acc_h.ap, f"f_acc{i}") for i in range(T // 256)]

    def accp(c0, tcx):
        return acc_p[c0 // 256:(c0 + tcx) // 256]
    ymid_h = hb("f_ymidh", BF16, [C, T])
    q_h = hb("f_qh", BF16, [C, T])
    k_h = hb("f_kh", BF16, [C, T])
    vt_h = hb("f_vth", BF16, [T, C])

    bk.scope()
    ln = LNScratch(bk, "f0")
    xn_b = bk.buf("f_xnb", [128, PT * T], BF16)
    xc = bk.buf("f_xc", [128, PT * TCB], F32)
    xcb = bk.buf("f_xcb", [128, PT * TCB], BF16)
    tmp = bk.buf("f_tmp", [128, PT * TCB], F32)
    xnf = bk.buf("f_xnf", [128, PT * TCB], F32)
    for ch in range(NCB):
        c0 = ch * TCB
        bk.do(bk.Y, lambda e, c0=c0: e.dma_start(
            out=sb3(xc.ap), in_=chunk_view(x_in, c0, TCB)), w=[xc], dma=True)
        for ft in range(PT):
            sl = (slice(None), slice(ft * TCB, (ft + 1) * TCB))
            bk.do(bk.V, lambda e, sl=sl: e.tensor_copy(xcb.ap[sl], xc.ap[sl]),
                  r=[xc], w=[xcb] if ft == PT - 1 else (),
                  after=xcb.dep_write() if ft == 0 else ())
        emit_ln_stats(bk, ln, ones_b, xcb)
        emit_ln_apply(bk, ln, xc, tmp, xnf)
        bk.do(bk.Y, lambda e, c0=c0: e.dma_start(
            out=chunk_view(acc_h.ap, c0, TCB), in_=sb3(xnf.ap)),
            r=[xnf], w=accp(c0, TCB), dma=True)
        for ft in range(PT):
            dsl = (slice(None), slice(ft * T + c0, ft * T + c0 + TCB))
            sl = (slice(None), slice(ft * TCB, (ft + 1) * TCB))
            bk.do(bk.V, lambda e, sl=sl, dsl=dsl: e.tensor_copy(
                xn_b.ap[dsl], xnf.ap[sl]),
                r=[xnf], w=[xn_b] if (ch, ft) == (NCB - 1, PT - 1) else (),
                after=xn_b.dep_write() if (ch, ft) == (0, 0) else ())
    wq = load_lhsT(bk, "f_wq", W['L0_ffn_Wq'], PT, 1024)
    wk = load_lhsT(bk, "f_wk", W['L0_ffn_Wk'], PT, 1024)
    wv = load_lhsT(bk, "f_wv", W['L0_ffn_Wv'], PT, 1024)
    bqc = load_cols(bk, "f_bq", W['L0_ffn_bq'], PT)
    bkc = load_cols(bk, "f_bk", W['L0_ffn_bk'], PT)
    prj = bk.buf("f_prj", [128, PT * TCB], BF16)
    mmF = [bk.pbuf(f"f_mmF{i}", [128, TCB], F32) for i in range(2)]
    for ch in range(NCB):
        c0 = ch * TCB
        for wmat, bc, dst_h in ((wq, bqc, q_h), (wk, bkc, k_h)):
            for fo in range(PT):
                ps = mmF[fo % 2]
                bk.mm_group([(lambda e, fo=fo, kt=kt, ps=ps, c0=c0,
                              wmat=wmat: e.matmul(
                    ps.ap[:], wmat.ap[:, kt * 1024 + fo * 128:
                                     kt * 1024 + (fo + 1) * 128],
                    xn_b.ap[:, kt * T + c0:kt * T + c0 + TCB],
                    start=(kt == 0), stop=(kt == PT - 1)))
                    for kt in range(PT)], r=[wmat, xn_b], w=[ps])
                bk.do(bk.V, lambda e, fo=fo, ps=ps, bc=bc: e.tensor_scalar(
                    prj.ap[:, fo * TCB:(fo + 1) * TCB], ps.ap[:],
                    bc.ap[:, fo:fo + 1], None, ALU.add),
                    r=[ps, bc], w=[prj] if fo == PT - 1 else (),
                    after=prj.dep_write() if fo == 0 else ())
            bk.do(bk.Y, lambda e, c0=c0, dst_h=dst_h: e.dma_start(
                out=chunk_view(dst_h.ap, c0, TCB), in_=sb3(prj.ap)),
                r=[prj], w=[dst_h] if ch == NCB - 1 else (), dma=True,
                after=dst_h.dep_write() if ch == 0 else ())
        for ts4 in range(4):
            for fo2 in range(2):
                ps = mmF[fo2 % 2]
                bk.mm_group([(lambda e, ts4=ts4, fo2=fo2, kt=kt, ps=ps,
                              c0=c0: e.matmul(
                    ps.ap[:], xn_b.ap[:, kt * T + c0 + ts4 * 128:
                                     kt * T + c0 + (ts4 + 1) * 128],
                    wv.ap[:, kt * 1024 + fo2 * 512:
                          kt * 1024 + (fo2 + 1) * 512],
                    start=(kt == 0), stop=(kt == PT - 1)))
                    for kt in range(PT)], r=[wv, xn_b], w=[ps])
                bk.do(bk.V, lambda e, ts4=ts4, fo2=fo2, ps=ps: e.tensor_copy(
                    prj.ap[:, ts4 * 1024 + fo2 * 512:
                           ts4 * 1024 + (fo2 + 1) * 512], ps.ap[:]),
                    r=[ps], w=[prj] if (ts4, fo2) == (3, 1) else (),
                    after=prj.dep_write() if (ts4, fo2) == (0, 0) else ())
        bk.do(bk.Y, lambda e, c0=c0: e.dma_start(
            out=vt_h.ap[c0:c0 + TCB, :].rearrange(
                "(s p) c -> p s c", p=128),
            in_=prj.ap.rearrange("p (s c) -> p s c", c=1024)),
            r=[prj], w=[vt_h] if ch == NCB - 1 else (), dma=True,
            after=vt_h.dep_write() if ch == 0 else ())
    bk.end_scope()

    QTC = 256
    NQB = T // QTC
    bk.scope()
    k_b = bk.buf("f_kb", [128, PT * T], BF16)
    v_tb = bk.buf("f_vtb", [128, 32 * 1024], BF16)
    bk.do(bk.Y, lambda e: e.dma_start(
        out=k_b.ap.rearrange("p (ft t) -> p ft t", t=T),
        in_=k_h.ap.rearrange("(ft p) t -> p ft t", p=128)),
        r=[k_h], w=[k_b], dma=True)
    bk.do(bk.Y, lambda e: e.dma_start(
        out=v_tb.ap.rearrange("p (s c) -> p s c", c=1024),
        in_=vt_h.ap.rearrange("(s p) c -> p s c", p=128)),
        r=[vt_h], w=[v_tb], dma=True)
    wo = load_lhsT(bk, "f_wo", W['L0_ffn_Wo'], PT, 1024)
    bvc = load_cols(bk, "f_bv", W['L0_ffn_bv'], PT)
    boc = load_cols(bk, "f_bo", W['L0_ffn_bo'], PT)
    bvb = bk.buf("f_bvb", [128, PT], BF16)
    for ft in range(PT):
        bk.do(bk.V, lambda e, ft=ft: e.tensor_copy(
            bvb.ap[:, ft:ft + 1], bvc.ap[:, ft:ft + 1]), r=[bvc],
            w=[bvb] if ft == PT - 1 else (),
            after=bvb.dep_write() if ft == 0 else ())
    bvps = bk.pbuf("f_bvps", [128, 8], F32)
    for fo in range(PT):
        bk.mm_group([(lambda e, fo=fo, kt=kt: e.matmul(
            bvps.ap[:, fo:fo + 1],
            wo.ap[:, kt * 1024 + fo * 128:kt * 1024 + (fo + 1) * 128],
            bvb.ap[:, kt:kt + 1],
            start=(kt == 0), stop=(kt == PT - 1))) for kt in range(PT)],
            r=[wo, bvb], w=() if fo < PT - 1 else [bvps],
            after=bvps.dep_write() if fo == 0 else ())
    bk.do(bk.V, lambda e: e.tensor_tensor(
        boc.ap[:], boc.ap[:], bvps.ap[:], ALU.add), r=[bvps], w=[boc])
    q_ch = bk.buf("f_qch", [128, PT * QTC], BF16)
    xnc = bk.buf("f_xnc", [128, PT * QTC], F32)
    p_t = bk.buf("f_pt", [128, 32 * QTC], BF16)
    zr = bk.buf("f_zr", [128, QTC], F32)
    a_b = bk.buf("f_ab", [128, PT * QTC], BF16)
    a_f = bk.buf("f_af", [128, PT * QTC], F32)
    mmS = [bk.pbuf(f"f_mmS{i}", [128, QTC], F32) for i in range(2)]
    mmP = [bk.pbuf(f"f_mmP{i}", [128, QTC], F32) for i in range(2)]
    zps = bk.pbuf("f_zps", [128, QTC], F32)
    SCALE = 1.0 / 32.0
    for ch in range(NQB):
        c0 = ch * QTC
        bk.do(bk.Y, lambda e, c0=c0: e.dma_start(
            out=sb3(q_ch.ap, QTC), in_=chunk_view(q_h.ap, c0, QTC)),
            r=[q_h], w=[q_ch], dma=True)
        z_tok = None
        for kt in range(32):
            ps = mmS[kt % 2]
            bk.mm_group([(lambda e, kt=kt, ft=ft, ps=ps: e.matmul(
                ps.ap[:], k_b.ap[:, ft * T + kt * 128:ft * T + (kt + 1) * 128],
                q_ch.ap[:, ft * QTC:(ft + 1) * QTC],
                start=(ft == 0), stop=(ft == PT - 1),
                skip_group_check=True)) for ft in range(PT)],
                r=[k_b, q_ch], w=[ps])
            e_tok = bk.do(bk.S, lambda e, kt=kt, ps=ps: e.activation(
                p_t.ap[:, kt * QTC:(kt + 1) * QTC], ps.ap[:], AF.Exp,
                scale=SCALE), r=[ps],
                w=[p_t] if kt == 31 else (),
                after=p_t.dep_write() if kt == 0 else ())
            z_tok = bk.mm_group([lambda e, kt=kt: e.matmul(
                zps.ap[:], ones_b.ap, p_t.ap[:, kt * QTC:(kt + 1) * QTC],
                start=(kt == 0), stop=(kt == 31), skip_group_check=True)],
                r=[ones_b], after=[e_tok] + (
                    zps.dep_write() if kt == 0 else []))
        zps.w = z_tok
        zps.rs = []
        bk.do(bk.V, lambda e: e.reciprocal(zr.ap[:], zps.ap[:]),
              r=[zps], w=[zr])
        for ft in range(PT):
            ps = mmP[ft % 2]
            bk.mm_group([(lambda e, ft=ft, kt=kt, ps=ps: e.matmul(
                ps.ap[:], v_tb.ap[:, kt * 1024 + ft * 128:
                                 kt * 1024 + (ft + 1) * 128],
                p_t.ap[:, kt * QTC:(kt + 1) * QTC],
                start=(kt == 0), stop=(kt == 31),
                skip_group_check=True)) for kt in range(32)],
                r=[v_tb, p_t], w=[ps])
            bk.do(bk.V, lambda e, ft=ft, ps=ps: e.tensor_tensor(
                a_b.ap[:, ft * QTC:(ft + 1) * QTC], ps.ap[:], zr.ap[:],
                ALU.mult), r=[ps, zr], w=[a_b] if ft == PT - 1 else (),
                after=a_b.dep_write() if ft == 0 else ())
        if dbg and 'ab' in dbg and ch == 0:
            bk.do(bk.Y, lambda e: e.dma_start(out=dbg['ab'], in_=a_b.ap[:]),
                  r=[a_b], dma=True)
            bk.do(bk.Y, lambda e: e.dma_start(out=dbg['zr'], in_=zr.ap[:]),
                  r=[zr], dma=True)
            bk.do(bk.Y, lambda e: e.dma_start(out=dbg['pt'], in_=p_t.ap[:]),
                  r=[p_t], dma=True)
        bk.do(bk.Y, lambda e, c0=c0: e.dma_start(
            out=sb3(xnc.ap, QTC), in_=chunk_view(acc_h.ap, c0, QTC)),
            r=accp(c0, QTC), w=[xnc], dma=True)
        for fo in range(PT):
            ps = mmS[fo % 2]
            bk.mm_group([(lambda e, fo=fo, kt=kt, ps=ps: e.matmul(
                ps.ap[:], wo.ap[:, kt * 1024 + fo * 128:
                               kt * 1024 + (fo + 1) * 128],
                a_b.ap[:, kt * QTC:(kt + 1) * QTC],
                start=(kt == 0), stop=(kt == PT - 1))) for kt in range(PT)],
                r=[wo, a_b], w=[ps])
            bk.do(bk.V, lambda e, fo=fo, ps=ps: e.tensor_scalar(
                a_f.ap[:, fo * QTC:(fo + 1) * QTC], ps.ap[:],
                boc.ap[:, fo:fo + 1], None, ALU.add),
                r=[ps, boc], w=[a_f] if fo == PT - 1 else (),
                after=a_f.dep_write() if fo == 0 else ())
        for ft in range(PT):
            sl = (slice(None), slice(ft * QTC, (ft + 1) * QTC))
            bk.do(bk.V, lambda e, sl=sl: e.tensor_tensor(
                q_ch.ap[sl], a_f.ap[sl], xnc.ap[sl], ALU.add),
                r=[a_f, xnc], w=[q_ch] if ft == PT - 1 else (),
                after=q_ch.dep_write() if ft == 0 else ())
        bk.do(bk.Y, lambda e, c0=c0: e.dma_start(
            out=chunk_view(ymid_h.ap, c0, QTC), in_=sb3(q_ch.ap, QTC)),
            r=[q_ch], w=[ymid_h] if ch == NQB - 1 else (), dma=True,
            after=ymid_h.dep_write() if ch == 0 else ())
        bk.do(bk.G, lambda e, c0=c0: e.dma_start(
            out=chunk_view(acc_h.ap, c0, QTC), in_=sb3(a_f.ap, QTC),
            accum_op=ALU.add), r=[a_f], w=accp(c0, QTC), dma=True)
    if dbg and 'ymid' in dbg:
        bk.do(bk.G, lambda e: e.dma_start(out=dbg['ymid'], in_=ymid_h.ap[:]),
              r=[ymid_h], dma=True)
    bk.end_scope()

    bk.scope()
    lng = LNScratch(bk, "fg")
    wg = load_lhsT(bk, "f_wg", W['L0_ffn_Wg'], PT, 8)
    bg8 = bk.buf("f_bg8", [8, 1], F32)
    bk.do(bk.Y, lambda e: e.dma_start(out=bg8.ap[:],
                                      in_=W['L0_ffn_bg'].unsqueeze(1)),
          w=[bg8], dma=True)
    ymc = bk.buf("f_ymc", [128, PT * TCB], BF16)
    ge = bk.buf("f_ge", [8, TCB], BF16)
    rzg = bk.buf("f_rzg", [128, TCB], F32)
    gps = bk.pbuf("f_gps", [8, TCB], F32)
    zgps = bk.pbuf("f_zgps", [128, TCB], F32)
    for ch in range(NCB):
        c0 = ch * TCB
        bk.do(bk.Y, lambda e, c0=c0: e.dma_start(
            out=sb3(ymc.ap), in_=chunk_view(ymid_h.ap, c0, TCB)),
            r=[ymid_h], w=[ymc], dma=True)
        bk.mm_group([(lambda e, kt=kt: e.matmul(
            gps.ap[:], wg.ap[:, kt * 8:(kt + 1) * 8],
            ymc.ap[:, kt * TCB:(kt + 1) * TCB],
            start=(kt == 0), stop=(kt == PT - 1))) for kt in range(PT)],
            r=[wg, ymc], w=[gps])
        bk.do(bk.S, lambda e: e.activation(
            ge.ap[:], gps.ap[:], AF.Exp, bias=bg8.ap[:, 0:1]),
            r=[gps, bg8], w=[ge])
        bk.mm_group([lambda e: e.matmul(
            zgps.ap[:], ones_b.ap[0:8, :], ge.ap[:],
            start=True, stop=True)], r=[ge, ones_b], w=[zgps])
        bk.do(bk.V, lambda e: e.reciprocal(rzg.ap[:], zgps.ap[:]),
              r=[zgps], w=[rzg])
        bk.do(bk.V, lambda e, c0=c0: e.tensor_tensor(
            gnb.ap[:, c0:c0 + TCB], ge.ap[:], rzg.ap[0:8, :], ALU.mult),
            r=[ge, rzg], w=[gnb] if ch == NCB - 1 else (),
            after=gnb.dep_write() if ch == 0 else ())
    if dbg and 'gnb' in dbg:
        bk.do(bk.G, lambda e: e.dma_start(out=dbg['gnb'], in_=gnb.ap[:]),
              r=[gnb], dma=True)
    bk.end_scope()

    TCM = 512
    NCM = T // TCM
    bk.scope()
    b2sb = bk.buf("f_b2sb", [8, 1024], BF16)
    bk.do(bk.G, lambda e: e.dma_start(out=b2sb.ap[:], in_=W['L0_ffn_b2'][:]),
          w=[b2sb], dma=True)
    ymc3 = bk.buf("f_ymc33", [128, PT * TCM], BF16)
    grep = bk.buf("f_grep", [128, TCM], BF16)
    hgb = bk.buf("f_hgb", [128, 32 * TCM], BF16)
    htmp = bk.buf("f_htmp", [128, TCM], BF16)
    stg = bk.buf("f_stg", [128, PT * TCM], F32)
    mmM = [bk.pbuf(f"f_mmM{i}", [128, TCM], F32) for i in range(2)]
    gbps = bk.pbuf("f_gbps", [128, TCM], F32)
    for ex in range(8):
        bk.scope()
        w1e = load_lhsT(bk, f"f_w1_{ex}", W['L0_ffn_W1'][ex], PT, 4096)
        w2e = load_lhsT(bk, f"f_w2_{ex}", W['L0_ffn_W2'][ex], 32, 1024)
        b1c = load_cols(bk, f"f_b1_{ex}", W['L0_ffn_b1'][ex], 32)
        for ch in range(NCM):
            c0 = ch * TCM
            bk.do(bk.Y, lambda e, c0=c0: e.dma_start(
                out=sb3(ymc3.ap, TCM), in_=chunk_view(ymid_h.ap, c0, TCM)),
                r=[ymid_h], w=[ymc3], dma=True)
            bk.mm_group([lambda e, ex=ex, c0=c0: e.matmul(
                gbps.ap[:], e8.ap[:, ex * 128:(ex + 1) * 128],
                gnb.ap[:, c0:c0 + TCM], start=True, stop=True)],
                r=[e8, gnb], w=[gbps])
            bk.do(bk.V, lambda e: e.tensor_copy(grep.ap[:], gbps.ap[:]),
                  r=[gbps], w=[grep])
            for fo in range(32):
                ps = mmM[fo % 2]
                bk.mm_group([(lambda e, fo=fo, kt=kt, ps=ps: e.matmul(
                    ps.ap[:], w1e.ap[:, kt * 4096 + fo * 128:
                                    kt * 4096 + (fo + 1) * 128],
                    ymc3.ap[:, kt * TCM:(kt + 1) * TCM],
                    start=(kt == 0), stop=(kt == PT - 1)))
                    for kt in range(PT)], r=[w1e, ymc3], w=[ps])
                bk.do(bk.S, lambda e, fo=fo, ps=ps: e.activation(
                    htmp.ap[:], ps.ap[:], AF.Relu,
                    bias=b1c.ap[:, fo:fo + 1]), r=[ps, b1c], w=[htmp])
                bk.do(bk.V, lambda e, fo=fo: e.tensor_tensor(
                    hgb.ap[:, fo * TCM:(fo + 1) * TCM], htmp.ap[:],
                    grep.ap[:], ALU.mult), r=[htmp, grep],
                    w=[hgb] if fo == 31 else (),
                    after=hgb.dep_write() if fo == 0 else ())
            for fo in range(PT):
                ps = mmM[fo % 2]
                fns = [(lambda e, fo=fo, kt=kt, ps=ps: e.matmul(
                    ps.ap[:], w2e.ap[:, kt * 1024 + fo * 128:
                                    kt * 1024 + (fo + 1) * 128],
                    hgb.ap[:, kt * TCM:(kt + 1) * TCM],
                    start=(kt == 0), stop=(kt == 31 and ex != 0)))
                    for kt in range(32)]
                if ex == 0:
                    fns.append(lambda e, fo=fo, ps=ps, c0=c0: e.matmul(
                        ps.ap[:], b2sb.ap[:, fo * 128:(fo + 1) * 128],
                        gnb.ap[:, c0:c0 + TCM], start=False, stop=True))
                bk.mm_group(fns, r=[w2e, hgb, b2sb, gnb], w=[ps])
                bk.do(bk.V, lambda e, fo=fo, ps=ps: e.tensor_copy(
                    stg.ap[:, fo * TCM:(fo + 1) * TCM], ps.ap[:]),
                    r=[ps], w=[stg] if fo == PT - 1 else (),
                    after=stg.dep_write() if fo == 0 else ())
            bk.do(bk.G, lambda e, c0=c0: e.dma_start(
                out=chunk_view(acc_h.ap, c0, TCM), in_=sb3(stg.ap, TCM),
                accum_op=ALU.add), r=[stg], w=accp(c0, TCM), dma=True)
        bk.end_scope()
    bk.end_scope()

    bk.scope()
    ln4 = LNScratch(bk, "f4")
    bo_f = bk.buf("f_bof", [128, PT * TCB], F32)
    bo_b = bk.buf("f_bob", [128, PT * TCB], BF16)
    xc4 = bk.buf("f_xc4", [128, PT * TCB], F32)
    tmp4 = bk.buf("f_tmp4", [128, PT * TCB], F32)
    out_toks = []
    for ch in range(NCB):
        c0 = ch * TCB
        bk.do(bk.Y, lambda e, c0=c0: e.dma_start(
            out=sb3(bo_f.ap), in_=chunk_view(acc_h.ap, c0, TCB)),
            r=accp(c0, TCB), w=[bo_f], dma=True)
        for ft in range(PT):
            sl = (slice(None), slice(ft * TCB, (ft + 1) * TCB))
            bk.do(bk.V, lambda e, sl=sl: e.tensor_copy(bo_b.ap[sl],
                                                       bo_f.ap[sl]),
                  r=[bo_f], w=[bo_b] if ft == PT - 1 else (),
                  after=bo_b.dep_write() if ft == 0 else ())
        emit_ln_stats(bk, ln4, ones_b, bo_b)
        bk.do(bk.Y, lambda e, c0=c0: e.dma_start(
            out=sb3(xc4.ap), in_=chunk_view(x_in, c0, TCB)),
            w=[xc4], dma=True)
        emit_ln_apply(bk, ln4, bo_f, tmp4, tmp4, res=xc4)
        out_toks.append(bk.do(bk.Y, lambda e, c0=c0: e.dma_start(
            out=chunk_view(x_out, c0, TCB), in_=sb3(tmp4.ap)),
            r=[tmp4, xc4], dma=True))
    bk.end_scope(barrier=False)
    bk.end_scope(barrier=False)
    return out_toks



def emit_wT(bk: BK, name, w_hbm, ident: Buf, nt=8):
    wt = bk.buf(name + "_t", [128, nt * nt * 128], BF16)
    bk.scope()
    wsb = bk.buf(name + "_w", [128, nt * nt * 128], BF16)
    tps = bk.pbuf(name + "_ps", [128, 128], BF16)
    for i in range(nt):
        bk.do(bk.G, lambda e, i=i: e.dma_start(
            out=wsb.ap[:, i * nt * 128:(i + 1) * nt * 128],
            in_=w_hbm[i * 128:(i + 1) * 128, :]),
            w=[wsb] if i == nt - 1 else (), dma=True,
            after=wsb.dep_write() if i == 0 else ())
    for i in range(nt):
        for j in range(nt):
            bk.mm_group([lambda e, i=i, j=j: e.transpose(
                tps.ap[:], wsb.ap[:, i * nt * 128 + j * 128:
                                  i * nt * 128 + (j + 1) * 128],
                ident.ap[:])], r=[wsb, ident], w=[tps])
            bk.do(bk.V, lambda e, i=i, j=j: e.tensor_copy(
                wt.ap[:, j * nt * 128 + i * 128: j * nt * 128 + (i + 1) * 128],
                tps.ap[:]), r=[tps],
                w=[wt] if (i, j) == (nt - 1, nt - 1) else (),
                after=wt.dep_write() if (i, j) == (0, 0) else ())
    bk.end_scope()
    return wt


def load_pcol(bk: BK, dst: Buf, vec_hbm, n=PT):
    for i in range(n):
        bk.do(bk.Y, lambda e, i=i: e.dma_start(
            out=dst.ap[:, i:i + 1],
            in_=vec_hbm[i * 128:(i + 1) * 128].unsqueeze(1)),
            w=[dst] if i == n - 1 else (), dma=True,
            after=dst.dep_write() if i == 0 else ())


def emit_l1_attn(bk: BK, W, consts, x_in, x_out, dbg=None):
    nc = bk.nc
    TCB = 256
    NCB = T // TCB
    TM = T // 4
    bk.scope()
    ln = LNScratch(bk)
    ones_b = bk.buf("ones_a", [128, 128], BF16)
    bk.do(bk.V, lambda e: e.memset(ones_b.ap[:], 1.0), w=[ones_b])
    ident = bk.buf("identa", [128, 128], BF16)
    bk.do(bk.G, lambda e: e.dma_start(out=ident.ap[:], in_=consts['ident'][:]),
          w=[ident], dma=True)
    xn_b = bk.buf("a_xnb", [128, PT * T], BF16)
    xc = bk.buf("a_xc", [128, PT * TCB], F32)
    xcb = bk.buf("a_xcb", [128, PT * TCB], BF16)
    tmp = bk.buf("a_tmp", [128, PT * TCB], F32)
    kv2 = bk.buf("a_kv2", [128, 8 * 64], BF16)
    k_tm = bk.buf("a_ktm", [128, 8 * 1024], BF16)
    v_tm = bk.buf("a_vtm", [128, 8 * 1024], BF16)
    mmps = [bk.pbuf(f"a_mm{i}", [128, TC], F32) for i in range(2)]

    for ch in range(NCB):
        c0 = ch * TCB
        bk.do(bk.Y, lambda e, c0=c0: e.dma_start(
            out=sb3(xc.ap, TCB), in_=chunk_view(x_in, c0, TCB)),
            w=[xc], dma=True)
        for ft in range(PT):
            sl = (slice(None), slice(ft * TCB, (ft + 1) * TCB))
            bk.do(bk.V, lambda e, sl=sl: e.tensor_copy(xcb.ap[sl], xc.ap[sl]),
                  r=[xc], w=[xcb] if ft == PT - 1 else (),
                  after=xcb.dep_write() if ft == 0 else ())
        emit_ln_stats(bk, ln, ones_b, xcb, tc=TCB)
        for ft in range(PT):
            sl = (slice(None), slice(ft * TCB, (ft + 1) * TCB))
            bk.do(bk.V, lambda e, sl=sl: e.tensor_tensor(
                tmp.ap[sl], xc.ap[sl], ln.r.ap[:, 0:TCB], ALU.mult),
                r=[xc, ln.r], w=[tmp] if ft == PT - 1 else (),
                after=tmp.dep_write() if ft == 0 else ())
        for ft in range(PT):
            dsl = (slice(None), slice(ft * T + c0, ft * T + c0 + TCB))
            sl = (slice(None), slice(ft * TCB, (ft + 1) * TCB))
            bk.do(bk.V, lambda e, sl=sl, dsl=dsl: e.tensor_tensor(
                xn_b.ap[dsl], tmp.ap[sl], ln.b2.ap[:, 0:TCB], ALU.add),
                r=[tmp, ln.b2], w=[xn_b] if (ch, ft) == (NCB - 1, PT - 1) else (),
                after=xn_b.dep_write() if (ch, ft) == (0, 0) else ())

    if dbg and 'xnb' in dbg:
        bk.do(bk.Y, lambda e: e.dma_start(out=dbg['xnb'], in_=xn_b.ap[:]),
              r=[xn_b], dma=True)
    bk.scope()
    xr = bk.buf("a_xr", [128, PT * TM], BF16)
    xr1 = bk.buf("a_xr1", [128, TM], F32)
    xr2 = bk.buf("a_xr2", [128, TM], F32)
    bnc = bk.buf("a_bn", [128, PT * 4], F32)
    for k, key in enumerate(('bn_gamma', 'bn_beta', 'bn_mean', 'bn_var')):
        for i in range(PT):
            bk.do(bk.Y, lambda e, i=i, k=k, key=key: e.dma_start(
                out=bnc.ap[:, k * PT + i:k * PT + i + 1],
                in_=W['L1_attn_' + key][i * 128:(i + 1) * 128].unsqueeze(1)),
                w=[bnc] if (k, i) == (3, PT - 1) else (), dma=True,
                after=bnc.dep_write() if (k, i) == (0, 0) else ())
    bns = bk.buf("a_bns", [128, PT], F32)
    bnb = bk.buf("a_bnb", [128, PT], F32)
    bnt = bk.buf("a_bnt", [128, PT], F32)
    epsb6 = bk.const_col(1e-6)
    bk.do(bk.S, lambda e: e.activation(
        bnt.ap[:], bnc.ap[:, 3 * PT:4 * PT], AF.Ln, bias=epsb6.ap[:, 0:1]),
        r=[bnc, epsb6], w=[bnt])
    bk.do(bk.S, lambda e: e.activation(
        bns.ap[:], bnt.ap[:], AF.Exp, scale=-0.5), r=[bnt], w=[bns],
        drain=True)
    bk.do(bk.V, lambda e: e.tensor_tensor(
        bns.ap[:], bns.ap[:], bnc.ap[:, 0:PT], ALU.mult), r=[bnc], w=[bns])
    bk.do(bk.V, lambda e: e.scalar_tensor_tensor(
        bnt.ap[:], bnc.ap[:, 2 * PT:3 * PT], -1.0, bns.ap[:],
        ALU.mult, ALU.mult), r=[bnc, bns], w=[bnt])
    bk.do(bk.V, lambda e: e.tensor_tensor(
        bnb.ap[:], bnt.ap[:], bnc.ap[:, PT:2 * PT], ALU.add),
        r=[bnt, bnc], w=[bnb])
    bk.do(bk.V, lambda e: e.tensor_scalar_mul(bns.ap[:], bns.ap[:], 0.25),
          w=[bns])
    for ft in range(PT):
        v4 = xn_b.ap[:, ft * T:(ft + 1) * T].rearrange("p (m f) -> p m f", f=4)
        bk.do(bk.V, lambda e, v4=v4: e.tensor_tensor(
            xr1.ap[:], v4[:, :, 0], v4[:, :, 1], ALU.add),
            r=[xn_b], w=[xr1])
        bk.do(bk.V, lambda e, v4=v4: e.tensor_tensor(
            xr2.ap[:], v4[:, :, 2], v4[:, :, 3], ALU.add),
            r=[xn_b], w=[xr2])
        bk.do(bk.V, lambda e: e.tensor_tensor(
            xr1.ap[:], xr1.ap[:], xr2.ap[:], ALU.add), r=[xr2], w=[xr1])
        if os.environ.get('MK_XR_RAW'):
            bk.do(bk.V, lambda e, ft=ft: e.tensor_copy(
                xr.ap[:, ft * TM:(ft + 1) * TM], xr1.ap[:]),
                r=[xr1], w=[xr] if ft == PT - 1 else (),
                after=xr.dep_write() if ft == 0 else ())
        else:
            bk.do(bk.V, lambda e, ft=ft: e.tensor_scalar(
                xr.ap[:, ft * TM:(ft + 1) * TM], xr1.ap[:],
                bns.ap[:, ft:ft + 1], bnb.ap[:, ft:ft + 1], ALU.mult, ALU.add),
                r=[xr1, bns, bnb], w=[xr] if ft == PT - 1 else (),
                after=xr.dep_write() if ft == 0 else ())

    if dbg and 'xr' in dbg:
        bk.do(bk.Y, lambda e: e.dma_start(out=dbg['xr'], in_=xr.ap[:]),
              r=[xr], dma=True)
    for name, key, dst in (("a_wk", 'L1_attn_Wk', k_tm),
                           ("a_wv", 'L1_attn_Wv', v_tm)):
        bk.scope()
        wt = emit_wT(bk, name, W[key], ident)
        for mt in range(8):
            for nc2 in range(2):
                ps = mmps[nc2 % 2]
                bk.mm_group([(lambda e, mt=mt, nc2=nc2, kt=kt, wt=wt, ps=ps:
                              e.matmul(ps.ap[:],
                                       xr.ap[:, kt * TM + mt * 128:
                                             kt * TM + (mt + 1) * 128],
                                       wt.ap[:, kt * 1024 + nc2 * 512:
                                             kt * 1024 + (nc2 + 1) * 512],
                                       start=(kt == 0), stop=(kt == PT - 1)))
                             for kt in range(PT)], r=[wt, xr], w=[ps])
                bk.do(bk.V, lambda e, mt=mt, nc2=nc2, dst=dst, ps=ps:
                      e.tensor_copy(dst.ap[:, mt * 1024 + nc2 * 512:
                                           mt * 1024 + (nc2 + 1) * 512],
                                    ps.ap[:]),
                      r=[ps], w=[dst] if (mt, nc2) == (7, 1) else (),
                      after=dst.dep_write() if (mt, nc2) == (0, 0) else ())
        bk.end_scope()

    if dbg and 'ktm' in dbg:
        bk.do(bk.Y, lambda e: e.dma_start(out=dbg['ktm'], in_=k_tm.ap[:]),
              r=[k_tm], dma=True)
    if dbg and 'vtm' in dbg:
        bk.do(bk.Y, lambda e: e.dma_start(out=dbg['vtm'], in_=v_tm.ap[:]),
              r=[v_tm], dma=True)
    kvp = bk.pbuf("a_kvp", [128, 64], F32)
    bk.do(bk.V, lambda e: e.memset(kv2.ap[:], 0.0), w=[kv2])
    SC = 32 ** -0.5
    for h in range(32):
        bk.mm_group([(lambda e, h=h, mt=mt: e.matmul(
            kvp.ap[0:32, 0:32],
            k_tm.ap[:, mt * 1024 + h * 32:mt * 1024 + (h + 1) * 32],
            v_tm.ap[:, mt * 1024 + h * 32:mt * 1024 + (h + 1) * 32],
            start=(mt == 0), stop=(mt == 7))) for mt in range(8)],
            r=[k_tm, v_tm], w=[kvp])
        hp, hh = divmod(h, 2)
        pb, cb = 64 * (hp % 2) + 32 * hh, 64 * (hp // 2) + 32 * hh
        bk.do(bk.V, lambda e, pb=pb, cb=cb: e.tensor_scalar_mul(
            kv2.ap[pb:pb + 32, cb:cb + 32], kvp.ap[0:32, 0:32], SC),
            r=[kvp], w=[kv2] if h == 31 else (),
            after=kv2.dep_write() if h == 0 else ())

    if dbg and 'kv2' in dbg:
        bk.do(bk.Y, lambda e: e.dma_start(out=dbg['kv2'], in_=kv2.ap[:]),
              r=[kv2], dma=True)
    bk.end_scope()

    bk.scope()
    wqT = emit_wT(bk, "a_wq", W['L1_attn_Wq'], ident)
    wpT = emit_wT(bk, "a_wp", W['L1_attn_Wp'], ident)
    bpc = bk.buf("a_bp", [128, PT], F32)
    load_pcol(bk, bpc, W['L1_attn_bp'])
    q_ch = bk.buf("a_qch", [128, PT * TCB], BF16)
    ao_ch = bk.buf("a_aoch", [128, PT * TCB], BF16)
    bo_f = bk.buf("a_bof", [128, PT * TCB], F32)
    bo_b = bk.buf("a_bob", [128, PT * TCB], BF16)
    out_toks = []
    for ch in range(NCB):
        c0 = ch * TCB
        for co in range(PT):
            ps = mmps[co % 2]
            bk.mm_group([(lambda e, co=co, kt=kt, ps=ps, c0=c0: e.matmul(
                ps.ap[:, 0:TCB], wqT.ap[:, kt * 1024 + co * 128:
                                        kt * 1024 + (co + 1) * 128],
                xn_b.ap[:, kt * T + c0: kt * T + c0 + TCB],
                start=(kt == 0), stop=(kt == PT - 1))) for kt in range(PT)],
                r=[wqT, xn_b], w=[ps])
            bk.do(bk.V, lambda e, co=co, ps=ps: e.tensor_copy(
                q_ch.ap[:, co * TCB:(co + 1) * TCB], ps.ap[:, 0:TCB]),
                r=[ps], w=[q_ch] if co == PT - 1 else (),
                after=q_ch.dep_write() if co == 0 else ())
        if dbg and 'qch' in dbg and ch == 0:
            bk.do(bk.Y, lambda e: e.dma_start(out=dbg['qch'], in_=q_ch.ap[:]),
                  r=[q_ch], dma=True)
        for pt in range(PT):
            ps = mmps[pt % 2]
            fns = []
            for half in (0, 1):
                hp = 2 * pt + half
                pb, cb = 64 * (hp % 2), 64 * (hp // 2)
                fns.append(lambda e, pt=pt, half=half, pb=pb, cb=cb, ps=ps:
                           e.matmul(ps.ap[64 * half:64 * half + 64, 0:TCB],
                                    kv2.ap[pb:pb + 64, cb:cb + 64],
                                    q_ch.ap[64 * half:64 * half + 64,
                                            pt * TCB:(pt + 1) * TCB],
                                    start=True, stop=True))
            bk.mm_group(fns, r=[kv2, q_ch], w=[ps])
            bk.do(bk.V, lambda e, pt=pt, ps=ps: e.tensor_copy(
                ao_ch.ap[:, pt * TCB:(pt + 1) * TCB], ps.ap[:, 0:TCB]),
                r=[ps], w=[ao_ch] if pt == PT - 1 else (),
                after=ao_ch.dep_write() if pt == 0 else ())
        if dbg and 'aoch' in dbg and ch == 0:
            bk.do(bk.Y, lambda e: e.dma_start(out=dbg['aoch'], in_=ao_ch.ap[:]),
                  r=[ao_ch], dma=True)
        for co in range(PT):
            ps = mmps[co % 2]
            bk.mm_group([(lambda e, co=co, kt=kt, ps=ps: e.matmul(
                ps.ap[:, 0:TCB], wpT.ap[:, kt * 1024 + co * 128:
                                        kt * 1024 + (co + 1) * 128],
                ao_ch.ap[:, kt * TCB:(kt + 1) * TCB],
                start=(kt == 0), stop=(kt == PT - 1))) for kt in range(PT)],
                r=[wpT, ao_ch], w=[ps])
            bk.do(bk.V, lambda e, co=co, ps=ps: e.tensor_scalar_add(
                bo_f.ap[:, co * TCB:(co + 1) * TCB], ps.ap[:, 0:TCB],
                bpc.ap[:, co:co + 1]),
                r=[ps, bpc], w=[bo_f] if co == PT - 1 else (),
                after=bo_f.dep_write() if co == 0 else ())
        for ft in range(PT):
            sl = (slice(None), slice(ft * TCB, (ft + 1) * TCB))
            bk.do(bk.V, lambda e, sl=sl: e.tensor_copy(bo_b.ap[sl],
                                                       bo_f.ap[sl]),
                  r=[bo_f], w=[bo_b] if ft == PT - 1 else (),
                  after=bo_b.dep_write() if ft == 0 else ())
        bk.do(bk.Y, lambda e, c0=c0: e.dma_start(
            out=sb3(xc.ap, TCB), in_=chunk_view(x_in, c0, TCB)),
            w=[xc], dma=True)
        emit_ln_stats(bk, ln, ones_b, bo_b, tc=TCB)
        emit_ln_apply(bk, ln, bo_f, tmp, tmp, tc=TCB, res=xc)
        out_toks.append(bk.do(bk.Y, lambda e, c0=c0: e.dma_start(
            out=chunk_view(x_out, c0, TCB), in_=sb3(tmp.ap, TCB)),
            r=[tmp, xc], dma=True))
    bk.end_scope(barrier=False)
    bk.end_scope(barrier=False)
    return out_toks



def hbm_inputs(nc, names_shapes):
    return {name: nc.dram_tensor(name, list(shape), F32,
                                 kind="ExternalInput").ap()
            for name, shape in names_shapes}


IN_SHAPES = {
    'L1_ffn_G1_R': (1, 32, 4, 32), 'L1_ffn_G2_R': (4, 16, 1, 16),
    'L1_ffn_G1_I': (1, 32, 4, 32), 'L1_ffn_G2_I': (4, 16, 1, 16),
    'L1_ffn_bias_real': (512,), 'L1_ffn_bias_imag': (512,),
}


L0A_SHAPES = {
    'L0_attn_cpe1_w': (1024, 1, 3), 'L0_attn_cpe1_b': (1024,),
    'L0_attn_norm1_g': (1024,), 'L0_attn_norm1_b': (1024,),
    'L0_attn_in_w': (1024, 1024), 'L0_attn_in_b': (1024,),
    'L0_attn_actp_w': (1024, 1024), 'L0_attn_actp_b': (1024,),
    'L0_attn_dwc_w': (1024, 1, 3), 'L0_attn_dwc_b': (1024,),
    'L0_attn_qk_w': (1024, 2048), 'L0_attn_qk_b': (2048,),
    'L0_attn_lepe_w': (1024, 1, 3), 'L0_attn_lepe_b': (1024,),
    'L0_attn_out_w': (1024, 1024), 'L0_attn_out_b': (1024,),
    'L0_attn_cpe2_w': (1024, 1, 3), 'L0_attn_cpe2_b': (1024,),
    'L0_attn_norm2_g': (1024,), 'L0_attn_norm2_b': (1024,),
    'L0_attn_fc1_w': (1024, 4096), 'L0_attn_fc1_b': (4096,),
    'L0_attn_fc2_w': (4096, 1024), 'L0_attn_fc2_b': (1024,),
}


L0F_SHAPES = {
    'L0_ffn_Wq': (1024, 1024), 'L0_ffn_bq': (1024,),
    'L0_ffn_Wk': (1024, 1024), 'L0_ffn_bk': (1024,),
    'L0_ffn_Wv': (1024, 1024), 'L0_ffn_bv': (1024,),
    'L0_ffn_Wo': (1024, 1024), 'L0_ffn_bo': (1024,),
    'L0_ffn_Wg': (1024, 8), 'L0_ffn_bg': (8,),
    'L0_ffn_W1': (8, 1024, 4096), 'L0_ffn_b1': (8, 4096),
    'L0_ffn_W2': (8, 4096, 1024), 'L0_ffn_b2': (8, 1024),
}


def build_l0f_only():
    bk = BK()
    nc = bk.nc
    x_in = nc.dram_tensor("x", [C, T], F32, kind="ExternalInput").ap()
    x_out = nc.dram_tensor("out", [C, T], F32, kind="ExternalOutput").ap()
    consts = {'e8': nc.dram_tensor("e8", [8, 1024], F32,
                                   kind="ExternalInput").ap()}
    W = hbm_inputs(nc, list(L0F_SHAPES.items()))
    toks = emit_l0_ffn(bk, W, consts, x_in, x_out)
    for t in toks:
        bk.Y.wait(t)
    bk.barrier()
    return bk.finalize()


def build_l0a_only():
    bk = BK()
    nc = bk.nc
    x_in = nc.dram_tensor("x", [C, T], F32, kind="ExternalInput").ap()
    x_out = nc.dram_tensor("out", [C, T], F32, kind="ExternalOutput").ap()
    consts = {'ident': nc.dram_tensor("ident", [128, 128], F32,
                                      kind="ExternalInput").ap(),
              'ind64': nc.dram_tensor("ind64", [128, 128], F32,
                                      kind="ExternalInput").ap()}
    W = hbm_inputs(nc, list(L0A_SHAPES.items()))
    import os as _os
    dbg = None
    if _os.environ.get('MK_L0A_DBG'):
        sel = _os.environ['MK_L0A_DBG'].split(',')
        if sel == ['1']:
            sel = ['xn', 'y', 'xi0', 'act', 'q', 'y2']
        dbg = {}
        for nm in sel:
            if nm == 'qq':
                dbg[nm] = nc.dram_tensor("dbg_qq", [C, T + 128], BF16,
                                         kind="ExternalOutput").ap()
            elif nm in ('afd', 'af0d', 'ltd'):
                dbg[nm] = nc.dram_tensor("dbg_" + nm, [128, 64 * 512 + 64],
                                         F32, kind="ExternalOutput").ap()
            elif nm == 'zrd':
                dbg[nm] = nc.dram_tensor("dbg_zrd", [128, 8 * T + 64], F32,
                                         kind="ExternalOutput").ap()
            elif nm == 'gat':
                dbg[nm] = nc.dram_tensor("dbg_gat", [128, 8 * T + 64], BF16,
                                         kind="ExternalOutput").ap()
            else:
                dbg[nm] = nc.dram_tensor("dbg_" + nm, [C, T],
                                         BF16 if nm in ('act',) else F32,
                                         kind="ExternalOutput").ap()
    toks = emit_l0_attn(bk, W, consts, x_in, x_out, dbg=dbg)
    for t in toks:
        bk.Y.wait(t)
    bk.barrier()
    return bk.finalize()


def build_l1a_only():
    bk = BK()
    nc = bk.nc
    x_in = nc.dram_tensor("x", [C, T], F32, kind="ExternalInput").ap()
    x_out = nc.dram_tensor("out", [C, T], F32, kind="ExternalOutput").ap()
    consts = {'ident': nc.dram_tensor("ident", [128, 128], F32,
                                      kind="ExternalInput").ap()}
    names = ['L1_attn_Wq', 'L1_attn_Wk', 'L1_attn_Wv', 'L1_attn_Wp',
             'L1_attn_bp', 'L1_attn_bn_gamma', 'L1_attn_bn_beta',
             'L1_attn_bn_mean', 'L1_attn_bn_var']
    shapes = {n: ((1024, 1024) if n.endswith(('Wq', 'Wk', 'Wv', 'Wp'))
                  else (1024,)) for n in names}
    W = hbm_inputs(nc, [(n, shapes[n]) for n in names])
    toks = emit_l1_attn(bk, W, consts, x_in, x_out)
    for t in toks:
        bk.Y.wait(t)
    bk.barrier()
    return bk.finalize()


def build_l1f_only():
    bk = BK()
    nc = bk.nc
    x_in = nc.dram_tensor("x", [C, T], F32, kind="ExternalInput").ap()
    x_out = nc.dram_tensor("out", [C, T], F32, kind="ExternalOutput").ap()
    consts = {'ident': nc.dram_tensor("ident", [128, 128], F32,
                                      kind="ExternalInput").ap()}
    W = hbm_inputs(nc, [(k, v) for k, v in IN_SHAPES.items()
                        if k.startswith('L1_ffn')])
    toks = emit_l1_ffn(bk, W, consts, x_in, x_out)
    for t in toks:
        bk.Y.wait(t)
    bk.barrier()
    return bk.finalize()


L1A_NAMES = ['L1_attn_Wq', 'L1_attn_Wk', 'L1_attn_Wv', 'L1_attn_Wp',
             'L1_attn_bp', 'L1_attn_bn_gamma', 'L1_attn_bn_beta',
             'L1_attn_bn_mean', 'L1_attn_bn_var']
L1F_NAMES = list(IN_SHAPES)


def host_consts():
    ident = np.eye(128, dtype=np.float32)
    ind64 = np.kron(np.eye(2, dtype=np.float32),
                    np.ones((64, 64), np.float32))
    e8 = np.zeros((8, 1024), np.float32)
    for e in range(8):
        e8[e, e * 128:(e + 1) * 128] = 1.0
    return ident, ind64, e8


def kernel(**inputs):
    ident, ind64, e8 = host_consts()
    x = np.asarray(inputs['L0_attn_x'])
    cur = [np.ascontiguousarray(x[c].T) for c in range(N_CORES)]
    stages = [
        (build_l0a_only, list(L0A_SHAPES), {'ident': ident, 'ind64': ind64}),
        (build_l0f_only, list(L0F_SHAPES), {'e8': e8}),
        (build_l1a_only, L1A_NAMES, {'ident': ident}),
        (build_l1f_only, L1F_NAMES, {'ident': ident}),
    ]
    for builder, names, consts in stages:
        nc = builder()
        wmap = {n: np.ascontiguousarray(np.asarray(inputs[n])) for n in names}
        in_maps = [{'x': cur[c], **consts, **wmap} for c in range(N_CORES)]
        res = run_bass_kernel_spmd(nc, in_maps, core_ids=list(range(N_CORES)))
        cur = [np.ascontiguousarray(res.results[c]['out'])
               for c in range(N_CORES)]
    return np.stack([cur[c].T for c in range(N_CORES)], 0)


# version 03
# speedup vs baseline: 1.1557x; 1.1557x over previous
import numpy as np
import os
from contextlib import ExitStack

import concourse.bass as bass
import concourse.mybir as mybir
from concourse.bass_utils import run_bass_kernel_spmd

F32 = mybir.dt.float32
BF16 = mybir.dt.bfloat16
AF = mybir.ActivationFunctionType
ALU = mybir.AluOpType
AX = mybir.AxisListType

C = 1024
T = 4096
PT = 8
TC = 512
NCH = T // TC
N_CORES = 8
LN_EPS = 1e-5
DEDUP_WAITS = True



class SemTrack:

    def __init__(self, nc, name):
        self.sem = nc.alloc_semaphore(name)
        self.name = name
        self.n = 0


N_DMA_SLOTS = int(os.environ.get('MK_DMA_SLOTS', 10))


class Stream:

    def __init__(self, bk, name):
        self.bk = bk
        self.name = name
        self.sem = bk.nc.alloc_semaphore(f"sem_{name}")
        self.n = 0
        self.code = []
        self.wmax = {}
        self.dma_pool = [SemTrack(bk.nc, f"dma_{name}{i}")
                         for i in range(N_DMA_SLOTS)]
        self.dma_i = 0
        self.mark_slots = ([SemTrack(bk.nc, f"mk_{name}{i}") for i in range(8)]
                           if name == 't' else None)
        self.mark_i = 0

    def force_wait(self, tok):
        st, v = tok
        self.code.append(('w', st.sem, v))
        self.wmax[st.name] = max(self.wmax.get(st.name, 0), v)

    def wait(self, tok):
        if tok is None:
            return
        st, v = tok
        if st is self and v > self.n:
            raise RuntimeError(f"wait on future own token {v} > {self.n}")
        if self.mark_slots is not None and st in self.mark_slots:
            if v > st.n:
                raise RuntimeError("wait on future slot token")
        if self.wmax.get(st.name, 0) >= v:
            if not DEDUP_WAITS:
                self.code.append(('w', st.sem, v))
            return
        self.wmax[st.name] = v
        self.code.append(('w', st.sem, v))

    def op(self, fn, after=(), mark=False, dma=False, drain=False):
        for t in after:
            self.wait(t)
        if drain and mark:
            self.code.append(('i', fn, None, 0))
            fn = lambda e: e.drain()
        elif self.mark_slots is not None and mark:
            self.code.append(('i', fn, None, 0))
            fn = lambda e: e.drain()
        if dma:
            tr = self.dma_pool[self.dma_i % N_DMA_SLOTS]
            self.dma_i += 1
            tr.n += 16
            self.code.append(('i', fn, tr.sem, 16))
            return (tr, tr.n)
        if mark:
            if self.mark_slots is not None:
                tr = self.mark_slots[self.mark_i % len(self.mark_slots)]
                self.mark_i += 1
                if tr.n:
                    self.wait((tr, tr.n))
                tr.n += 1
                self.code.append(('i', fn, tr.sem, 1))
                return (tr, tr.n)
            self.n += 1
            self.code.append(('i', fn, self.sem, 1))
            return (self, self.n)
        self.code.append(('i', fn, None, 0))
        return None


class Buf:

    __slots__ = ('ap', 'w', 'rs', 'name')

    def __init__(self, ap=None, name=""):
        self.ap = ap
        self.w = None
        self.rs = []
        self.name = name

    def dep_read(self):
        return [self.w]

    def dep_write(self):
        return self.rs + [self.w]


class BK:

    def __init__(self):
        self.nc = bass.Bass(
            "TRN2", target_bir_lowering=False,
            detect_race_conditions=bool(os.environ.get("MK_DETECT_RACES")))
        self.T = Stream(self, 't')
        self.V = Stream(self, 'v')
        self.S = Stream(self, 's')
        self.G = Stream(self, 'g')
        self.Y = Stream(self, 'y')
        self.streams = [self.T, self.V, self.S, self.G, self.Y]
        self.stack = None
        self.gstack = ExitStack()
        self._consts = {}
        self._cn = 0
        self._uniq = 0

    def _un(self, name):
        self._uniq += 1
        return f"{name}_u{self._uniq}"

    def scope(self):
        if self.stack is None:
            self.stack = ExitStack()
            self._scopes = [self.stack]
        else:
            self._scopes.append(ExitStack())
            self.stack = self._scopes[-1]

    def end_scope(self, barrier=True):
        if barrier:
            self.barrier()
        self._scopes.pop().close()
        self.stack = self._scopes[-1] if self._scopes else None

    def sbuf(self, name, shape, dtype, glob=False):
        name = self._un(name)
        if os.environ.get('MK_NOSCOPE'):
            return self.nc.alloc_sbuf_tensor(
                name, shape, dtype, side="right" if glob else "left").ap()
        stk = self.gstack if glob else self.stack
        h = stk.enter_context(self.nc.sbuf_tensor(
            name, shape, dtype, side="right" if glob else "left"))
        return h.ap()

    def const_col(self, value):
        if value not in self._consts:
            self._cn += 1
            b = Buf(self.sbuf(f"cc{self._cn}", [128, 1], F32, glob=True),
                    f"cc{self._cn}")
            self.do(self.V, lambda e, b=b, value=value: e.memset(b.ap[:],
                                                                 value), w=[b])
            self._consts[value] = b
        return self._consts[value]

    def psum(self, name, shape, dtype=F32):
        h = self.stack.enter_context(self.nc.psum_tensor(self._un(name),
                                                         shape, dtype))
        return h.ap()

    def buf(self, name, shape, dtype):
        return Buf(self.sbuf(name, shape, dtype), name)

    def pbuf(self, name, shape, dtype=F32):
        return Buf(self.psum(name, shape, dtype), name)

    def do(self, st: Stream, fn, r=(), w=(), after=(), mark=True, dma=False,
           drain=None):
        if drain is None:
            drain = st is self.S or st is self.V
        deps = list(after)
        for b in r:
            deps += b.dep_read()
        for b in w:
            deps += b.dep_write()
        tok = st.op(fn, after=deps, mark=mark, dma=dma, drain=drain)
        if tok is not None:
            for b in r:
                b.rs.append(tok)
            for b in w:
                b.w = tok
                b.rs = []
        return tok

    def mm_group(self, fns, r=(), w=(), after=()):
        deps = list(after)
        for b in r:
            deps += b.dep_read()
        for b in w:
            deps += b.dep_write()
        for i, fn in enumerate(fns):
            last = (i == len(fns) - 1)
            tok = self.T.op(fn, after=deps if i == 0 else (), mark=last)
        for b in r:
            b.rs.append(tok)
        for b in w:
            b.w = tok
            b.rs = []
        return tok

    def barrier(self):
        for st in self.streams:
            for tr in st.dma_pool:
                if tr.n:
                    st.wait((tr, tr.n))
        toks = [st.op(lambda e: e.nop(), mark=True) for st in self.streams]
        for st in self.streams:
            for t in toks:
                st.wait(t)
        return toks

    def finalize(self):
        with self.nc.Block() as block:
            regs = ((self.T, block.tensor), (self.V, block.vector),
                    (self.S, block.scalar), (self.G, block.gpsimd),
                    (self.Y, block.sync))
            for st, reg in regs:
                def make(code):
                    def emit(eng):
                        for item in code:
                            if item[0] == 'w':
                                eng.wait_ge(item[1], item[2])
                            else:
                                inst = item[1](eng)
                                if item[2] is not None:
                                    inst.then_inc(item[2], item[3])
                    return emit
                reg(make(st.code))
        return self.nc


def chunk_view(hbm_ap, c0, tc):
    return hbm_ap.rearrange("(ft p) t -> p ft t", p=128)[:, :, c0:c0 + tc]


def sb3(ap, tc=TC):
    return ap.rearrange("p (ft t) -> p ft t", t=tc)



class LNScratch:

    def __init__(self, bk: BK, tag=""):
        self.sq = bk.buf(f"ln_sq{tag}", [128, PT * TC], BF16)
        self.mu = bk.buf(f"ln_mu{tag}", [128, TC], F32)
        self.msq = bk.buf(f"ln_msq{tag}", [128, TC], F32)
        self.var = bk.buf(f"ln_var{tag}", [128, TC], F32)
        self.r = bk.buf(f"ln_r{tag}", [128, TC], F32)
        self.b2 = bk.buf(f"ln_b2{tag}", [128, TC], F32)
        self.psA = bk.pbuf(f"ln_psA{tag}", [128, TC], F32)
        self.psB = bk.pbuf(f"ln_psB{tag}", [128, TC], F32)


def emit_ln_stats(bk: BK, ln: LNScratch, ones_b: Buf, src: Buf, tc=TC, nf=PT,
                  src_tiles=None):
    inv_n = 1.0 / (128 * nf)
    tiles = src_tiles or [src.ap[:, ft * tc:(ft + 1) * tc] for ft in range(nf)]
    for ft in range(nf):
        bk.do(bk.S, lambda e, ft=ft, t=tiles[ft]: e.activation(
            ln.sq.ap[:, ft * tc: (ft + 1) * tc], t, AF.Square),
            r=[src], w=[ln.sq] if ft == nf - 1 else (),
            after=ln.sq.dep_write() if ft == 0 else ())
    bk.mm_group([(lambda e, ft=ft, t=tiles[ft]: e.matmul(
        ln.psA.ap[:, 0:tc], ones_b.ap, t,
        start=(ft == 0), stop=(ft == nf - 1))) for ft in range(nf)],
        r=[src, ones_b], w=[ln.psA])
    bk.mm_group([(lambda e, ft=ft: e.matmul(
        ln.psB.ap[:, 0:tc], ones_b.ap,
        ln.sq.ap[:, ft * tc:(ft + 1) * tc],
        start=(ft == 0), stop=(ft == nf - 1))) for ft in range(nf)],
        r=[ln.sq, ones_b], w=[ln.psB])
    bk.do(bk.V, lambda e: e.tensor_scalar_mul(
        ln.mu.ap[:, 0:tc], ln.psA.ap[:, 0:tc], inv_n), r=[ln.psA], w=[ln.mu])
    bk.do(bk.V, lambda e: e.tensor_scalar_mul(
        ln.msq.ap[:, 0:tc], ln.psB.ap[:, 0:tc], inv_n), r=[ln.psB], w=[ln.msq])
    bk.do(bk.V, lambda e: e.tensor_tensor(
        ln.var.ap[:, 0:tc], ln.mu.ap[:, 0:tc], ln.mu.ap[:, 0:tc], ALU.mult),
        r=[ln.mu], w=[ln.var])
    bk.do(bk.V, lambda e: e.tensor_tensor(
        ln.var.ap[:, 0:tc], ln.msq.ap[:, 0:tc], ln.var.ap[:, 0:tc],
        ALU.subtract), r=[ln.msq, ln.var], w=[ln.var])
    epsb = bk.const_col(LN_EPS)
    bk.do(bk.S, lambda e: e.activation(
        ln.var.ap[:, 0:tc], ln.var.ap[:, 0:tc], AF.Ln, bias=epsb.ap[:, 0:1]),
        r=[epsb], w=[ln.var])
    bk.do(bk.S, lambda e: e.activation(
        ln.r.ap[:, 0:tc], ln.var.ap[:, 0:tc], AF.Exp, scale=-0.5),
        r=[ln.var], w=[ln.r], drain=True)
    bk.do(bk.V, lambda e: e.scalar_tensor_tensor(
        ln.b2.ap[:, 0:tc], ln.mu.ap[:, 0:tc], -1.0, ln.r.ap[:, 0:tc],
        ALU.mult, ALU.mult), r=[ln.mu, ln.r], w=[ln.b2])


def emit_ln_apply(bk: BK, ln: LNScratch, src: Buf, tmp: Buf, dst: Buf,
                  tc=TC, nf=PT, res: Buf = None, streams=None):
    sa, sbt, scs = streams or (bk.V, bk.V, bk.V)
    for ft in range(nf):
        sl = (slice(None), slice(ft * tc, (ft + 1) * tc))
        bk.do(sa, lambda e, sl=sl: e.tensor_tensor(
            tmp.ap[sl], src.ap[sl], ln.r.ap[:, 0:tc], ALU.mult),
            r=[src, ln.r], w=[tmp] if ft == nf - 1 else (),
            after=tmp.dep_write() if ft == 0 else ())
    if res is None:
        for ft in range(nf):
            sl = (slice(None), slice(ft * tc, (ft + 1) * tc))
            bk.do(sbt, lambda e, sl=sl: e.tensor_tensor(
                dst.ap[sl], tmp.ap[sl], ln.b2.ap[:, 0:tc], ALU.add),
                r=[tmp, ln.b2], w=[dst] if ft == nf - 1 else (),
                after=dst.dep_write() if ft == 0 else ())
    else:
        for ft in range(nf):
            sl = (slice(None), slice(ft * tc, (ft + 1) * tc))
            bk.do(sbt, lambda e, sl=sl: e.tensor_tensor(
                tmp.ap[sl], tmp.ap[sl], ln.b2.ap[:, 0:tc], ALU.add),
                r=[ln.b2, tmp], w=[tmp] if ft == nf - 1 else ())
        for ft in range(nf):
            sl = (slice(None), slice(ft * tc, (ft + 1) * tc))
            bk.do(scs, lambda e, sl=sl: e.tensor_tensor(
                dst.ap[sl], tmp.ap[sl], res.ap[sl], ALU.add),
                r=[tmp, res], w=[dst] if ft == nf - 1 else (),
                after=dst.dep_write() if ft == 0 else ())



def emit_l1_ffn(bk: BK, W, consts, x_in, x_out):
    nc = bk.nc
    bk.scope()
    ln = LNScratch(bk)
    ones_b = bk.buf("ones_b", [128, 128], BF16)
    bk.do(bk.V, lambda e: e.memset(ones_b.ap[:], 1.0), w=[ones_b])
    ident = bk.buf("identb", [128, 128], BF16)
    bk.do(bk.G, lambda e: e.dma_start(out=ident.ap[:], in_=consts['ident'][:]),
          w=[ident], dma=True)

    g1r = bk.buf("g1r", [4, 1024], BF16)
    g2r = bk.buf("g2r", [4, 256], BF16)
    g1i = bk.buf("g1i", [4, 1024], BF16)
    g2i = bk.buf("g2i", [4, 256], BF16)
    for b, key, rr in ((g1r, 'L1_ffn_G1_R', 1), (g2r, 'L1_ffn_G2_R', 2),
                       (g1i, 'L1_ffn_G1_I', 1), (g2i, 'L1_ffn_G2_I', 2)):
        if rr == 1:
            srcv = W[key][0].rearrange("b c d -> c b d")
            dstv = b.ap.rearrange("c (b d) -> c b d", d=32)
        else:
            srcv = W[key][:, :, 0, :]
            dstv = b.ap.rearrange("c (e g) -> c e g", g=16)
        bk.do(bk.G, lambda e, srcv=srcv, dstv=dstv: e.dma_start(
            out=dstv, in_=srcv), w=[b], dma=True)

    vhbm = {k: Buf(nc.dram_tensor(f"l1f_v_{k}", [1024, 256], BF16,
                                  kind="Internal").ap(), f"vhbm{k}")
            for k in 'ri'}
    vps = bk.pbuf("l1f_vps", [128, 256], F32)
    vsb = bk.buf("l1f_vsb", [128, 256], BF16)
    w_sb = {'r': bk.buf("l1f_wr", [128, 4 * 512], BF16),
            'i': bk.buf("l1f_wi", [128, 4 * 512], BF16)}
    wt = {'r': bk.buf("l1f_wtr", [128, 4 * 512], BF16),
          'i': bk.buf("l1f_wti", [128, 4 * 512], BF16),
          'in': bk.buf("l1f_wtin", [128, 4 * 512], BF16)}
    tps = bk.pbuf("l1f_tps", [128, 128], BF16)

    for key, g1, g2 in (('r', g1r, g2r), ('i', g1i, g2i)):
        for mc in range(8):
            bk.mm_group([lambda e, g1=g1, mc=mc, g2=g2: e.matmul(
                vps.ap[:], g1.ap[:, mc * 128:(mc + 1) * 128], g2.ap[:],
                start=True, stop=True)], r=[g1, g2], w=[vps])
            bk.do(bk.V, lambda e: e.tensor_copy(vsb.ap[:], vps.ap[:]),
                  r=[vps], w=[vsb])
            bk.do(bk.Y, lambda e, key=key, mc=mc: e.dma_start(
                out=vhbm[key].ap[mc * 128:(mc + 1) * 128, :], in_=vsb.ap[:]),
                r=[vsb], w=[vhbm[key]] if mc == 7 else (), dma=True,
                after=vhbm[key].dep_write() if mc == 0 else ())
    for key in 'ri':
        for p4 in range(4):
            for bl in range(8):
                b = p4 * 8 + bl
                src = vhbm[key].ap[b * 32:(b + 1) * 32, :].rearrange(
                    "d (e g) -> e d g", g=16)
                bk.do(bk.Y, lambda e, key=key, p4=p4, bl=bl, src=src:
                      e.dma_start(out=w_sb[key].ap[
                          16 * bl:16 * (bl + 1),
                          p4 * 512:(p4 + 1) * 512].rearrange(
                          "p (d g) -> p d g", g=16),
                          in_=src),
                      r=[vhbm[key]], w=[w_sb[key]] if (p4, bl) == (3, 7) else (),
                      dma=True,
                      after=w_sb[key].dep_write() if (p4, bl) == (0, 0) else ())
    for key in 'ri':
        for i in range(4):
            for j in range(4):
                bk.mm_group([lambda e, key=key, i=i, j=j: e.transpose(
                    tps.ap[:], w_sb[key].ap[:, i * 512 + j * 128:
                                            i * 512 + (j + 1) * 128],
                    ident.ap[:])], r=[w_sb[key], ident], w=[tps])
                bk.do(bk.V, lambda e, key=key, i=i, j=j: e.tensor_copy(
                    wt[key].ap[:, j * 512 + i * 128: j * 512 + (i + 1) * 128],
                    tps.ap[:]), r=[tps],
                    w=[wt[key]] if (i, j) == (3, 3) else (),
                    after=wt[key].dep_write() if (i, j) == (0, 0) else ())
    bk.do(bk.V, lambda e: e.tensor_scalar_mul(wt['in'].ap[:], wt['i'].ap[:],
                                              -1.0), r=[wt['i']], w=[wt['in']])

    biasc = bk.buf("l1f_bias", [128, 8], F32)
    for rt in range(4):
        for cc, key in ((rt, 'L1_ffn_bias_real'), (4 + rt, 'L1_ffn_bias_imag')):
            bk.do(bk.Y, lambda e, cc=cc, key=key, rt=rt: e.dma_start(
                out=biasc.ap[:, cc:cc + 1],
                in_=W[key][rt * 128:(rt + 1) * 128].unsqueeze(1)),
                w=[biasc] if cc == 7 else (), dma=True,
                after=biasc.dep_write() if cc == 0 else ())

    xc = bk.buf("l1f_xc", [128, PT * TC], F32)
    xcb = bk.buf("l1f_xcb", [128, PT * TC], BF16)
    xnb = bk.buf("l1f_xnb", [128, PT * TC], BF16)
    bo_f = bk.buf("l1f_bof", [128, PT * TC], F32)
    bo_b = bk.buf("l1f_bob", [128, PT * TC], BF16)
    tmp = bk.buf("l1f_tmp", [128, PT * TC], F32)
    mm_ps = [bk.pbuf(f"l1f_mm{i}", [128, TC], F32) for i in range(2)]

    out_toks = []
    for ch in range(NCH):
        c0 = ch * TC
        bk.do(bk.Y, lambda e, c0=c0: e.dma_start(
            out=sb3(xc.ap), in_=chunk_view(x_in, c0, TC)), w=[xc], dma=True)
        for ft in range(PT):
            sl = (slice(None), slice(ft * TC, (ft + 1) * TC))
            bk.do(bk.V, lambda e, sl=sl: e.tensor_copy(xcb.ap[sl], xc.ap[sl]),
                  r=[xc], w=[xcb] if ft == PT - 1 else (),
                  after=xcb.dep_write() if ft == 0 else ())
        emit_ln_stats(bk, ln, ones_b, xcb)
        emit_ln_apply(bk, ln, xc, tmp, xnb)
        for rt in range(8):
            ps = mm_ps[rt % 2]
            fns = []
            if rt < 4:
                ka, kb, sl_r = wt['r'], wt['in'], rt
            else:
                ka, kb, sl_r = wt['i'], wt['r'], rt - 4
            for kt in range(4):
                fns.append(lambda e, kt=kt, ka=ka, sl_r=sl_r, ps=ps: e.matmul(
                    ps.ap[:], ka.ap[:, kt * 512 + sl_r * 128:
                                    kt * 512 + (sl_r + 1) * 128],
                    xnb.ap[:, kt * TC:(kt + 1) * TC],
                    start=(kt == 0), stop=False))
            for kt in range(4):
                fns.append(lambda e, kt=kt, kb=kb, sl_r=sl_r, ps=ps: e.matmul(
                    ps.ap[:], kb.ap[:, kt * 512 + sl_r * 128:
                                    kt * 512 + (sl_r + 1) * 128],
                    xnb.ap[:, (4 + kt) * TC:(5 + kt) * TC],
                    start=False, stop=(kt == 3)))
            bk.mm_group(fns, r=[wt['r'], wt['in'], wt['i'], xnb], w=[ps])
            sl = (slice(None), slice(rt * TC, (rt + 1) * TC))
            bk.do(bk.V, lambda e, sl=sl, rt=rt, ps=ps: e.tensor_scalar_add(
                bo_f.ap[sl], ps.ap[:], biasc.ap[:, rt:rt + 1]),
                r=[ps, biasc], w=[bo_f] if rt == 7 else (),
                after=bo_f.dep_write() if rt == 0 else ())
        for ft in range(PT):
            sl = (slice(None), slice(ft * TC, (ft + 1) * TC))
            bk.do(bk.V, lambda e, sl=sl: e.tensor_copy(bo_b.ap[sl],
                                                       bo_f.ap[sl]),
                  r=[bo_f], w=[bo_b] if ft == PT - 1 else (),
                  after=bo_b.dep_write() if ft == 0 else ())
        emit_ln_stats(bk, ln, ones_b, bo_b)
        emit_ln_apply(bk, ln, bo_f, tmp, tmp, res=xc,
                      streams=(bk.V, bk.V, bk.V))
        out_toks.append(bk.do(bk.Y, lambda e, c0=c0: e.dma_start(
            out=chunk_view(x_out, c0, TC), in_=sb3(tmp.ap)),
            r=[tmp, xc], dma=True))
    bk.end_scope()
    return out_toks





def load_cols(bk: BK, name, vec_hbm, n, q=None):
    dst = bk.buf(name, [128, n], F32)
    q = q or bk.Y
    for i in range(n):
        bk.do(q, lambda e, i=i: e.dma_start(
            out=dst.ap[:, i:i + 1],
            in_=vec_hbm[i * 128:(i + 1) * 128].unsqueeze(1)),
            w=[dst] if i == n - 1 else (), dma=True,
            after=dst.dep_write() if i == 0 else ())
    return dst


def load_conv_cols(bk: BK, name, w_hbm, b_hbm):
    dst = bk.buf(name, [128, 4 * PT], F32)
    first = True
    for k in range(3):
        for ft in range(PT):
            bk.do(bk.Y, lambda e, k=k, ft=ft: e.dma_start(
                out=dst.ap[:, k * PT + ft:k * PT + ft + 1],
                in_=w_hbm[ft * 128:(ft + 1) * 128, 0, k].unsqueeze(1),
                allow_slow_non_contiguous=True),
                dma=True, after=dst.dep_write() if first else ())
            first = False
    for ft in range(PT):
        bk.do(bk.Y, lambda e, ft=ft: e.dma_start(
            out=dst.ap[:, 3 * PT + ft:3 * PT + ft + 1],
            in_=b_hbm[ft * 128:(ft + 1) * 128].unsqueeze(1)),
            w=[dst] if ft == PT - 1 else (), dma=True)
    return dst


def load_lhsT(bk: BK, name, w_hbm, kt_n, n_cols):
    dst = bk.buf(name, [128, kt_n * n_cols], BF16)
    for i in range(kt_n):
        bk.do(bk.G, lambda e, i=i: e.dma_start(
            out=dst.ap[:, i * n_cols:(i + 1) * n_cols],
            in_=w_hbm[i * 128:(i + 1) * 128, :]),
            w=[dst] if i == kt_n - 1 else (), dma=True,
            after=dst.dep_write() if i == 0 else ())
    return dst


def load_halo(bk: BK, dst: Buf, src_hbm, c0, tcx):
    W2 = tcx + 2
    d3 = dst.ap.rearrange("p (ft t) -> p ft t", t=W2)
    a, b = max(c0 - 1, 0), min(c0 + tcx + 1, T)
    lo = 1 - (c0 - a)
    if c0 == 0:
        bk.do(bk.V, lambda e: e.memset(d3[:, :, 0:1], 0.0),
              w=(), after=dst.dep_write())
    if c0 + tcx == T:
        bk.do(bk.V, lambda e: e.memset(d3[:, :, W2 - 1:W2], 0.0),
              w=(), after=dst.dep_write())
    return bk.do(bk.Y, lambda e: e.dma_start(
        out=d3[:, :, lo:lo + (b - a)], in_=chunk_view(src_hbm, a, b - a)),
        w=[dst], dma=True)


def emit_l0_attn(bk: BK, W, consts, x_in, x_out, dbg=None):
    nc = bk.nc
    TCB = 512
    NCB = T // TCB
    H2 = TCB + 2
    bk.scope()
    ln = LNScratch(bk)
    ones_b = bk.buf("ones_m", [128, 128], BF16)
    bk.do(bk.V, lambda e: e.memset(ones_b.ap[:], 1.0), w=[ones_b])
    ident = bk.buf("identm", [128, 128], BF16)
    bk.do(bk.G, lambda e: e.dma_start(out=ident.ap[:], in_=consts['ident'][:]),
          w=[ident], dma=True)
    ind64 = bk.buf("ind64b", [128, 128], BF16)
    bk.do(bk.G, lambda e: e.dma_start(out=ind64.ap[:], in_=consts['ind64'][:]),
          w=[ind64], dma=True)
    kv_b = bk.buf("m_kvb", [128, 8 * 128], BF16)
    m2 = bk.buf("m_m2", [128, PT * 128], BF16)
    kmean = bk.buf("m_kmean", [128, PT], F32)
    def hb(n, dt=F32, key=None):
        if dbg and key in dbg:
            return Buf(dbg[key], n)
        return Buf(nc.dram_tensor(n, [C, T], dt, kind="Internal").ap(), n)
    xn_h, y_h = hb("m_xnh", key='xn'), hb("m_yh", key='y')
    xi0_h = hb("m_xi0h", key='xi0')
    act_h = hb("m_acth", BF16, key='act')
    q_h = Buf(xn_h.ap.bitcast(BF16)[:, 0:T], "m_qh")
    y2_h = Buf(xi0_h.ap, "m_y2h")
    if dbg and 'y2' in dbg:
        y2_h = Buf(dbg['y2'], "m_y2h")

    def conv_tile(dstb, dsl, srcb, sb, cw, ft, tcx, w1p=None, bias=True,
                  last=False):
        w1c = (w1p.ap[:, ft:ft + 1] if w1p is not None
               else cw.ap[:, PT + ft:PT + ft + 1])
        bc = cw.ap[:, 3 * PT + ft:3 * PT + ft + 1]
        rd = [srcb, cw] + ([w1p] if w1p is not None else [])
        bk.do(bk.V, lambda e: e.tensor_scalar(
            dstb.ap[:, dsl:dsl + tcx], srcb.ap[:, sb + 1:sb + 1 + tcx],
            w1c, bc, ALU.mult, ALU.add), r=rd,
            w=(), after=dstb.dep_write() if dsl == 0 else ())
        bk.do(bk.V, lambda e: e.scalar_tensor_tensor(
            dstb.ap[:, dsl:dsl + tcx], srcb.ap[:, sb:sb + tcx],
            cw.ap[:, 0 * PT + ft:0 * PT + ft + 1],
            dstb.ap[:, dsl:dsl + tcx], ALU.mult, ALU.add), r=rd)
        bk.do(bk.V, lambda e: e.scalar_tensor_tensor(
            dstb.ap[:, dsl:dsl + tcx], srcb.ap[:, sb + 2:sb + 2 + tcx],
            cw.ap[:, 2 * PT + ft:2 * PT + ft + 1],
            dstb.ap[:, dsl:dsl + tcx], ALU.mult, ALU.add), r=rd,
            w=[dstb] if last else ())

    bk.scope()
    xi_b = bk.buf("m_xib", [128, PT * (T + 2)], BF16)
    x3 = xi_b.ap.rearrange("p (ft t) -> p ft t", t=T + 2)
    bk.do(bk.V, lambda e: e.memset(x3[:, :, 0:1], 0.0), w=())
    bk.do(bk.V, lambda e: e.memset(x3[:, :, T + 1:T + 2], 0.0), w=())

    TCA, H2A = 256, 258
    NCA = T // TCA
    bk.scope()
    cpe1c = load_conv_cols(bk, "m_cpe1", W['L0_attn_cpe1_w'],
                           W['L0_attn_cpe1_b'])
    cpe1p = bk.buf("m_cpe1p", [128, PT], F32)
    bk.do(bk.V, lambda e: e.tensor_scalar_add(
        cpe1p.ap[:], cpe1c.ap[:, PT:2 * PT], 1.0), r=[cpe1c], w=[cpe1p])
    dwcc = load_conv_cols(bk, "m_dwc", W['L0_attn_dwc_w'], W['L0_attn_dwc_b'])
    n1g = load_cols(bk, "m_n1g", W['L0_attn_norm1_g'], PT)
    n1b = load_cols(bk, "m_n1b", W['L0_attn_norm1_b'], PT)
    actw = load_lhsT(bk, "m_actw", W['L0_attn_actp_w'], PT, 1024)
    actb = load_cols(bk, "m_actb", W['L0_attn_actp_b'], PT)
    inw = load_lhsT(bk, "m_inw", W['L0_attn_in_w'], PT, 1024)
    inb = load_cols(bk, "m_inb", W['L0_attn_in_b'], PT)
    xc = bk.buf("m_xc", [128, PT * TCA], F32)
    xcb = bk.buf("m_xcb", [128, PT * TCA], BF16)
    tmpA = bk.buf("m_tmpA", [128, PT * TCA], F32)
    xnf = bk.buf("m_xnf", [128, PT * TCA], F32)
    mmA = [bk.pbuf(f"m_mmA{i}", [128, TCA], F32) for i in range(2)]
    for ch in range(NCA):
        c0 = ch * TCA
        bk.do(bk.Y, lambda e, c0=c0: e.dma_start(
            out=sb3(xc.ap), in_=chunk_view(x_in, c0, TCA)), w=[xc], dma=True)
        for ft in range(PT):
            sl = (slice(None), slice(ft * TCA, (ft + 1) * TCA))
            bk.do(bk.V, lambda e, sl=sl: e.tensor_copy(xcb.ap[sl], xc.ap[sl]),
                  r=[xc], w=[xcb] if ft == PT - 1 else (),
                  after=xcb.dep_write() if ft == 0 else ())
        emit_ln_stats(bk, ln, ones_b, xcb, tc=TCA)
        emit_ln_apply(bk, ln, xc, tmpA, xnf, tc=TCA)
        bk.do(bk.Y, lambda e, c0=c0: e.dma_start(
            out=chunk_view(xn_h.ap, c0, TCA), in_=sb3(xnf.ap)),
            r=[xnf], w=[xn_h] if ch == NCA - 1 else (), dma=True,
            after=xn_h.dep_write() if ch == 0 else ())
    xh = bk.buf("m_xh", [128, PT * H2A], F32)
    yA = bk.buf("m_yA", [128, PT * TCA], F32)
    yb = bk.buf("m_yb", [128, PT * TCA], BF16)
    ynb = bk.buf("m_ynb", [128, PT * TCA], BF16)
    acb = bk.buf("m_acb", [128, PT * TCA], BF16)
    for ch in range(NCA):
        c0 = ch * TCA
        load_halo(bk, xh, xn_h.ap, c0, TCA)
        for ft in range(PT):
            conv_tile(yA, ft * TCA, xh, ft * H2A, cpe1c, ft, TCA,
                      w1p=cpe1p, last=(ft == PT - 1))
        bk.do(bk.Y, lambda e, c0=c0: e.dma_start(
            out=chunk_view(y_h.ap, c0, TCA), in_=sb3(yA.ap)),
            r=[yA], w=[y_h] if ch == NCA - 1 else (), dma=True,
            after=y_h.dep_write() if ch == 0 else ())
        for ft in range(PT):
            sl = (slice(None), slice(ft * TCA, (ft + 1) * TCA))
            bk.do(bk.V, lambda e, sl=sl: e.tensor_copy(yb.ap[sl], yA.ap[sl]),
                  r=[yA], w=[yb] if ft == PT - 1 else (),
                  after=yb.dep_write() if ft == 0 else ())
        emit_ln_stats(bk, ln, ones_b, yb, tc=TCA)
        for ft in range(PT):
            sl = (slice(None), slice(ft * TCA, (ft + 1) * TCA))
            bk.do(bk.V, lambda e, sl=sl: e.tensor_tensor(
                tmpA.ap[sl], yA.ap[sl], ln.r.ap[:, 0:TCA], ALU.mult),
                r=[yA, ln.r], w=[tmpA] if ft == PT - 1 else (),
                after=tmpA.dep_write() if ft == 0 else ())
        for ft in range(PT):
            sl = (slice(None), slice(ft * TCA, (ft + 1) * TCA))
            bk.do(bk.V, lambda e, sl=sl: e.tensor_tensor(
                tmpA.ap[sl], tmpA.ap[sl], ln.b2.ap[:, 0:TCA], ALU.add),
                r=[ln.b2, tmpA], w=[tmpA] if ft == PT - 1 else ())
        for ft in range(PT):
            sl = (slice(None), slice(ft * TCA, (ft + 1) * TCA))
            bk.do(bk.V, lambda e, sl=sl, ft=ft: e.tensor_scalar(
                ynb.ap[sl], tmpA.ap[sl], n1g.ap[:, ft:ft + 1],
                n1b.ap[:, ft:ft + 1], ALU.mult, ALU.add),
                r=[tmpA, n1g, n1b], w=[ynb] if ft == PT - 1 else (),
                after=ynb.dep_write() if ft == 0 else ())
        for fo in range(PT):
            ps = mmA[fo % 2]
            bk.mm_group([(lambda e, fo=fo, kt=kt, ps=ps: e.matmul(
                ps.ap[:], actw.ap[:, kt * 1024 + fo * 128:
                                 kt * 1024 + (fo + 1) * 128],
                ynb.ap[:, kt * TCA:(kt + 1) * TCA],
                start=(kt == 0), stop=(kt == PT - 1))) for kt in range(PT)],
                r=[actw, ynb], w=[ps])
            bk.do(bk.S, lambda e, fo=fo, ps=ps: e.activation(
                acb.ap[:, fo * TCA:(fo + 1) * TCA], ps.ap[:], AF.Silu,
                bias=actb.ap[:, fo:fo + 1]),
                r=[ps, actb], w=[acb] if fo == PT - 1 else (),
                after=acb.dep_write() if fo == 0 else ())
        bk.do(bk.Y, lambda e, c0=c0: e.dma_start(
            out=chunk_view(act_h.ap, c0, TCA), in_=sb3(acb.ap)),
            r=[acb], w=[act_h] if ch == NCA - 1 else (), dma=True,
            after=act_h.dep_write() if ch == 0 else ())
        for fo in range(PT):
            ps = mmA[fo % 2]
            bk.mm_group([(lambda e, fo=fo, kt=kt, ps=ps: e.matmul(
                ps.ap[:], inw.ap[:, kt * 1024 + fo * 128:
                                kt * 1024 + (fo + 1) * 128],
                ynb.ap[:, kt * TCA:(kt + 1) * TCA],
                start=(kt == 0), stop=(kt == PT - 1))) for kt in range(PT)],
                r=[inw, ynb], w=[ps])
            bk.do(bk.V, lambda e, fo=fo, ps=ps: e.tensor_scalar_add(
                xnf.ap[:, fo * TCA:(fo + 1) * TCA], ps.ap[:],
                inb.ap[:, fo:fo + 1]),
                r=[ps, inb], w=[xnf] if fo == PT - 1 else (),
                after=xnf.dep_write() if fo == 0 else ())
        bk.do(bk.Y, lambda e, c0=c0: e.dma_start(
            out=chunk_view(xi0_h.ap, c0, TCA), in_=sb3(xnf.ap)),
            r=[xnf], w=[xi0_h] if ch == NCA - 1 else (), dma=True,
            after=xi0_h.dep_write() if ch == 0 else ())
    for ch in range(NCA):
        c0 = ch * TCA
        load_halo(bk, xh, xi0_h.ap, c0, TCA)
        for ft in range(PT):
            conv_tile(tmpA, ft * TCA, xh, ft * H2A, dwcc, ft, TCA,
                      last=(ft == PT - 1))
        for ft in range(PT):
            bk.do(bk.S, lambda e, ft=ft, c0=c0: e.activation(
                xi_b.ap[:, ft * (T + 2) + 1 + c0:
                        ft * (T + 2) + 1 + c0 + TCA],
                tmpA.ap[:, ft * TCA:(ft + 1) * TCA], AF.Silu),
                r=[tmpA], w=[xi_b] if (ch, ft) == (NCA - 1, PT - 1) else (),
                after=xi_b.dep_write() if (ch, ft) == (0, 0) else ())
    bk.end_scope()

    bk.scope()
    qkw = load_lhsT(bk, "m_qkw", W['L0_attn_qk_w'], PT, 2048)
    qkb = load_cols(bk, "m_qkb", W['L0_attn_qk_b'], 16)
    q_ch = bk.buf("m_qch", [128, PT * TCB], BF16)
    k_ch = bk.buf("m_kch", [128, PT * TCB], BF16)
    e1 = bk.buf("m_e1", [128, TCB], F32)
    e2 = bk.buf("m_e2", [128, TCB], F32)
    e3 = bk.buf("m_e3", [128, TCB], F32)
    red = bk.buf("m_red", [128, 1], F32)
    ktm = bk.buf("m_ktm", [128, 4 * 1024], BF16)
    xitm = bk.buf("m_xitm", [128, 4 * 1024], BF16)
    mmB = [bk.pbuf(f"m_mmB{i}", [128, TCB], F32) for i in range(2)]
    kvps = bk.pbuf("m_kvps", [128, 512], F32)
    kvacc = bk.buf("m_kvacc", [128, 512], F32)
    tpsb = bk.pbuf("m_tpsb", [128, 128], BF16)
    bk.do(bk.V, lambda e: e.memset(kmean.ap[:], 0.0), w=[kmean])
    for ch in range(NCB):
        c0 = ch * TCB
        for fo2 in range(16):
            ps = mmB[fo2 % 2]
            bk.mm_group([(lambda e, fo2=fo2, kt=kt, ps=ps, c0=c0: e.matmul(
                ps.ap[:], qkw.ap[:, kt * 2048 + fo2 * 128:
                                kt * 2048 + (fo2 + 1) * 128],
                xi_b.ap[:, kt * (T + 2) + 1 + c0:
                        kt * (T + 2) + 1 + c0 + TCB],
                start=(kt == 0), stop=(kt == PT - 1))) for kt in range(PT)],
                r=[qkw, xi_b], w=[ps])
            bk.do(bk.V, lambda e, fo2=fo2, ps=ps: e.tensor_scalar(
                e1.ap[:], ps.ap[:], qkb.ap[:, fo2:fo2 + 1], 0.0,
                ALU.add, ALU.min), r=[ps, qkb], w=[e1])
            bk.do(bk.S, lambda e: e.activation(e2.ap[:], e1.ap[:], AF.Exp),
                  r=[e1], w=[e2])
            bk.do(bk.V, lambda e, fo2=fo2, ps=ps: e.tensor_scalar(
                e3.ap[:], ps.ap[:], qkb.ap[:, fo2:fo2 + 1], 0.0,
                ALU.add, ALU.max), r=[ps, qkb], w=[e3])
            dst, dc = (q_ch, fo2) if fo2 < 8 else (k_ch, fo2 - 8)
            bk.do(bk.V, lambda e, dst=dst, dc=dc: e.tensor_tensor(
                dst.ap[:, dc * TCB:(dc + 1) * TCB], e2.ap[:], e3.ap[:],
                ALU.add), r=[e2, e3], w=[dst] if dc == PT - 1 else (),
                after=dst.dep_write() if dc == 0 else ())
        bk.do(bk.Y, lambda e, c0=c0: e.dma_start(
            out=chunk_view(q_h.ap, c0, TCB), in_=sb3(q_ch.ap)),
            r=[q_ch], w=[q_h] if ch == NCB - 1 else (), dma=True,
            after=q_h.dep_write() if ch == 0 else ())
        if dbg and 'q' in dbg:
            bk.do(bk.Y, lambda e, c0=c0: e.dma_start(
                out=chunk_view(dbg['q'], c0, TCB), in_=sb3(q_ch.ap)),
                r=[q_ch], dma=True)
        for ft in range(PT):
            bk.do(bk.V, lambda e, ft=ft: e.reduce_sum(
                red.ap[:], k_ch.ap[:, ft * TCB:(ft + 1) * TCB], axis=AX.X),
                r=[k_ch], w=[red])
            bk.do(bk.V, lambda e, ft=ft: e.tensor_tensor(
                kmean.ap[:, ft:ft + 1], kmean.ap[:, ft:ft + 1], red.ap[:],
                ALU.add), r=[red], w=[kmean])
        for src, dst, ha in ((k_ch, ktm, False), (None, xitm, True)):
            for ft in range(PT):
                for tt in range(4):
                    if ha:
                        sap = xi_b.ap[:, ft * (T + 2) + 1 + c0 + tt * 128:
                                      ft * (T + 2) + 1 + c0 + (tt + 1) * 128]
                        rsrc = xi_b
                    else:
                        sap = src.ap[:, ft * TCB + tt * 128:
                                     ft * TCB + (tt + 1) * 128]
                        rsrc = src
                    bk.mm_group([lambda e, sap=sap: e.transpose(
                        tpsb.ap[:], sap, ident.ap[:])],
                        r=[rsrc, ident], w=[tpsb])
                    bk.do(bk.V, lambda e, ft=ft, tt=tt, dst=dst: e.tensor_copy(
                        dst.ap[:, tt * 1024 + ft * 128:
                               tt * 1024 + (ft + 1) * 128], tpsb.ap[:]),
                        r=[tpsb], w=[dst] if (ft, tt) == (PT - 1, 3) else (),
                        after=dst.dep_write() if (ft, tt) == (0, 0) else ())
        fns = []
        for h in range(16):
            pb, cb = 64 * (h % 2), 64 * (h // 2)
            for tt in range(4):
                fns.append(lambda e, h=h, tt=tt, pb=pb, cb=cb: e.matmul(
                    kvps.ap[pb:pb + 64, cb:cb + 64],
                    ktm.ap[:, tt * 1024 + h * 64:tt * 1024 + (h + 1) * 64],
                    xitm.ap[:, tt * 1024 + h * 64:tt * 1024 + (h + 1) * 64],
                    start=(h <= 1 and tt == 0), stop=(h == 15 and tt == 3),
                    skip_group_check=True))
        bk.mm_group(fns, r=[ktm, xitm], w=[kvps])
        if ch == 0:
            bk.do(bk.V, lambda e: e.tensor_copy(kvacc.ap[:], kvps.ap[:]),
                  r=[kvps], w=[kvacc])
        else:
            bk.do(bk.V, lambda e: e.tensor_tensor(
                kvacc.ap[:], kvacc.ap[:], kvps.ap[:], ALU.add),
                r=[kvps], w=[kvacc])
    bk.do(bk.V, lambda e: e.memset(kv_b.ap[:], 0.0), w=[kv_b])
    for h in range(16):
        pt, par = divmod(h, 2)
        pb, cb = 64 * par, 64 * (h // 2)
        db = pt * 128 + 64 * par
        bk.do(bk.V, lambda e, pb=pb, cb=cb, db=db: e.tensor_scalar_mul(
            kv_b.ap[pb:pb + 64, db:db + 64],
            kvacc.ap[pb:pb + 64, cb:cb + 64], 1.0 / T),
            r=[kvacc], w=[kv_b] if h == 15 else ())
    bk.do(bk.V, lambda e: e.tensor_scalar_mul(kmean.ap[:], kmean.ap[:],
                                              1.0 / T), w=[kmean])
    for ft in range(PT):
        bk.do(bk.V, lambda e, ft=ft: e.tensor_scalar_mul(
            m2.ap[:, ft * 128:(ft + 1) * 128], ind64.ap[:],
            kmean.ap[:, ft:ft + 1]), r=[ind64, kmean],
            w=[m2] if ft == PT - 1 else (),
            after=m2.dep_write() if ft == 0 else ())
    bk.end_scope()

    bk.scope()
    outw = load_lhsT(bk, "m_outw", W['L0_attn_out_w'], PT, 1024)
    outb = load_cols(bk, "m_outb", W['L0_attn_out_b'], PT)
    lepc = load_conv_cols(bk, "m_lep", W['L0_attn_lepe_w'],
                          W['L0_attn_lepe_b'])
    qC = bk.buf("m_qch2", [128, PT * TCB], BF16)
    a_ch = bk.buf("m_ach2", [128, PT * TCB], BF16)
    y_ch = bk.buf("m_ych", [128, PT * TCB], F32)
    zr = bk.buf("m_zr", [128, PT * TCB], F32)
    zden = bk.buf("m_zden", [128, TCB], F32)
    lt = bk.buf("m_lt", [128, TCB], F32)
    af = bk.buf("m_af", [128, TCB], F32)
    af0 = bk.buf("m_af0", [128, TCB], F32)
    gat = bk.buf("m_gat", [128, PT * TCB], BF16)
    y2 = bk.buf("m_y2", [128, PT * TCB], F32)
    mmC = [bk.pbuf(f"m_mmC{i}", [128, TCB], F32) for i in range(2)]
    for ch in range(NCB):
        c0 = ch * TCB
        bk.do(bk.Y, lambda e, c0=c0: e.dma_start(
            out=sb3(qC.ap), in_=chunk_view(q_h.ap, c0, TCB)),
            r=[q_h], w=[qC], dma=True)
        bk.do(bk.Y, lambda e, c0=c0: e.dma_start(
            out=sb3(a_ch.ap), in_=chunk_view(act_h.ap, c0, TCB)),
            r=[act_h], w=[a_ch], dma=True)
        bk.do(bk.Y, lambda e, c0=c0: e.dma_start(
            out=sb3(y_ch.ap), in_=chunk_view(y_h.ap, c0, TCB)),
            r=[y_h], w=[y_ch], dma=True)
        for ft in range(PT):
            ps = mmC[ft % 2]
            bk.mm_group([lambda e, ft=ft, ps=ps: e.matmul(
                ps.ap[:], m2.ap[:, ft * 128:(ft + 1) * 128],
                qC.ap[:, ft * TCB:(ft + 1) * TCB],
                start=True, stop=True)], r=[m2, qC], w=[ps])
            bk.do(bk.V, lambda e, ps=ps: e.tensor_scalar_add(
                zden.ap[:], ps.ap[:], 1e-6), r=[ps], w=[zden], drain=True)
            bk.do(bk.V, lambda e, ft=ft: e.reciprocal(
                zr.ap[:, ft * TCB:(ft + 1) * TCB], zden.ap[:]),
                r=[zden], w=[zr] if ft == PT - 1 else (), drain=True,
                after=zr.dep_write() if ft == 0 else ())
        bk.V.force_wait(zr.w)
        for ft in range(PT):
            ps = mmC[ft % 2]
            bk.mm_group([lambda e, ft=ft, ps=ps: e.matmul(
                ps.ap[:], kv_b.ap[:, ft * 128:(ft + 1) * 128],
                qC.ap[:, ft * TCB:(ft + 1) * TCB],
                start=True, stop=True)], r=[kv_b, qC], w=[ps])
            sb = ft * (T + 2) + c0
            bk.do(bk.V, lambda e, ft=ft, sb=sb: e.tensor_scalar(
                lt.ap[:], xi_b.ap[:, sb + 1:sb + 1 + TCB],
                lepc.ap[:, PT + ft:PT + ft + 1],
                lepc.ap[:, 3 * PT + ft:3 * PT + ft + 1], ALU.mult, ALU.add),
                r=[xi_b, lepc], w=[lt])
            bk.do(bk.V, lambda e, ft=ft, sb=sb: e.scalar_tensor_tensor(
                lt.ap[:], xi_b.ap[:, sb:sb + TCB],
                lepc.ap[:, ft:ft + 1], lt.ap[:], ALU.mult, ALU.add),
                r=[xi_b, lepc, lt], w=[lt])
            bk.do(bk.V, lambda e, ft=ft, sb=sb: e.scalar_tensor_tensor(
                lt.ap[:], xi_b.ap[:, sb + 2:sb + 2 + TCB],
                lepc.ap[:, 2 * PT + ft:2 * PT + ft + 1], lt.ap[:],
                ALU.mult, ALU.add), r=[xi_b, lepc, lt], w=[lt])
            bk.do(bk.S, lambda e, ps=ps: e.copy(af0.ap[:], ps.ap[:]),
                  r=[ps], w=[af0])
            bk.do(bk.V, lambda e, ft=ft: e.tensor_tensor(
                af.ap[:], af0.ap[:], zr.ap[:, ft * TCB:(ft + 1) * TCB],
                ALU.mult), r=[af0, zr], w=[af])
            bk.do(bk.V, lambda e: e.tensor_tensor(
                af.ap[:], af.ap[:], lt.ap[:], ALU.add), r=[lt, af], w=[af])
            if dbg and 'afd' in dbg:
                bk.do(bk.Y, lambda e, ch=ch, ft=ft: e.dma_start(
                    out=dbg['afd'][:, (ch * 8 + ft) * 512:
                                   (ch * 8 + ft + 1) * 512], in_=af.ap[:]),
                    r=[af], dma=True)
                bk.do(bk.Y, lambda e, ch=ch, ft=ft: e.dma_start(
                    out=dbg['af0d'][:, (ch * 8 + ft) * 512:
                                    (ch * 8 + ft + 1) * 512], in_=af0.ap[:]),
                    r=[af0], dma=True)
                bk.do(bk.Y, lambda e, ch=ch, ft=ft: e.dma_start(
                    out=dbg['ltd'][:, (ch * 8 + ft) * 512:
                                   (ch * 8 + ft + 1) * 512], in_=lt.ap[:]),
                    r=[lt], dma=True)
            bk.do(bk.V, lambda e, ft=ft: e.tensor_tensor(
                gat.ap[:, ft * TCB:(ft + 1) * TCB], af.ap[:],
                a_ch.ap[:, ft * TCB:(ft + 1) * TCB], ALU.mult),
                r=[af, a_ch], w=[gat] if ft == PT - 1 else (),
                after=gat.dep_write() if ft == 0 else ())
            if os.environ.get('MK_SERC3'):
                bk.barrier()
        for fo in range(PT):
            ps = mmC[fo % 2]
            bk.mm_group([(lambda e, fo=fo, kt=kt, ps=ps: e.matmul(
                ps.ap[:], outw.ap[:, kt * 1024 + fo * 128:
                                 kt * 1024 + (fo + 1) * 128],
                gat.ap[:, kt * TCB:(kt + 1) * TCB],
                start=(kt == 0), stop=(kt == PT - 1))) for kt in range(PT)],
                r=[outw, gat], w=[ps])
            bk.do(bk.V, lambda e, fo=fo, ps=ps: e.scalar_tensor_tensor(
                y2.ap[:, fo * TCB:(fo + 1) * TCB], ps.ap[:],
                outb.ap[:, fo:fo + 1], y_ch.ap[:, fo * TCB:(fo + 1) * TCB],
                ALU.add, ALU.add), r=[ps, outb, y_ch],
                w=[y2] if fo == PT - 1 else (),
                after=y2.dep_write() if fo == 0 else ())
        bk.do(bk.Y, lambda e, c0=c0: e.dma_start(
            out=chunk_view(y2_h.ap, c0, TCB), in_=sb3(y2.ap)),
            r=[y2], w=[y2_h] if ch == NCB - 1 else (), dma=True,
            after=y2_h.dep_write() if ch == 0 else ())
        if os.environ.get('MK_SERC'):
            bk.barrier()
        if dbg and 'zrd' in dbg:
            bk.do(bk.Y, lambda e, ch=ch: e.dma_start(
                out=dbg['zrd'][:, ch * 4096:(ch + 1) * 4096], in_=zr.ap[:]),
                r=[zr], dma=True)
        if dbg and 'gat' in dbg:
            bk.do(bk.Y, lambda e, ch=ch: e.dma_start(
                out=dbg['gat'][:, ch * 4096:(ch + 1) * 4096], in_=gat.ap[:]),
                r=[gat], dma=True)
    bk.end_scope()
    bk.end_scope()

    x_i = hb("m_xint")
    bk.scope()
    xcp = bk.buf("m_xcp", [128, PT * 512], F32)
    for ch in range(8):
        c0 = ch * 512
        bk.do(bk.Y, lambda e, c0=c0: e.dma_start(
            out=sb3(xcp.ap, 512), in_=chunk_view(x_in, c0, 512)),
            w=[xcp], dma=True)
        bk.do(bk.Y, lambda e, c0=c0: e.dma_start(
            out=chunk_view(x_i.ap, c0, 512), in_=sb3(xcp.ap, 512)),
            r=[xcp], w=[x_i] if ch == 7 else (), dma=True,
            after=x_i.dep_write() if ch == 0 else ())
    bk.end_scope(barrier=True)
    bk.scope()
    TCD = 256
    NCD = T // TCD
    H2D = TCD + 2
    fc1w = load_lhsT(bk, "m_fc1w", W['L0_attn_fc1_w'], PT, 4096)
    fc2w = load_lhsT(bk, "m_fc2w", W['L0_attn_fc2_w'], 32, 1024)
    fc1b = load_cols(bk, "m_fc1b", W['L0_attn_fc1_b'], 32)
    fc2b = load_cols(bk, "m_fc2b", W['L0_attn_fc2_b'], PT)
    cpe2c = load_conv_cols(bk, "m_cpe2", W['L0_attn_cpe2_w'],
                           W['L0_attn_cpe2_b'])
    cpe2p = bk.buf("m_cpe2p", [128, PT], F32)
    bk.do(bk.V, lambda e: e.tensor_scalar_add(
        cpe2p.ap[:], cpe2c.ap[:, PT:2 * PT], 1.0), r=[cpe2c], w=[cpe2p])
    n2g = load_cols(bk, "m_n2g", W['L0_attn_norm2_g'], PT)
    n2b = load_cols(bk, "m_n2b", W['L0_attn_norm2_b'], PT)
    y2h = bk.buf("m_y2hh", [128, PT * H2D], F32)
    y2c = bk.buf("m_y2c", [128, PT * TCD], F32)
    y2b = bk.buf("m_y2b", [128, PT * TCD], BF16)
    yn2 = bk.buf("m_yn2", [128, PT * TCD], BF16)
    hbuf = bk.buf("m_hbuf", [128, 32 * TCD], BF16)
    xcD = bk.buf("m_xcD", [128, PT * TCD], F32)
    tmpD = bk.buf("m_tmpD", [128, PT * TCD], F32)
    mmD = [bk.pbuf(f"m_mmD{i}", [128, TCD], F32) for i in range(2)]
    out_toks = []
    for ch in range(NCD):
        c0 = ch * TCD
        load_halo(bk, y2h, y2_h.ap, c0, TCD)
        for ft in range(PT):
            conv_tile(y2c, ft * TCD, y2h, ft * H2D, cpe2c, ft, TCD,
                      w1p=cpe2p, last=(ft == PT - 1))
        for ft in range(PT):
            sl = (slice(None), slice(ft * TCD, (ft + 1) * TCD))
            bk.do(bk.V, lambda e, sl=sl: e.tensor_copy(y2b.ap[sl],
                                                       y2c.ap[sl]),
                  r=[y2c], w=[y2b] if ft == PT - 1 else (),
                  after=y2b.dep_write() if ft == 0 else ())
        emit_ln_stats(bk, ln, ones_b, y2b, tc=TCD)
        for ft in range(PT):
            sl = (slice(None), slice(ft * TCD, (ft + 1) * TCD))
            bk.do(bk.V, lambda e, sl=sl: e.tensor_tensor(
                tmpD.ap[sl], y2c.ap[sl], ln.r.ap[:, 0:TCD], ALU.mult),
                r=[y2c, ln.r], w=[tmpD] if ft == PT - 1 else (),
                after=tmpD.dep_write() if ft == 0 else ())
        for ft in range(PT):
            sl = (slice(None), slice(ft * TCD, (ft + 1) * TCD))
            bk.do(bk.V, lambda e, sl=sl: e.tensor_tensor(
                tmpD.ap[sl], tmpD.ap[sl], ln.b2.ap[:, 0:TCD], ALU.add),
                r=[ln.b2, tmpD], w=[tmpD] if ft == PT - 1 else ())
        for ft in range(PT):
            sl = (slice(None), slice(ft * TCD, (ft + 1) * TCD))
            bk.do(bk.V, lambda e, sl=sl, ft=ft: e.tensor_scalar(
                yn2.ap[sl], tmpD.ap[sl], n2g.ap[:, ft:ft + 1],
                n2b.ap[:, ft:ft + 1], ALU.mult, ALU.add),
                r=[tmpD, n2g, n2b], w=[yn2] if ft == PT - 1 else (),
                after=yn2.dep_write() if ft == 0 else ())
        for fo in range(32):
            ps = mmD[fo % 2]
            bk.mm_group([(lambda e, fo=fo, kt=kt, ps=ps: e.matmul(
                ps.ap[:], fc1w.ap[:, kt * 4096 + fo * 128:
                                 kt * 4096 + (fo + 1) * 128],
                yn2.ap[:, kt * TCD:(kt + 1) * TCD],
                start=(kt == 0), stop=(kt == PT - 1))) for kt in range(PT)],
                r=[fc1w, yn2], w=[ps])
            bk.do(bk.S, lambda e, fo=fo, ps=ps: e.activation(
                hbuf.ap[:, fo * TCD:(fo + 1) * TCD], ps.ap[:], AF.Gelu,
                bias=fc1b.ap[:, fo:fo + 1]),
                r=[ps, fc1b], w=[hbuf] if fo == 31 else (),
                after=hbuf.dep_write() if fo == 0 else ())
        for fo in range(PT):
            ps = mmD[fo % 2]
            bk.mm_group([(lambda e, fo=fo, kt=kt, ps=ps: e.matmul(
                ps.ap[:], fc2w.ap[:, kt * 1024 + fo * 128:
                                 kt * 1024 + (fo + 1) * 128],
                hbuf.ap[:, kt * TCD:(kt + 1) * TCD],
                start=(kt == 0), stop=(kt == 31))) for kt in range(32)],
                r=[fc2w, hbuf], w=[ps])
            bk.do(bk.V, lambda e, fo=fo, ps=ps: e.scalar_tensor_tensor(
                y2c.ap[:, fo * TCD:(fo + 1) * TCD], ps.ap[:],
                fc2b.ap[:, fo:fo + 1], y2c.ap[:, fo * TCD:(fo + 1) * TCD],
                ALU.add, ALU.add), r=[ps, fc2b, y2c],
                w=[y2c] if fo == PT - 1 else ())
        for ft in range(PT):
            sl = (slice(None), slice(ft * TCD, (ft + 1) * TCD))
            bk.do(bk.V, lambda e, sl=sl: e.tensor_copy(y2b.ap[sl],
                                                       y2c.ap[sl]),
                  r=[y2c], w=[y2b] if ft == PT - 1 else (),
                  after=y2b.dep_write() if ft == 0 else ())
        emit_ln_stats(bk, ln, ones_b, y2b, tc=TCD)
        bk.do(bk.Y, lambda e, c0=c0: e.dma_start(
            out=sb3(xcD.ap, TCD), in_=chunk_view(x_i.ap, c0, TCD)),
            r=[x_i], w=[xcD], dma=True)
        emit_ln_apply(bk, ln, y2c, tmpD, tmpD, tc=TCD, res=xcD)
        out_toks.append(bk.do(bk.Y, lambda e, c0=c0: e.dma_start(
            out=chunk_view(x_out, c0, TCD), in_=sb3(tmpD.ap, TCD)),
            r=[tmpD, xcD], dma=True))
    bk.end_scope(barrier=bool(dbg and 'qq' in dbg))
    if dbg and 'qq' in dbg:
        bk.scope()
        bq = bk.buf("m_dbgq", [128, PT * 256], BF16)
        for ch2 in range(16):
            c02 = ch2 * 256
            bk.do(bk.Y, lambda e, c02=c02: e.dma_start(
                out=sb3(bq.ap, 256), in_=chunk_view(q_h.ap, c02, 256)),
                r=[q_h], w=[bq], dma=True)
            bk.do(bk.Y, lambda e, c02=c02: e.dma_start(
                out=dbg['qq'].rearrange("(ft p) t -> p ft t", p=128)
                [:, :, c02:c02 + 256], in_=sb3(bq.ap, 256)),
                r=[bq], dma=True)
        bk.end_scope(barrier=False)
    bk.end_scope(barrier=False)
    return out_toks




def emit_l0_ffn(bk: BK, W, consts, x_in, x_out, dbg=None):
    nc = bk.nc
    TCB = 512
    NCB = T // TCB
    bk.scope()
    ones_b = bk.buf("ones_f", [128, 128], BF16)
    bk.do(bk.V, lambda e: e.memset(ones_b.ap[:], 1.0), w=[ones_b])
    e8 = bk.buf("e8b", [8, 1024], BF16)
    bk.do(bk.G, lambda e: e.dma_start(out=e8.ap[:], in_=consts['e8'][:]),
          w=[e8], dma=True)
    gnb = bk.buf("f_gnb", [8, T], BF16)
    hb = lambda n, dt, shape: Buf(nc.dram_tensor(n, shape, dt,
                                                 kind="Internal").ap(), n)
    acc_h = hb("f_acch", F32, [C, T])
    acc_p = [Buf(acc_h.ap, f"f_acc{i}") for i in range(T // 256)]

    def accp(c0, tcx):
        return acc_p[c0 // 256:(c0 + tcx) // 256]
    ymid_h = hb("f_ymidh", BF16, [C, T])
    q_h = hb("f_qh", BF16, [C, T])
    k_h = hb("f_kh", BF16, [C, T])
    vt_h = hb("f_vth", BF16, [T, C])

    bk.scope()
    ln = LNScratch(bk, "f0")
    xn_b = bk.buf("f_xnb", [128, PT * T], BF16)
    xc = bk.buf("f_xc", [128, PT * TCB], F32)
    xcb = bk.buf("f_xcb", [128, PT * TCB], BF16)
    tmp = bk.buf("f_tmp", [128, PT * TCB], F32)
    xnf = bk.buf("f_xnf", [128, PT * TCB], F32)
    for ch in range(NCB):
        c0 = ch * TCB
        bk.do(bk.Y, lambda e, c0=c0: e.dma_start(
            out=sb3(xc.ap), in_=chunk_view(x_in, c0, TCB)), w=[xc], dma=True)
        for ft in range(PT):
            sl = (slice(None), slice(ft * TCB, (ft + 1) * TCB))
            bk.do(bk.V, lambda e, sl=sl: e.tensor_copy(xcb.ap[sl], xc.ap[sl]),
                  r=[xc], w=[xcb] if ft == PT - 1 else (),
                  after=xcb.dep_write() if ft == 0 else ())
        emit_ln_stats(bk, ln, ones_b, xcb)
        emit_ln_apply(bk, ln, xc, tmp, xnf)
        bk.do(bk.Y, lambda e, c0=c0: e.dma_start(
            out=chunk_view(acc_h.ap, c0, TCB), in_=sb3(xnf.ap)),
            r=[xnf], w=accp(c0, TCB), dma=True)
        for ft in range(PT):
            dsl = (slice(None), slice(ft * T + c0, ft * T + c0 + TCB))
            sl = (slice(None), slice(ft * TCB, (ft + 1) * TCB))
            bk.do(bk.V, lambda e, sl=sl, dsl=dsl: e.tensor_copy(
                xn_b.ap[dsl], xnf.ap[sl]),
                r=[xnf], w=[xn_b] if (ch, ft) == (NCB - 1, PT - 1) else (),
                after=xn_b.dep_write() if (ch, ft) == (0, 0) else ())
    wq = load_lhsT(bk, "f_wq", W['L0_ffn_Wq'], PT, 1024)
    wk = load_lhsT(bk, "f_wk", W['L0_ffn_Wk'], PT, 1024)
    wv = load_lhsT(bk, "f_wv", W['L0_ffn_Wv'], PT, 1024)
    bqc = load_cols(bk, "f_bq", W['L0_ffn_bq'], PT)
    bkc = load_cols(bk, "f_bk", W['L0_ffn_bk'], PT)
    prj = bk.buf("f_prj", [128, PT * TCB], BF16)
    mmF = [bk.pbuf(f"f_mmF{i}", [128, TCB], F32) for i in range(2)]
    for ch in range(NCB):
        c0 = ch * TCB
        for wmat, bc, dst_h in ((wq, bqc, q_h), (wk, bkc, k_h)):
            for fo in range(PT):
                ps = mmF[fo % 2]
                bk.mm_group([(lambda e, fo=fo, kt=kt, ps=ps, c0=c0,
                              wmat=wmat: e.matmul(
                    ps.ap[:], wmat.ap[:, kt * 1024 + fo * 128:
                                     kt * 1024 + (fo + 1) * 128],
                    xn_b.ap[:, kt * T + c0:kt * T + c0 + TCB],
                    start=(kt == 0), stop=(kt == PT - 1)))
                    for kt in range(PT)], r=[wmat, xn_b], w=[ps])
                bk.do(bk.V, lambda e, fo=fo, ps=ps, bc=bc: e.tensor_scalar(
                    prj.ap[:, fo * TCB:(fo + 1) * TCB], ps.ap[:],
                    bc.ap[:, fo:fo + 1], None, ALU.add),
                    r=[ps, bc], w=[prj] if fo == PT - 1 else (),
                    after=prj.dep_write() if fo == 0 else ())
            bk.do(bk.Y, lambda e, c0=c0, dst_h=dst_h: e.dma_start(
                out=chunk_view(dst_h.ap, c0, TCB), in_=sb3(prj.ap)),
                r=[prj], w=[dst_h] if ch == NCB - 1 else (), dma=True,
                after=dst_h.dep_write() if ch == 0 else ())
        for ts4 in range(4):
            for fo2 in range(2):
                ps = mmF[fo2 % 2]
                bk.mm_group([(lambda e, ts4=ts4, fo2=fo2, kt=kt, ps=ps,
                              c0=c0: e.matmul(
                    ps.ap[:], xn_b.ap[:, kt * T + c0 + ts4 * 128:
                                     kt * T + c0 + (ts4 + 1) * 128],
                    wv.ap[:, kt * 1024 + fo2 * 512:
                          kt * 1024 + (fo2 + 1) * 512],
                    start=(kt == 0), stop=(kt == PT - 1)))
                    for kt in range(PT)], r=[wv, xn_b], w=[ps])
                bk.do(bk.V, lambda e, ts4=ts4, fo2=fo2, ps=ps: e.tensor_copy(
                    prj.ap[:, ts4 * 1024 + fo2 * 512:
                           ts4 * 1024 + (fo2 + 1) * 512], ps.ap[:]),
                    r=[ps], w=[prj] if (ts4, fo2) == (3, 1) else (),
                    after=prj.dep_write() if (ts4, fo2) == (0, 0) else ())
        bk.do(bk.Y, lambda e, c0=c0: e.dma_start(
            out=vt_h.ap[c0:c0 + TCB, :].rearrange(
                "(s p) c -> p s c", p=128),
            in_=prj.ap.rearrange("p (s c) -> p s c", c=1024)),
            r=[prj], w=[vt_h] if ch == NCB - 1 else (), dma=True,
            after=vt_h.dep_write() if ch == 0 else ())
    bk.end_scope()

    QTC = 256
    NQB = T // QTC
    bk.scope()
    k_b = bk.buf("f_kb", [128, PT * T], BF16)
    v_tb = bk.buf("f_vtb", [128, 32 * 1024], BF16)
    bk.do(bk.Y, lambda e: e.dma_start(
        out=k_b.ap.rearrange("p (ft t) -> p ft t", t=T),
        in_=k_h.ap.rearrange("(ft p) t -> p ft t", p=128)),
        r=[k_h], w=[k_b], dma=True)
    bk.do(bk.Y, lambda e: e.dma_start(
        out=v_tb.ap.rearrange("p (s c) -> p s c", c=1024),
        in_=vt_h.ap.rearrange("(s p) c -> p s c", p=128)),
        r=[vt_h], w=[v_tb], dma=True)
    wo = load_lhsT(bk, "f_wo", W['L0_ffn_Wo'], PT, 1024)
    bvc = load_cols(bk, "f_bv", W['L0_ffn_bv'], PT)
    boc = load_cols(bk, "f_bo", W['L0_ffn_bo'], PT)
    bvb = bk.buf("f_bvb", [128, PT], BF16)
    for ft in range(PT):
        bk.do(bk.V, lambda e, ft=ft: e.tensor_copy(
            bvb.ap[:, ft:ft + 1], bvc.ap[:, ft:ft + 1]), r=[bvc],
            w=[bvb] if ft == PT - 1 else (),
            after=bvb.dep_write() if ft == 0 else ())
    bvps = bk.pbuf("f_bvps", [128, 8], F32)
    for fo in range(PT):
        bk.mm_group([(lambda e, fo=fo, kt=kt: e.matmul(
            bvps.ap[:, fo:fo + 1],
            wo.ap[:, kt * 1024 + fo * 128:kt * 1024 + (fo + 1) * 128],
            bvb.ap[:, kt:kt + 1],
            start=(kt == 0), stop=(kt == PT - 1))) for kt in range(PT)],
            r=[wo, bvb], w=() if fo < PT - 1 else [bvps],
            after=bvps.dep_write() if fo == 0 else ())
    bk.do(bk.V, lambda e: e.tensor_tensor(
        boc.ap[:], boc.ap[:], bvps.ap[:], ALU.add), r=[bvps], w=[boc])
    q_ch = bk.buf("f_qch", [128, PT * QTC], BF16)
    xnc = bk.buf("f_xnc", [128, PT * QTC], F32)
    p_t = bk.buf("f_pt", [128, 32 * QTC], BF16)
    zr = bk.buf("f_zr", [128, QTC], F32)
    a_b = bk.buf("f_ab", [128, PT * QTC], BF16)
    a_f = bk.buf("f_af", [128, PT * QTC], F32)
    mmS = [bk.pbuf(f"f_mmS{i}", [128, QTC], F32) for i in range(2)]
    mmP = [bk.pbuf(f"f_mmP{i}", [128, QTC], F32) for i in range(2)]
    zps = bk.pbuf("f_zps", [128, QTC], F32)
    SCALE = 1.0 / 32.0
    for ch in range(NQB):
        c0 = ch * QTC
        bk.do(bk.Y, lambda e, c0=c0: e.dma_start(
            out=sb3(q_ch.ap, QTC), in_=chunk_view(q_h.ap, c0, QTC)),
            r=[q_h], w=[q_ch], dma=True)
        z_tok = None
        for kt in range(32):
            ps = mmS[kt % 2]
            bk.mm_group([(lambda e, kt=kt, ft=ft, ps=ps: e.matmul(
                ps.ap[:], k_b.ap[:, ft * T + kt * 128:ft * T + (kt + 1) * 128],
                q_ch.ap[:, ft * QTC:(ft + 1) * QTC],
                start=(ft == 0), stop=(ft == PT - 1),
                skip_group_check=True)) for ft in range(PT)],
                r=[k_b, q_ch], w=[ps])
            e_tok = bk.do(bk.S, lambda e, kt=kt, ps=ps: e.activation(
                p_t.ap[:, kt * QTC:(kt + 1) * QTC], ps.ap[:], AF.Exp,
                scale=SCALE), r=[ps],
                w=[p_t] if kt == 31 else (),
                after=p_t.dep_write() if kt == 0 else ())
            z_tok = bk.mm_group([lambda e, kt=kt: e.matmul(
                zps.ap[:], ones_b.ap, p_t.ap[:, kt * QTC:(kt + 1) * QTC],
                start=(kt == 0), stop=(kt == 31), skip_group_check=True)],
                r=[ones_b], after=[e_tok] + (
                    zps.dep_write() if kt == 0 else []))
        zps.w = z_tok
        zps.rs = []
        bk.do(bk.V, lambda e: e.reciprocal(zr.ap[:], zps.ap[:]),
              r=[zps], w=[zr])
        for ft in range(PT):
            ps = mmP[ft % 2]
            bk.mm_group([(lambda e, ft=ft, kt=kt, ps=ps: e.matmul(
                ps.ap[:], v_tb.ap[:, kt * 1024 + ft * 128:
                                 kt * 1024 + (ft + 1) * 128],
                p_t.ap[:, kt * QTC:(kt + 1) * QTC],
                start=(kt == 0), stop=(kt == 31),
                skip_group_check=True)) for kt in range(32)],
                r=[v_tb, p_t], w=[ps])
            bk.do(bk.V, lambda e, ft=ft, ps=ps: e.tensor_tensor(
                a_b.ap[:, ft * QTC:(ft + 1) * QTC], ps.ap[:], zr.ap[:],
                ALU.mult), r=[ps, zr], w=[a_b] if ft == PT - 1 else (),
                after=a_b.dep_write() if ft == 0 else ())
        if dbg and 'ab' in dbg and ch == 0:
            bk.do(bk.Y, lambda e: e.dma_start(out=dbg['ab'], in_=a_b.ap[:]),
                  r=[a_b], dma=True)
            bk.do(bk.Y, lambda e: e.dma_start(out=dbg['zr'], in_=zr.ap[:]),
                  r=[zr], dma=True)
            bk.do(bk.Y, lambda e: e.dma_start(out=dbg['pt'], in_=p_t.ap[:]),
                  r=[p_t], dma=True)
        bk.do(bk.Y, lambda e, c0=c0: e.dma_start(
            out=sb3(xnc.ap, QTC), in_=chunk_view(acc_h.ap, c0, QTC)),
            r=accp(c0, QTC), w=[xnc], dma=True)
        for fo in range(PT):
            ps = mmS[fo % 2]
            bk.mm_group([(lambda e, fo=fo, kt=kt, ps=ps: e.matmul(
                ps.ap[:], wo.ap[:, kt * 1024 + fo * 128:
                               kt * 1024 + (fo + 1) * 128],
                a_b.ap[:, kt * QTC:(kt + 1) * QTC],
                start=(kt == 0), stop=(kt == PT - 1))) for kt in range(PT)],
                r=[wo, a_b], w=[ps])
            bk.do(bk.V, lambda e, fo=fo, ps=ps: e.tensor_scalar(
                a_f.ap[:, fo * QTC:(fo + 1) * QTC], ps.ap[:],
                boc.ap[:, fo:fo + 1], None, ALU.add),
                r=[ps, boc], w=[a_f] if fo == PT - 1 else (),
                after=a_f.dep_write() if fo == 0 else ())
        for ft in range(PT):
            sl = (slice(None), slice(ft * QTC, (ft + 1) * QTC))
            bk.do(bk.V, lambda e, sl=sl: e.tensor_tensor(
                q_ch.ap[sl], a_f.ap[sl], xnc.ap[sl], ALU.add),
                r=[a_f, xnc], w=[q_ch] if ft == PT - 1 else (),
                after=q_ch.dep_write() if ft == 0 else ())
        bk.do(bk.Y, lambda e, c0=c0: e.dma_start(
            out=chunk_view(ymid_h.ap, c0, QTC), in_=sb3(q_ch.ap, QTC)),
            r=[q_ch], w=[ymid_h] if ch == NQB - 1 else (), dma=True,
            after=ymid_h.dep_write() if ch == 0 else ())
        bk.do(bk.G, lambda e, c0=c0: e.dma_start(
            out=chunk_view(acc_h.ap, c0, QTC), in_=sb3(a_f.ap, QTC),
            accum_op=ALU.add), r=[a_f], w=accp(c0, QTC), dma=True)
    if dbg and 'ymid' in dbg:
        bk.do(bk.G, lambda e: e.dma_start(out=dbg['ymid'], in_=ymid_h.ap[:]),
              r=[ymid_h], dma=True)
    bk.end_scope()

    bk.scope()
    lng = LNScratch(bk, "fg")
    wg = load_lhsT(bk, "f_wg", W['L0_ffn_Wg'], PT, 8)
    bg8 = bk.buf("f_bg8", [8, 1], F32)
    bk.do(bk.Y, lambda e: e.dma_start(out=bg8.ap[:],
                                      in_=W['L0_ffn_bg'].unsqueeze(1)),
          w=[bg8], dma=True)
    ymc = bk.buf("f_ymc", [128, PT * TCB], BF16)
    ge = bk.buf("f_ge", [8, TCB], BF16)
    rzg = bk.buf("f_rzg", [128, TCB], F32)
    gps = bk.pbuf("f_gps", [8, TCB], F32)
    zgps = bk.pbuf("f_zgps", [128, TCB], F32)
    for ch in range(NCB):
        c0 = ch * TCB
        bk.do(bk.Y, lambda e, c0=c0: e.dma_start(
            out=sb3(ymc.ap), in_=chunk_view(ymid_h.ap, c0, TCB)),
            r=[ymid_h], w=[ymc], dma=True)
        bk.mm_group([(lambda e, kt=kt: e.matmul(
            gps.ap[:], wg.ap[:, kt * 8:(kt + 1) * 8],
            ymc.ap[:, kt * TCB:(kt + 1) * TCB],
            start=(kt == 0), stop=(kt == PT - 1))) for kt in range(PT)],
            r=[wg, ymc], w=[gps])
        bk.do(bk.S, lambda e: e.activation(
            ge.ap[:], gps.ap[:], AF.Exp, bias=bg8.ap[:, 0:1]),
            r=[gps, bg8], w=[ge])
        bk.mm_group([lambda e: e.matmul(
            zgps.ap[:], ones_b.ap[0:8, :], ge.ap[:],
            start=True, stop=True)], r=[ge, ones_b], w=[zgps])
        bk.do(bk.V, lambda e: e.reciprocal(rzg.ap[:], zgps.ap[:]),
              r=[zgps], w=[rzg])
        bk.do(bk.V, lambda e, c0=c0: e.tensor_tensor(
            gnb.ap[:, c0:c0 + TCB], ge.ap[:], rzg.ap[0:8, :], ALU.mult),
            r=[ge, rzg], w=[gnb] if ch == NCB - 1 else (),
            after=gnb.dep_write() if ch == 0 else ())
    if dbg and 'gnb' in dbg:
        bk.do(bk.G, lambda e: e.dma_start(out=dbg['gnb'], in_=gnb.ap[:]),
              r=[gnb], dma=True)
    bk.end_scope()

    TCM = 512
    NCM = T // TCM
    bk.scope()
    b2sb = bk.buf("f_b2sb", [8, 1024], BF16)
    bk.do(bk.G, lambda e: e.dma_start(out=b2sb.ap[:], in_=W['L0_ffn_b2'][:]),
          w=[b2sb], dma=True)
    ymc3 = bk.buf("f_ymc33", [128, PT * TCM], BF16)
    grep = bk.buf("f_grep", [128, TCM], BF16)
    hgb = bk.buf("f_hgb", [128, 32 * TCM], BF16)
    htmp = bk.buf("f_htmp", [128, TCM], BF16)
    stg = bk.buf("f_stg", [128, PT * TCM], F32)
    mmM = [bk.pbuf(f"f_mmM{i}", [128, TCM], F32) for i in range(2)]
    gbps = bk.pbuf("f_gbps", [128, TCM], F32)
    for ex in range(8):
        bk.scope()
        w1e = load_lhsT(bk, f"f_w1_{ex}", W['L0_ffn_W1'][ex], PT, 4096)
        w2e = load_lhsT(bk, f"f_w2_{ex}", W['L0_ffn_W2'][ex], 32, 1024)
        b1c = load_cols(bk, f"f_b1_{ex}", W['L0_ffn_b1'][ex], 32)
        for ch in range(NCM):
            c0 = ch * TCM
            bk.do(bk.Y, lambda e, c0=c0: e.dma_start(
                out=sb3(ymc3.ap, TCM), in_=chunk_view(ymid_h.ap, c0, TCM)),
                r=[ymid_h], w=[ymc3], dma=True)
            bk.mm_group([lambda e, ex=ex, c0=c0: e.matmul(
                gbps.ap[:], e8.ap[:, ex * 128:(ex + 1) * 128],
                gnb.ap[:, c0:c0 + TCM], start=True, stop=True)],
                r=[e8, gnb], w=[gbps])
            bk.do(bk.V, lambda e: e.tensor_copy(grep.ap[:], gbps.ap[:]),
                  r=[gbps], w=[grep])
            for fo in range(32):
                ps = mmM[fo % 2]
                bk.mm_group([(lambda e, fo=fo, kt=kt, ps=ps: e.matmul(
                    ps.ap[:], w1e.ap[:, kt * 4096 + fo * 128:
                                    kt * 4096 + (fo + 1) * 128],
                    ymc3.ap[:, kt * TCM:(kt + 1) * TCM],
                    start=(kt == 0), stop=(kt == PT - 1)))
                    for kt in range(PT)], r=[w1e, ymc3], w=[ps])
                bk.do(bk.S, lambda e, fo=fo, ps=ps: e.activation(
                    htmp.ap[:], ps.ap[:], AF.Relu,
                    bias=b1c.ap[:, fo:fo + 1]), r=[ps, b1c], w=[htmp])
                bk.do(bk.V, lambda e, fo=fo: e.tensor_tensor(
                    hgb.ap[:, fo * TCM:(fo + 1) * TCM], htmp.ap[:],
                    grep.ap[:], ALU.mult), r=[htmp, grep],
                    w=[hgb] if fo == 31 else (),
                    after=hgb.dep_write() if fo == 0 else ())
            for fo in range(PT):
                ps = mmM[fo % 2]
                fns = [(lambda e, fo=fo, kt=kt, ps=ps: e.matmul(
                    ps.ap[:], w2e.ap[:, kt * 1024 + fo * 128:
                                    kt * 1024 + (fo + 1) * 128],
                    hgb.ap[:, kt * TCM:(kt + 1) * TCM],
                    start=(kt == 0), stop=(kt == 31 and ex != 0)))
                    for kt in range(32)]
                if ex == 0:
                    fns.append(lambda e, fo=fo, ps=ps, c0=c0: e.matmul(
                        ps.ap[:], b2sb.ap[:, fo * 128:(fo + 1) * 128],
                        gnb.ap[:, c0:c0 + TCM], start=False, stop=True))
                bk.mm_group(fns, r=[w2e, hgb, b2sb, gnb], w=[ps])
                bk.do(bk.V, lambda e, fo=fo, ps=ps: e.tensor_copy(
                    stg.ap[:, fo * TCM:(fo + 1) * TCM], ps.ap[:]),
                    r=[ps], w=[stg] if fo == PT - 1 else (),
                    after=stg.dep_write() if fo == 0 else ())
            bk.do(bk.G, lambda e, c0=c0: e.dma_start(
                out=chunk_view(acc_h.ap, c0, TCM), in_=sb3(stg.ap, TCM),
                accum_op=ALU.add), r=[stg], w=accp(c0, TCM), dma=True)
        bk.end_scope()
    bk.end_scope()

    bk.scope()
    ln4 = LNScratch(bk, "f4")
    bo_f = bk.buf("f_bof", [128, PT * TCB], F32)
    bo_b = bk.buf("f_bob", [128, PT * TCB], BF16)
    xc4 = bk.buf("f_xc4", [128, PT * TCB], F32)
    tmp4 = bk.buf("f_tmp4", [128, PT * TCB], F32)
    out_toks = []
    for ch in range(NCB):
        c0 = ch * TCB
        bk.do(bk.Y, lambda e, c0=c0: e.dma_start(
            out=sb3(bo_f.ap), in_=chunk_view(acc_h.ap, c0, TCB)),
            r=accp(c0, TCB), w=[bo_f], dma=True)
        for ft in range(PT):
            sl = (slice(None), slice(ft * TCB, (ft + 1) * TCB))
            bk.do(bk.V, lambda e, sl=sl: e.tensor_copy(bo_b.ap[sl],
                                                       bo_f.ap[sl]),
                  r=[bo_f], w=[bo_b] if ft == PT - 1 else (),
                  after=bo_b.dep_write() if ft == 0 else ())
        emit_ln_stats(bk, ln4, ones_b, bo_b)
        bk.do(bk.Y, lambda e, c0=c0: e.dma_start(
            out=sb3(xc4.ap), in_=chunk_view(x_in, c0, TCB)),
            w=[xc4], dma=True)
        emit_ln_apply(bk, ln4, bo_f, tmp4, tmp4, res=xc4)
        out_toks.append(bk.do(bk.Y, lambda e, c0=c0: e.dma_start(
            out=chunk_view(x_out, c0, TCB), in_=sb3(tmp4.ap)),
            r=[tmp4, xc4], dma=True))
    bk.end_scope(barrier=False)
    bk.end_scope(barrier=False)
    return out_toks



def emit_wT(bk: BK, name, w_hbm, ident: Buf, nt=8):
    wt = bk.buf(name + "_t", [128, nt * nt * 128], BF16)
    bk.scope()
    wsb = bk.buf(name + "_w", [128, nt * nt * 128], BF16)
    tps = bk.pbuf(name + "_ps", [128, 128], BF16)
    for i in range(nt):
        bk.do(bk.G, lambda e, i=i: e.dma_start(
            out=wsb.ap[:, i * nt * 128:(i + 1) * nt * 128],
            in_=w_hbm[i * 128:(i + 1) * 128, :]),
            w=[wsb] if i == nt - 1 else (), dma=True,
            after=wsb.dep_write() if i == 0 else ())
    for i in range(nt):
        for j in range(nt):
            bk.mm_group([lambda e, i=i, j=j: e.transpose(
                tps.ap[:], wsb.ap[:, i * nt * 128 + j * 128:
                                  i * nt * 128 + (j + 1) * 128],
                ident.ap[:])], r=[wsb, ident], w=[tps])
            bk.do(bk.V, lambda e, i=i, j=j: e.tensor_copy(
                wt.ap[:, j * nt * 128 + i * 128: j * nt * 128 + (i + 1) * 128],
                tps.ap[:]), r=[tps],
                w=[wt] if (i, j) == (nt - 1, nt - 1) else (),
                after=wt.dep_write() if (i, j) == (0, 0) else ())
    bk.end_scope()
    return wt


def load_pcol(bk: BK, dst: Buf, vec_hbm, n=PT):
    for i in range(n):
        bk.do(bk.Y, lambda e, i=i: e.dma_start(
            out=dst.ap[:, i:i + 1],
            in_=vec_hbm[i * 128:(i + 1) * 128].unsqueeze(1)),
            w=[dst] if i == n - 1 else (), dma=True,
            after=dst.dep_write() if i == 0 else ())


def emit_l1_attn(bk: BK, W, consts, x_in, x_out, dbg=None):
    nc = bk.nc
    TCB = 256
    NCB = T // TCB
    TM = T // 4
    bk.scope()
    ln = LNScratch(bk)
    ones_b = bk.buf("ones_a", [128, 128], BF16)
    bk.do(bk.V, lambda e: e.memset(ones_b.ap[:], 1.0), w=[ones_b])
    ident = bk.buf("identa", [128, 128], BF16)
    bk.do(bk.G, lambda e: e.dma_start(out=ident.ap[:], in_=consts['ident'][:]),
          w=[ident], dma=True)
    xn_b = bk.buf("a_xnb", [128, PT * T], BF16)
    xc = bk.buf("a_xc", [128, PT * TCB], F32)
    xcb = bk.buf("a_xcb", [128, PT * TCB], BF16)
    tmp = bk.buf("a_tmp", [128, PT * TCB], F32)
    kv2 = bk.buf("a_kv2", [128, 8 * 64], BF16)
    k_tm = bk.buf("a_ktm", [128, 8 * 1024], BF16)
    v_tm = bk.buf("a_vtm", [128, 8 * 1024], BF16)
    mmps = [bk.pbuf(f"a_mm{i}", [128, TC], F32) for i in range(2)]

    for ch in range(NCB):
        c0 = ch * TCB
        bk.do(bk.Y, lambda e, c0=c0: e.dma_start(
            out=sb3(xc.ap, TCB), in_=chunk_view(x_in, c0, TCB)),
            w=[xc], dma=True)
        for ft in range(PT):
            sl = (slice(None), slice(ft * TCB, (ft + 1) * TCB))
            bk.do(bk.V, lambda e, sl=sl: e.tensor_copy(xcb.ap[sl], xc.ap[sl]),
                  r=[xc], w=[xcb] if ft == PT - 1 else (),
                  after=xcb.dep_write() if ft == 0 else ())
        emit_ln_stats(bk, ln, ones_b, xcb, tc=TCB)
        for ft in range(PT):
            sl = (slice(None), slice(ft * TCB, (ft + 1) * TCB))
            bk.do(bk.V, lambda e, sl=sl: e.tensor_tensor(
                tmp.ap[sl], xc.ap[sl], ln.r.ap[:, 0:TCB], ALU.mult),
                r=[xc, ln.r], w=[tmp] if ft == PT - 1 else (),
                after=tmp.dep_write() if ft == 0 else ())
        for ft in range(PT):
            dsl = (slice(None), slice(ft * T + c0, ft * T + c0 + TCB))
            sl = (slice(None), slice(ft * TCB, (ft + 1) * TCB))
            bk.do(bk.V, lambda e, sl=sl, dsl=dsl: e.tensor_tensor(
                xn_b.ap[dsl], tmp.ap[sl], ln.b2.ap[:, 0:TCB], ALU.add),
                r=[tmp, ln.b2], w=[xn_b] if (ch, ft) == (NCB - 1, PT - 1) else (),
                after=xn_b.dep_write() if (ch, ft) == (0, 0) else ())

    if dbg and 'xnb' in dbg:
        bk.do(bk.Y, lambda e: e.dma_start(out=dbg['xnb'], in_=xn_b.ap[:]),
              r=[xn_b], dma=True)
    bk.scope()
    xr = bk.buf("a_xr", [128, PT * TM], BF16)
    xr1 = bk.buf("a_xr1", [128, TM], F32)
    xr2 = bk.buf("a_xr2", [128, TM], F32)
    bnc = bk.buf("a_bn", [128, PT * 4], F32)
    for k, key in enumerate(('bn_gamma', 'bn_beta', 'bn_mean', 'bn_var')):
        for i in range(PT):
            bk.do(bk.Y, lambda e, i=i, k=k, key=key: e.dma_start(
                out=bnc.ap[:, k * PT + i:k * PT + i + 1],
                in_=W['L1_attn_' + key][i * 128:(i + 1) * 128].unsqueeze(1)),
                w=[bnc] if (k, i) == (3, PT - 1) else (), dma=True,
                after=bnc.dep_write() if (k, i) == (0, 0) else ())
    bns = bk.buf("a_bns", [128, PT], F32)
    bnb = bk.buf("a_bnb", [128, PT], F32)
    bnt = bk.buf("a_bnt", [128, PT], F32)
    epsb6 = bk.const_col(1e-6)
    bk.do(bk.S, lambda e: e.activation(
        bnt.ap[:], bnc.ap[:, 3 * PT:4 * PT], AF.Ln, bias=epsb6.ap[:, 0:1]),
        r=[bnc, epsb6], w=[bnt])
    bk.do(bk.S, lambda e: e.activation(
        bns.ap[:], bnt.ap[:], AF.Exp, scale=-0.5), r=[bnt], w=[bns],
        drain=True)
    bk.do(bk.V, lambda e: e.tensor_tensor(
        bns.ap[:], bns.ap[:], bnc.ap[:, 0:PT], ALU.mult), r=[bnc], w=[bns])
    bk.do(bk.V, lambda e: e.scalar_tensor_tensor(
        bnt.ap[:], bnc.ap[:, 2 * PT:3 * PT], -1.0, bns.ap[:],
        ALU.mult, ALU.mult), r=[bnc, bns], w=[bnt])
    bk.do(bk.V, lambda e: e.tensor_tensor(
        bnb.ap[:], bnt.ap[:], bnc.ap[:, PT:2 * PT], ALU.add),
        r=[bnt, bnc], w=[bnb])
    bk.do(bk.V, lambda e: e.tensor_scalar_mul(bns.ap[:], bns.ap[:], 0.25),
          w=[bns])
    for ft in range(PT):
        v4 = xn_b.ap[:, ft * T:(ft + 1) * T].rearrange("p (m f) -> p m f", f=4)
        bk.do(bk.V, lambda e, v4=v4: e.tensor_tensor(
            xr1.ap[:], v4[:, :, 0], v4[:, :, 1], ALU.add),
            r=[xn_b], w=[xr1])
        bk.do(bk.V, lambda e, v4=v4: e.tensor_tensor(
            xr2.ap[:], v4[:, :, 2], v4[:, :, 3], ALU.add),
            r=[xn_b], w=[xr2])
        bk.do(bk.V, lambda e: e.tensor_tensor(
            xr1.ap[:], xr1.ap[:], xr2.ap[:], ALU.add), r=[xr2], w=[xr1])
        if os.environ.get('MK_XR_RAW'):
            bk.do(bk.V, lambda e, ft=ft: e.tensor_copy(
                xr.ap[:, ft * TM:(ft + 1) * TM], xr1.ap[:]),
                r=[xr1], w=[xr] if ft == PT - 1 else (),
                after=xr.dep_write() if ft == 0 else ())
        else:
            bk.do(bk.V, lambda e, ft=ft: e.tensor_scalar(
                xr.ap[:, ft * TM:(ft + 1) * TM], xr1.ap[:],
                bns.ap[:, ft:ft + 1], bnb.ap[:, ft:ft + 1], ALU.mult, ALU.add),
                r=[xr1, bns, bnb], w=[xr] if ft == PT - 1 else (),
                after=xr.dep_write() if ft == 0 else ())

    if dbg and 'xr' in dbg:
        bk.do(bk.Y, lambda e: e.dma_start(out=dbg['xr'], in_=xr.ap[:]),
              r=[xr], dma=True)
    for name, key, dst in (("a_wk", 'L1_attn_Wk', k_tm),
                           ("a_wv", 'L1_attn_Wv', v_tm)):
        bk.scope()
        wt = emit_wT(bk, name, W[key], ident)
        for mt in range(8):
            for nc2 in range(2):
                ps = mmps[nc2 % 2]
                bk.mm_group([(lambda e, mt=mt, nc2=nc2, kt=kt, wt=wt, ps=ps:
                              e.matmul(ps.ap[:],
                                       xr.ap[:, kt * TM + mt * 128:
                                             kt * TM + (mt + 1) * 128],
                                       wt.ap[:, kt * 1024 + nc2 * 512:
                                             kt * 1024 + (nc2 + 1) * 512],
                                       start=(kt == 0), stop=(kt == PT - 1)))
                             for kt in range(PT)], r=[wt, xr], w=[ps])
                bk.do(bk.V, lambda e, mt=mt, nc2=nc2, dst=dst, ps=ps:
                      e.tensor_copy(dst.ap[:, mt * 1024 + nc2 * 512:
                                           mt * 1024 + (nc2 + 1) * 512],
                                    ps.ap[:]),
                      r=[ps], w=[dst] if (mt, nc2) == (7, 1) else (),
                      after=dst.dep_write() if (mt, nc2) == (0, 0) else ())
        bk.end_scope()

    if dbg and 'ktm' in dbg:
        bk.do(bk.Y, lambda e: e.dma_start(out=dbg['ktm'], in_=k_tm.ap[:]),
              r=[k_tm], dma=True)
    if dbg and 'vtm' in dbg:
        bk.do(bk.Y, lambda e: e.dma_start(out=dbg['vtm'], in_=v_tm.ap[:]),
              r=[v_tm], dma=True)
    kvp = bk.pbuf("a_kvp", [128, 64], F32)
    bk.do(bk.V, lambda e: e.memset(kv2.ap[:], 0.0), w=[kv2])
    SC = 32 ** -0.5
    for h in range(32):
        bk.mm_group([(lambda e, h=h, mt=mt: e.matmul(
            kvp.ap[0:32, 0:32],
            k_tm.ap[:, mt * 1024 + h * 32:mt * 1024 + (h + 1) * 32],
            v_tm.ap[:, mt * 1024 + h * 32:mt * 1024 + (h + 1) * 32],
            start=(mt == 0), stop=(mt == 7))) for mt in range(8)],
            r=[k_tm, v_tm], w=[kvp])
        hp, hh = divmod(h, 2)
        pb, cb = 64 * (hp % 2) + 32 * hh, 64 * (hp // 2) + 32 * hh
        bk.do(bk.V, lambda e, pb=pb, cb=cb: e.tensor_scalar_mul(
            kv2.ap[pb:pb + 32, cb:cb + 32], kvp.ap[0:32, 0:32], SC),
            r=[kvp], w=[kv2] if h == 31 else (),
            after=kv2.dep_write() if h == 0 else ())

    if dbg and 'kv2' in dbg:
        bk.do(bk.Y, lambda e: e.dma_start(out=dbg['kv2'], in_=kv2.ap[:]),
              r=[kv2], dma=True)
    bk.end_scope()

    bk.scope()
    wqT = emit_wT(bk, "a_wq", W['L1_attn_Wq'], ident)
    wpT = emit_wT(bk, "a_wp", W['L1_attn_Wp'], ident)
    bpc = bk.buf("a_bp", [128, PT], F32)
    load_pcol(bk, bpc, W['L1_attn_bp'])
    q_ch = bk.buf("a_qch", [128, PT * TCB], BF16)
    ao_ch = bk.buf("a_aoch", [128, PT * TCB], BF16)
    bo_f = bk.buf("a_bof", [128, PT * TCB], F32)
    bo_b = bk.buf("a_bob", [128, PT * TCB], BF16)
    out_toks = []
    for ch in range(NCB):
        c0 = ch * TCB
        for co in range(PT):
            ps = mmps[co % 2]
            bk.mm_group([(lambda e, co=co, kt=kt, ps=ps, c0=c0: e.matmul(
                ps.ap[:, 0:TCB], wqT.ap[:, kt * 1024 + co * 128:
                                        kt * 1024 + (co + 1) * 128],
                xn_b.ap[:, kt * T + c0: kt * T + c0 + TCB],
                start=(kt == 0), stop=(kt == PT - 1))) for kt in range(PT)],
                r=[wqT, xn_b], w=[ps])
            bk.do(bk.V, lambda e, co=co, ps=ps: e.tensor_copy(
                q_ch.ap[:, co * TCB:(co + 1) * TCB], ps.ap[:, 0:TCB]),
                r=[ps], w=[q_ch] if co == PT - 1 else (),
                after=q_ch.dep_write() if co == 0 else ())
        if dbg and 'qch' in dbg and ch == 0:
            bk.do(bk.Y, lambda e: e.dma_start(out=dbg['qch'], in_=q_ch.ap[:]),
                  r=[q_ch], dma=True)
        for pt in range(PT):
            ps = mmps[pt % 2]
            fns = []
            for half in (0, 1):
                hp = 2 * pt + half
                pb, cb = 64 * (hp % 2), 64 * (hp // 2)
                fns.append(lambda e, pt=pt, half=half, pb=pb, cb=cb, ps=ps:
                           e.matmul(ps.ap[64 * half:64 * half + 64, 0:TCB],
                                    kv2.ap[pb:pb + 64, cb:cb + 64],
                                    q_ch.ap[64 * half:64 * half + 64,
                                            pt * TCB:(pt + 1) * TCB],
                                    start=True, stop=True))
            bk.mm_group(fns, r=[kv2, q_ch], w=[ps])
            bk.do(bk.V, lambda e, pt=pt, ps=ps: e.tensor_copy(
                ao_ch.ap[:, pt * TCB:(pt + 1) * TCB], ps.ap[:, 0:TCB]),
                r=[ps], w=[ao_ch] if pt == PT - 1 else (),
                after=ao_ch.dep_write() if pt == 0 else ())
        if dbg and 'aoch' in dbg and ch == 0:
            bk.do(bk.Y, lambda e: e.dma_start(out=dbg['aoch'], in_=ao_ch.ap[:]),
                  r=[ao_ch], dma=True)
        for co in range(PT):
            ps = mmps[co % 2]
            bk.mm_group([(lambda e, co=co, kt=kt, ps=ps: e.matmul(
                ps.ap[:, 0:TCB], wpT.ap[:, kt * 1024 + co * 128:
                                        kt * 1024 + (co + 1) * 128],
                ao_ch.ap[:, kt * TCB:(kt + 1) * TCB],
                start=(kt == 0), stop=(kt == PT - 1))) for kt in range(PT)],
                r=[wpT, ao_ch], w=[ps])
            bk.do(bk.V, lambda e, co=co, ps=ps: e.tensor_scalar_add(
                bo_f.ap[:, co * TCB:(co + 1) * TCB], ps.ap[:, 0:TCB],
                bpc.ap[:, co:co + 1]),
                r=[ps, bpc], w=[bo_f] if co == PT - 1 else (),
                after=bo_f.dep_write() if co == 0 else ())
        for ft in range(PT):
            sl = (slice(None), slice(ft * TCB, (ft + 1) * TCB))
            bk.do(bk.V, lambda e, sl=sl: e.tensor_copy(bo_b.ap[sl],
                                                       bo_f.ap[sl]),
                  r=[bo_f], w=[bo_b] if ft == PT - 1 else (),
                  after=bo_b.dep_write() if ft == 0 else ())
        bk.do(bk.Y, lambda e, c0=c0: e.dma_start(
            out=sb3(xc.ap, TCB), in_=chunk_view(x_in, c0, TCB)),
            w=[xc], dma=True)
        emit_ln_stats(bk, ln, ones_b, bo_b, tc=TCB)
        emit_ln_apply(bk, ln, bo_f, tmp, tmp, tc=TCB, res=xc)
        out_toks.append(bk.do(bk.Y, lambda e, c0=c0: e.dma_start(
            out=chunk_view(x_out, c0, TCB), in_=sb3(tmp.ap, TCB)),
            r=[tmp, xc], dma=True))
    bk.end_scope(barrier=False)
    bk.end_scope(barrier=False)
    return out_toks



def hbm_inputs(nc, names_shapes):
    return {name: nc.dram_tensor(name, list(shape), F32,
                                 kind="ExternalInput").ap()
            for name, shape in names_shapes}


IN_SHAPES = {
    'L1_ffn_G1_R': (1, 32, 4, 32), 'L1_ffn_G2_R': (4, 16, 1, 16),
    'L1_ffn_G1_I': (1, 32, 4, 32), 'L1_ffn_G2_I': (4, 16, 1, 16),
    'L1_ffn_bias_real': (512,), 'L1_ffn_bias_imag': (512,),
}


L0A_SHAPES = {
    'L0_attn_cpe1_w': (1024, 1, 3), 'L0_attn_cpe1_b': (1024,),
    'L0_attn_norm1_g': (1024,), 'L0_attn_norm1_b': (1024,),
    'L0_attn_in_w': (1024, 1024), 'L0_attn_in_b': (1024,),
    'L0_attn_actp_w': (1024, 1024), 'L0_attn_actp_b': (1024,),
    'L0_attn_dwc_w': (1024, 1, 3), 'L0_attn_dwc_b': (1024,),
    'L0_attn_qk_w': (1024, 2048), 'L0_attn_qk_b': (2048,),
    'L0_attn_lepe_w': (1024, 1, 3), 'L0_attn_lepe_b': (1024,),
    'L0_attn_out_w': (1024, 1024), 'L0_attn_out_b': (1024,),
    'L0_attn_cpe2_w': (1024, 1, 3), 'L0_attn_cpe2_b': (1024,),
    'L0_attn_norm2_g': (1024,), 'L0_attn_norm2_b': (1024,),
    'L0_attn_fc1_w': (1024, 4096), 'L0_attn_fc1_b': (4096,),
    'L0_attn_fc2_w': (4096, 1024), 'L0_attn_fc2_b': (1024,),
}


L0F_SHAPES = {
    'L0_ffn_Wq': (1024, 1024), 'L0_ffn_bq': (1024,),
    'L0_ffn_Wk': (1024, 1024), 'L0_ffn_bk': (1024,),
    'L0_ffn_Wv': (1024, 1024), 'L0_ffn_bv': (1024,),
    'L0_ffn_Wo': (1024, 1024), 'L0_ffn_bo': (1024,),
    'L0_ffn_Wg': (1024, 8), 'L0_ffn_bg': (8,),
    'L0_ffn_W1': (8, 1024, 4096), 'L0_ffn_b1': (8, 4096),
    'L0_ffn_W2': (8, 4096, 1024), 'L0_ffn_b2': (8, 1024),
}


def build_l0f_only():
    bk = BK()
    nc = bk.nc
    x_in = nc.dram_tensor("x", [C, T], F32, kind="ExternalInput").ap()
    x_out = nc.dram_tensor("out", [C, T], F32, kind="ExternalOutput").ap()
    consts = {'e8': nc.dram_tensor("e8", [8, 1024], F32,
                                   kind="ExternalInput").ap()}
    W = hbm_inputs(nc, list(L0F_SHAPES.items()))
    toks = emit_l0_ffn(bk, W, consts, x_in, x_out)
    for t in toks:
        bk.Y.wait(t)
    bk.barrier()
    return bk.finalize()


def build_l0a_only():
    bk = BK()
    nc = bk.nc
    x_in = nc.dram_tensor("x", [C, T], F32, kind="ExternalInput").ap()
    x_out = nc.dram_tensor("out", [C, T], F32, kind="ExternalOutput").ap()
    consts = {'ident': nc.dram_tensor("ident", [128, 128], F32,
                                      kind="ExternalInput").ap(),
              'ind64': nc.dram_tensor("ind64", [128, 128], F32,
                                      kind="ExternalInput").ap()}
    W = hbm_inputs(nc, list(L0A_SHAPES.items()))
    import os as _os
    dbg = None
    if _os.environ.get('MK_L0A_DBG'):
        sel = _os.environ['MK_L0A_DBG'].split(',')
        if sel == ['1']:
            sel = ['xn', 'y', 'xi0', 'act', 'q', 'y2']
        dbg = {}
        for nm in sel:
            if nm == 'qq':
                dbg[nm] = nc.dram_tensor("dbg_qq", [C, T + 128], BF16,
                                         kind="ExternalOutput").ap()
            elif nm in ('afd', 'af0d', 'ltd'):
                dbg[nm] = nc.dram_tensor("dbg_" + nm, [128, 64 * 512 + 64],
                                         F32, kind="ExternalOutput").ap()
            elif nm == 'zrd':
                dbg[nm] = nc.dram_tensor("dbg_zrd", [128, 8 * T + 64], F32,
                                         kind="ExternalOutput").ap()
            elif nm == 'gat':
                dbg[nm] = nc.dram_tensor("dbg_gat", [128, 8 * T + 64], BF16,
                                         kind="ExternalOutput").ap()
            else:
                dbg[nm] = nc.dram_tensor("dbg_" + nm, [C, T],
                                         BF16 if nm in ('act',) else F32,
                                         kind="ExternalOutput").ap()
    toks = emit_l0_attn(bk, W, consts, x_in, x_out, dbg=dbg)
    for t in toks:
        bk.Y.wait(t)
    bk.barrier()
    return bk.finalize()


def build_l1a_only():
    bk = BK()
    nc = bk.nc
    x_in = nc.dram_tensor("x", [C, T], F32, kind="ExternalInput").ap()
    x_out = nc.dram_tensor("out", [C, T], F32, kind="ExternalOutput").ap()
    consts = {'ident': nc.dram_tensor("ident", [128, 128], F32,
                                      kind="ExternalInput").ap()}
    names = ['L1_attn_Wq', 'L1_attn_Wk', 'L1_attn_Wv', 'L1_attn_Wp',
             'L1_attn_bp', 'L1_attn_bn_gamma', 'L1_attn_bn_beta',
             'L1_attn_bn_mean', 'L1_attn_bn_var']
    shapes = {n: ((1024, 1024) if n.endswith(('Wq', 'Wk', 'Wv', 'Wp'))
                  else (1024,)) for n in names}
    W = hbm_inputs(nc, [(n, shapes[n]) for n in names])
    toks = emit_l1_attn(bk, W, consts, x_in, x_out)
    for t in toks:
        bk.Y.wait(t)
    bk.barrier()
    return bk.finalize()


def build_l1f_only():
    bk = BK()
    nc = bk.nc
    x_in = nc.dram_tensor("x", [C, T], F32, kind="ExternalInput").ap()
    x_out = nc.dram_tensor("out", [C, T], F32, kind="ExternalOutput").ap()
    consts = {'ident': nc.dram_tensor("ident", [128, 128], F32,
                                      kind="ExternalInput").ap()}
    W = hbm_inputs(nc, [(k, v) for k, v in IN_SHAPES.items()
                        if k.startswith('L1_ffn')])
    toks = emit_l1_ffn(bk, W, consts, x_in, x_out)
    for t in toks:
        bk.Y.wait(t)
    bk.barrier()
    return bk.finalize()


L1A_NAMES = ['L1_attn_Wq', 'L1_attn_Wk', 'L1_attn_Wv', 'L1_attn_Wp',
             'L1_attn_bp', 'L1_attn_bn_gamma', 'L1_attn_bn_beta',
             'L1_attn_bn_mean', 'L1_attn_bn_var']
L1F_NAMES = list(IN_SHAPES)


def host_consts():
    ident = np.eye(128, dtype=np.float32)
    ind64 = np.kron(np.eye(2, dtype=np.float32),
                    np.ones((64, 64), np.float32))
    e8 = np.zeros((8, 1024), np.float32)
    for e in range(8):
        e8[e, e * 128:(e + 1) * 128] = 1.0
    return ident, ind64, e8


L1A_SHAPES = {n: ((1024, 1024) if n.endswith(('Wq', 'Wk', 'Wv', 'Wp'))
                  else (1024,)) for n in L1A_NAMES}


def build_fused():
    bk = BK()
    nc = bk.nc
    x_in = nc.dram_tensor("x", [C, T], F32, kind="ExternalInput").ap()
    x_out = nc.dram_tensor("out", [C, T], F32, kind="ExternalOutput").ap()
    ident = nc.dram_tensor("ident", [128, 128], F32,
                           kind="ExternalInput").ap()
    ind64 = nc.dram_tensor("ind64", [128, 128], F32,
                           kind="ExternalInput").ap()
    e8 = nc.dram_tensor("e8", [8, 1024], F32, kind="ExternalInput").ap()
    W = hbm_inputs(nc, list(L0A_SHAPES.items()) + list(L0F_SHAPES.items())
                   + list(L1A_SHAPES.items()) + list(IN_SHAPES.items()))
    x1 = nc.dram_tensor("mk_x1", [C, T], F32, kind="Internal").ap()
    x2 = nc.dram_tensor("mk_x2", [C, T], F32, kind="Internal").ap()
    x3 = x1
    for emit, xi, xo, cn in (
            (emit_l0_attn, x_in, x1, {'ident': ident, 'ind64': ind64}),
            (emit_l0_ffn, x1, x2, {'e8': e8}),
            (emit_l1_attn, x2, x3, {'ident': ident}),
            (emit_l1_ffn, x3, x_out, {'ident': ident})):
        toks = emit(bk, W, cn, xi, xo)
        for t in toks:
            bk.Y.wait(t)
        bk.barrier()
    return bk.finalize()


ALL_NAMES = (list(L0A_SHAPES) + list(L0F_SHAPES) + list(L1A_SHAPES)
             + list(IN_SHAPES))


def kernel(**inputs):
    ident, ind64, e8 = host_consts()
    x = np.asarray(inputs['L0_attn_x'])
    cur = [np.ascontiguousarray(x[c].T) for c in range(N_CORES)]
    if os.environ.get('MK_UNFUSED'):
        stages = [
            (build_l0a_only, list(L0A_SHAPES),
             {'ident': ident, 'ind64': ind64}),
            (build_l0f_only, list(L0F_SHAPES), {'e8': e8}),
            (build_l1a_only, L1A_NAMES, {'ident': ident}),
            (build_l1f_only, L1F_NAMES, {'ident': ident}),
        ]
        for builder, names, consts in stages:
            nc = builder()
            wmap = {n: np.ascontiguousarray(np.asarray(inputs[n]))
                    for n in names}
            in_maps = [{'x': cur[c], **consts, **wmap}
                       for c in range(N_CORES)]
            res = run_bass_kernel_spmd(nc, in_maps,
                                       core_ids=list(range(N_CORES)))
            cur = [np.ascontiguousarray(res.results[c]['out'])
                   for c in range(N_CORES)]
        return np.stack([cur[c].T for c in range(N_CORES)], 0)
    nc = build_fused()
    wmap = {n: np.ascontiguousarray(np.asarray(inputs[n]))
            for n in ALL_NAMES}
    consts = {'ident': ident, 'ind64': ind64, 'e8': e8}
    in_maps = [{'x': cur[c], **consts, **wmap} for c in range(N_CORES)]
    res = run_bass_kernel_spmd(nc, in_maps, core_ids=list(range(N_CORES)))
    return np.stack([np.ascontiguousarray(res.results[c]['out']).T
                     for c in range(N_CORES)], 0)


# version 04
# speedup vs baseline: 1.1739x; 1.0157x over previous
import numpy as np
import os
from contextlib import ExitStack

import concourse.bass as bass
import concourse.mybir as mybir
from concourse.bass_utils import run_bass_kernel_spmd

F32 = mybir.dt.float32
BF16 = mybir.dt.bfloat16
AF = mybir.ActivationFunctionType
ALU = mybir.AluOpType
AX = mybir.AxisListType

C = 1024
T = 4096
PT = 8
TC = 512
NCH = T // TC
N_CORES = 8
LN_EPS = 1e-5
DEDUP_WAITS = True



class SemTrack:

    def __init__(self, nc, name):
        self.sem = nc.alloc_semaphore(name)
        self.name = name
        self.n = 0


N_DMA_SLOTS = int(os.environ.get('MK_DMA_SLOTS', 10))


class Stream:

    def __init__(self, bk, name):
        self.bk = bk
        self.name = name
        self.sem = bk.nc.alloc_semaphore(f"sem_{name}")
        self.n = 0
        self.code = []
        self.wmax = {}
        self.dma_pool = [SemTrack(bk.nc, f"dma_{name}{i}")
                         for i in range(N_DMA_SLOTS)]
        self.dma_i = 0
        self.mark_slots = ([SemTrack(bk.nc, f"mk_{name}{i}") for i in range(8)]
                           if name == 't' else None)
        self.mark_i = 0

    def force_wait(self, tok):
        st, v = tok
        self.code.append(('w', st.sem, v))
        self.wmax[st.name] = max(self.wmax.get(st.name, 0), v)

    def wait(self, tok):
        if tok is None:
            return
        st, v = tok
        if st is self and v > self.n:
            raise RuntimeError(f"wait on future own token {v} > {self.n}")
        if self.mark_slots is not None and st in self.mark_slots:
            if v > st.n:
                raise RuntimeError("wait on future slot token")
        if self.wmax.get(st.name, 0) >= v:
            if not DEDUP_WAITS:
                self.code.append(('w', st.sem, v))
            return
        self.wmax[st.name] = v
        self.code.append(('w', st.sem, v))

    def op(self, fn, after=(), mark=False, dma=False, drain=False):
        for t in after:
            self.wait(t)
        if drain and mark:
            self.code.append(('i', fn, None, 0))
            fn = lambda e: e.drain()
        elif self.mark_slots is not None and mark:
            self.code.append(('i', fn, None, 0))
            fn = lambda e: e.drain()
        if dma:
            tr = self.dma_pool[self.dma_i % N_DMA_SLOTS]
            self.dma_i += 1
            tr.n += 16
            self.code.append(('i', fn, tr.sem, 16))
            return (tr, tr.n)
        if mark:
            if self.mark_slots is not None:
                tr = self.mark_slots[self.mark_i % len(self.mark_slots)]
                self.mark_i += 1
                if tr.n:
                    self.wait((tr, tr.n))
                tr.n += 1
                self.code.append(('i', fn, tr.sem, 1))
                return (tr, tr.n)
            self.n += 1
            self.code.append(('i', fn, self.sem, 1))
            return (self, self.n)
        self.code.append(('i', fn, None, 0))
        return None


class Buf:

    __slots__ = ('ap', 'w', 'rs', 'name')

    def __init__(self, ap=None, name=""):
        self.ap = ap
        self.w = None
        self.rs = []
        self.name = name

    def dep_read(self):
        return [self.w]

    def dep_write(self):
        return self.rs + [self.w]


class BK:

    def __init__(self):
        self.nc = bass.Bass(
            "TRN2", target_bir_lowering=False,
            detect_race_conditions=bool(os.environ.get("MK_DETECT_RACES")))
        self.T = Stream(self, 't')
        self.V = Stream(self, 'v')
        self.S = Stream(self, 's')
        self.G = Stream(self, 'g')
        self.Y = Stream(self, 'y')
        self.streams = [self.T, self.V, self.S, self.G, self.Y]
        self.stack = None
        self.gstack = ExitStack()
        self._consts = {}
        self._cn = 0
        self._uniq = 0

    def _un(self, name):
        self._uniq += 1
        return f"{name}_u{self._uniq}"

    def scope(self):
        if self.stack is None:
            self.stack = ExitStack()
            self._scopes = [self.stack]
        else:
            self._scopes.append(ExitStack())
            self.stack = self._scopes[-1]

    def end_scope(self, barrier=True):
        if barrier:
            self.barrier()
        self._scopes.pop().close()
        self.stack = self._scopes[-1] if self._scopes else None

    def sbuf(self, name, shape, dtype, glob=False):
        name = self._un(name)
        if os.environ.get('MK_NOSCOPE'):
            return self.nc.alloc_sbuf_tensor(
                name, shape, dtype, side="right" if glob else "left").ap()
        stk = self.gstack if glob else self.stack
        h = stk.enter_context(self.nc.sbuf_tensor(
            name, shape, dtype, side="right" if glob else "left"))
        return h.ap()

    def const_col(self, value):
        if value not in self._consts:
            self._cn += 1
            b = Buf(self.sbuf(f"cc{self._cn}", [128, 1], F32, glob=True),
                    f"cc{self._cn}")
            self.do(self.V, lambda e, b=b, value=value: e.memset(b.ap[:],
                                                                 value), w=[b])
            self._consts[value] = b
        return self._consts[value]

    def psum(self, name, shape, dtype=F32):
        h = self.stack.enter_context(self.nc.psum_tensor(self._un(name),
                                                         shape, dtype))
        return h.ap()

    def buf(self, name, shape, dtype):
        return Buf(self.sbuf(name, shape, dtype), name)

    def pbuf(self, name, shape, dtype=F32):
        return Buf(self.psum(name, shape, dtype), name)

    def do(self, st: Stream, fn, r=(), w=(), after=(), mark=True, dma=False,
           drain=None):
        if drain is None:
            drain = st is self.S or st is self.V
        deps = list(after)
        for b in r:
            deps += b.dep_read()
        for b in w:
            deps += b.dep_write()
        tok = st.op(fn, after=deps, mark=mark, dma=dma, drain=drain)
        if tok is not None:
            for b in r:
                b.rs.append(tok)
            for b in w:
                b.w = tok
                b.rs = []
        return tok

    def mm_group(self, fns, r=(), w=(), after=()):
        deps = list(after)
        for b in r:
            deps += b.dep_read()
        for b in w:
            deps += b.dep_write()
        for i, fn in enumerate(fns):
            last = (i == len(fns) - 1)
            tok = self.T.op(fn, after=deps if i == 0 else (), mark=last)
        for b in r:
            b.rs.append(tok)
        for b in w:
            b.w = tok
            b.rs = []
        return tok

    def barrier(self):
        for st in self.streams:
            for tr in st.dma_pool:
                if tr.n:
                    st.wait((tr, tr.n))
        toks = [st.op(lambda e: e.nop(), mark=True) for st in self.streams]
        for st in self.streams:
            for t in toks:
                st.wait(t)
        return toks

    def finalize(self):
        with self.nc.Block() as block:
            regs = ((self.T, block.tensor), (self.V, block.vector),
                    (self.S, block.scalar), (self.G, block.gpsimd),
                    (self.Y, block.sync))
            for st, reg in regs:
                def make(code):
                    def emit(eng):
                        for item in code:
                            if item[0] == 'w':
                                eng.wait_ge(item[1], item[2])
                            else:
                                inst = item[1](eng)
                                if item[2] is not None:
                                    inst.then_inc(item[2], item[3])
                    return emit
                reg(make(st.code))
        return self.nc


def chunk_view(hbm_ap, c0, tc):
    return hbm_ap.rearrange("(ft p) t -> p ft t", p=128)[:, :, c0:c0 + tc]


def sb3(ap, tc=TC):
    return ap.rearrange("p (ft t) -> p ft t", t=tc)



class LNScratch:

    def __init__(self, bk: BK, tag=""):
        self.sq = bk.buf(f"ln_sq{tag}", [128, PT * TC], BF16)
        self.mu = bk.buf(f"ln_mu{tag}", [128, TC], F32)
        self.msq = bk.buf(f"ln_msq{tag}", [128, TC], F32)
        self.var = bk.buf(f"ln_var{tag}", [128, TC], F32)
        self.r = bk.buf(f"ln_r{tag}", [128, TC], F32)
        self.b2 = bk.buf(f"ln_b2{tag}", [128, TC], F32)
        self.psA = bk.pbuf(f"ln_psA{tag}", [128, TC], F32)
        self.psB = bk.pbuf(f"ln_psB{tag}", [128, TC], F32)


def emit_ln_stats(bk: BK, ln: LNScratch, ones_b: Buf, src: Buf, tc=TC, nf=PT,
                  src_tiles=None):
    inv_n = 1.0 / (128 * nf)
    tiles = src_tiles or [src.ap[:, ft * tc:(ft + 1) * tc] for ft in range(nf)]
    for ft in range(nf):
        bk.do(bk.S, lambda e, ft=ft, t=tiles[ft]: e.activation(
            ln.sq.ap[:, ft * tc: (ft + 1) * tc], t, AF.Square),
            r=[src], w=[ln.sq] if ft == nf - 1 else (),
            after=ln.sq.dep_write() if ft == 0 else ())
    bk.mm_group([(lambda e, ft=ft, t=tiles[ft]: e.matmul(
        ln.psA.ap[:, 0:tc], ones_b.ap, t,
        start=(ft == 0), stop=(ft == nf - 1))) for ft in range(nf)],
        r=[src, ones_b], w=[ln.psA])
    bk.mm_group([(lambda e, ft=ft: e.matmul(
        ln.psB.ap[:, 0:tc], ones_b.ap,
        ln.sq.ap[:, ft * tc:(ft + 1) * tc],
        start=(ft == 0), stop=(ft == nf - 1))) for ft in range(nf)],
        r=[ln.sq, ones_b], w=[ln.psB])
    bk.do(bk.V, lambda e: e.tensor_scalar_mul(
        ln.mu.ap[:, 0:tc], ln.psA.ap[:, 0:tc], inv_n), r=[ln.psA], w=[ln.mu])
    bk.do(bk.V, lambda e: e.tensor_scalar_mul(
        ln.msq.ap[:, 0:tc], ln.psB.ap[:, 0:tc], inv_n), r=[ln.psB], w=[ln.msq])
    bk.do(bk.V, lambda e: e.tensor_tensor(
        ln.var.ap[:, 0:tc], ln.mu.ap[:, 0:tc], ln.mu.ap[:, 0:tc], ALU.mult),
        r=[ln.mu], w=[ln.var])
    bk.do(bk.V, lambda e: e.tensor_tensor(
        ln.var.ap[:, 0:tc], ln.msq.ap[:, 0:tc], ln.var.ap[:, 0:tc],
        ALU.subtract), r=[ln.msq, ln.var], w=[ln.var])
    epsb = bk.const_col(LN_EPS)
    bk.do(bk.S, lambda e: e.activation(
        ln.var.ap[:, 0:tc], ln.var.ap[:, 0:tc], AF.Ln, bias=epsb.ap[:, 0:1]),
        r=[epsb], w=[ln.var])
    bk.do(bk.S, lambda e: e.activation(
        ln.r.ap[:, 0:tc], ln.var.ap[:, 0:tc], AF.Exp, scale=-0.5),
        r=[ln.var], w=[ln.r], drain=True)
    bk.do(bk.V, lambda e: e.scalar_tensor_tensor(
        ln.b2.ap[:, 0:tc], ln.mu.ap[:, 0:tc], -1.0, ln.r.ap[:, 0:tc],
        ALU.mult, ALU.mult), r=[ln.mu, ln.r], w=[ln.b2])


def emit_ln_apply(bk: BK, ln: LNScratch, src: Buf, tmp: Buf, dst: Buf,
                  tc=TC, nf=PT, res: Buf = None, streams=None):
    sa, sbt, scs = streams or (bk.V, bk.V, bk.V)
    for ft in range(nf):
        sl = (slice(None), slice(ft * tc, (ft + 1) * tc))
        bk.do(sa, lambda e, sl=sl: e.tensor_tensor(
            tmp.ap[sl], src.ap[sl], ln.r.ap[:, 0:tc], ALU.mult),
            r=[src, ln.r], w=[tmp] if ft == nf - 1 else (),
            after=tmp.dep_write() if ft == 0 else ())
    if res is None:
        for ft in range(nf):
            sl = (slice(None), slice(ft * tc, (ft + 1) * tc))
            bk.do(sbt, lambda e, sl=sl: e.tensor_tensor(
                dst.ap[sl], tmp.ap[sl], ln.b2.ap[:, 0:tc], ALU.add),
                r=[tmp, ln.b2], w=[dst] if ft == nf - 1 else (),
                after=dst.dep_write() if ft == 0 else ())
    else:
        for ft in range(nf):
            sl = (slice(None), slice(ft * tc, (ft + 1) * tc))
            bk.do(sbt, lambda e, sl=sl: e.tensor_tensor(
                tmp.ap[sl], tmp.ap[sl], ln.b2.ap[:, 0:tc], ALU.add),
                r=[ln.b2, tmp], w=[tmp] if ft == nf - 1 else ())
        for ft in range(nf):
            sl = (slice(None), slice(ft * tc, (ft + 1) * tc))
            bk.do(scs, lambda e, sl=sl: e.tensor_tensor(
                dst.ap[sl], tmp.ap[sl], res.ap[sl], ALU.add),
                r=[tmp, res], w=[dst] if ft == nf - 1 else (),
                after=dst.dep_write() if ft == 0 else ())



def emit_l1_ffn(bk: BK, W, consts, x_in, x_out):
    nc = bk.nc
    bk.scope()
    ones_b = bk.buf("ones_b", [128, 128], BF16)
    bk.do(bk.V, lambda e: e.memset(ones_b.ap[:], 1.0), w=[ones_b])
    ident = bk.buf("identb", [128, 128], BF16)
    bk.do(bk.G, lambda e: e.dma_start(out=ident.ap[:], in_=consts['ident'][:]),
          w=[ident], dma=True)

    g1r = bk.buf("g1r", [4, 1024], BF16)
    g2r = bk.buf("g2r", [4, 256], BF16)
    g1i = bk.buf("g1i", [4, 1024], BF16)
    g2i = bk.buf("g2i", [4, 256], BF16)
    for b, key, rr in ((g1r, 'L1_ffn_G1_R', 1), (g2r, 'L1_ffn_G2_R', 2),
                       (g1i, 'L1_ffn_G1_I', 1), (g2i, 'L1_ffn_G2_I', 2)):
        if rr == 1:
            srcv = W[key][0].rearrange("b c d -> c b d")
            dstv = b.ap.rearrange("c (b d) -> c b d", d=32)
        else:
            srcv = W[key][:, :, 0, :]
            dstv = b.ap.rearrange("c (e g) -> c e g", g=16)
        bk.do(bk.G, lambda e, srcv=srcv, dstv=dstv: e.dma_start(
            out=dstv, in_=srcv), w=[b], dma=True)

    vhbm = {k: Buf(nc.dram_tensor(f"l1f_v_{k}", [1024, 256], BF16,
                                  kind="Internal").ap(), f"vhbm{k}")
            for k in 'ri'}
    vps = bk.pbuf("l1f_vps", [128, 256], F32)
    vsb = bk.buf("l1f_vsb", [128, 256], BF16)
    w_sb = {'r': bk.buf("l1f_wr", [128, 4 * 512], BF16),
            'i': bk.buf("l1f_wi", [128, 4 * 512], BF16)}
    wt = {'r': bk.buf("l1f_wtr", [128, 4 * 512], BF16),
          'i': bk.buf("l1f_wti", [128, 4 * 512], BF16),
          'in': bk.buf("l1f_wtin", [128, 4 * 512], BF16)}
    tps = bk.pbuf("l1f_tps", [128, 128], BF16)

    for key, g1, g2 in (('r', g1r, g2r), ('i', g1i, g2i)):
        for mc in range(8):
            bk.mm_group([lambda e, g1=g1, mc=mc, g2=g2: e.matmul(
                vps.ap[:], g1.ap[:, mc * 128:(mc + 1) * 128], g2.ap[:],
                start=True, stop=True)], r=[g1, g2], w=[vps])
            bk.do(bk.V, lambda e: e.tensor_copy(vsb.ap[:], vps.ap[:]),
                  r=[vps], w=[vsb])
            bk.do(bk.Y, lambda e, key=key, mc=mc: e.dma_start(
                out=vhbm[key].ap[mc * 128:(mc + 1) * 128, :], in_=vsb.ap[:]),
                r=[vsb], w=[vhbm[key]] if mc == 7 else (), dma=True,
                after=vhbm[key].dep_write() if mc == 0 else ())
    for key in 'ri':
        for p4 in range(4):
            for bl in range(8):
                b = p4 * 8 + bl
                src = vhbm[key].ap[b * 32:(b + 1) * 32, :].rearrange(
                    "d (e g) -> e d g", g=16)
                bk.do(bk.Y, lambda e, key=key, p4=p4, bl=bl, src=src:
                      e.dma_start(out=w_sb[key].ap[
                          16 * bl:16 * (bl + 1),
                          p4 * 512:(p4 + 1) * 512].rearrange(
                          "p (d g) -> p d g", g=16),
                          in_=src),
                      r=[vhbm[key]], w=[w_sb[key]] if (p4, bl) == (3, 7) else (),
                      dma=True,
                      after=w_sb[key].dep_write() if (p4, bl) == (0, 0) else ())
    for key in 'ri':
        for i in range(4):
            for j in range(4):
                bk.mm_group([lambda e, key=key, i=i, j=j: e.transpose(
                    tps.ap[:], w_sb[key].ap[:, i * 512 + j * 128:
                                            i * 512 + (j + 1) * 128],
                    ident.ap[:])], r=[w_sb[key], ident], w=[tps])
                bk.do(bk.V, lambda e, key=key, i=i, j=j: e.tensor_copy(
                    wt[key].ap[:, j * 512 + i * 128: j * 512 + (i + 1) * 128],
                    tps.ap[:]), r=[tps],
                    w=[wt[key]] if (i, j) == (3, 3) else (),
                    after=wt[key].dep_write() if (i, j) == (0, 0) else ())
    bk.do(bk.V, lambda e: e.tensor_scalar_mul(wt['in'].ap[:], wt['i'].ap[:],
                                              -1.0), r=[wt['i']], w=[wt['in']])

    biasc = bk.buf("l1f_bias", [128, 8], F32)
    for rt in range(4):
        for cc, key in ((rt, 'L1_ffn_bias_real'), (4 + rt, 'L1_ffn_bias_imag')):
            bk.do(bk.Y, lambda e, cc=cc, key=key, rt=rt: e.dma_start(
                out=biasc.ap[:, cc:cc + 1],
                in_=W[key][rt * 128:(rt + 1) * 128].unsqueeze(1)),
                w=[biasc] if cc == 7 else (), dma=True,
                after=biasc.dep_write() if cc == 0 else ())

    xc = [bk.buf(f"l1f_xc{i}", [128, PT * TC], F32) for i in range(2)]
    xnb = [bk.buf(f"l1f_xnb{i}", [128, PT * TC], BF16) for i in range(2)]
    lns = [LNScratch(bk, f"l1f{i}") for i in range(2)]
    xcb = bk.buf("l1f_xcb", [128, PT * TC], BF16)
    bo_f = bk.buf("l1f_bof", [128, PT * TC], F32)
    bo_b = bk.buf("l1f_bob", [128, PT * TC], BF16)
    tmp = bk.buf("l1f_tmp", [128, PT * TC], F32)
    mm_ps = [bk.pbuf(f"l1f_mm{i}", [128, TC], F32) for i in range(2)]

    def load_ch(ch):
        p = ch % 2
        bk.do(bk.Y, lambda e, c0=ch * TC, p=p: e.dma_start(
            out=sb3(xc[p].ap), in_=chunk_view(x_in, c0, TC)),
            w=[xc[p]], dma=True)

    out_toks = []
    load_ch(0)
    for ch in range(NCH):
        p = ch % 2
        c0 = ch * TC
        ln = lns[p]
        for ft in range(PT):
            sl = (slice(None), slice(ft * TC, (ft + 1) * TC))
            bk.do(bk.V, lambda e, sl=sl, p=p: e.tensor_copy(xcb.ap[sl],
                                                            xc[p].ap[sl]),
                  r=[xc[p]], w=[xcb] if ft == PT - 1 else (),
                  after=xcb.dep_write() if ft == 0 else ())
        if ch + 1 < NCH:
            load_ch(ch + 1)
        emit_ln_stats(bk, ln, ones_b, xcb)
        emit_ln_apply(bk, ln, xc[p], tmp, xnb[p])
        for rt in range(8):
            ps = mm_ps[rt % 2]
            fns = []
            if rt < 4:
                ka, kb, sl_r = wt['r'], wt['in'], rt
            else:
                ka, kb, sl_r = wt['i'], wt['r'], rt - 4
            for kt in range(4):
                fns.append(lambda e, kt=kt, ka=ka, sl_r=sl_r, ps=ps, p=p:
                           e.matmul(
                    ps.ap[:], ka.ap[:, kt * 512 + sl_r * 128:
                                    kt * 512 + (sl_r + 1) * 128],
                    xnb[p].ap[:, kt * TC:(kt + 1) * TC],
                    start=(kt == 0), stop=False))
            for kt in range(4):
                fns.append(lambda e, kt=kt, kb=kb, sl_r=sl_r, ps=ps, p=p:
                           e.matmul(
                    ps.ap[:], kb.ap[:, kt * 512 + sl_r * 128:
                                    kt * 512 + (sl_r + 1) * 128],
                    xnb[p].ap[:, (4 + kt) * TC:(5 + kt) * TC],
                    start=False, stop=(kt == 3)))
            bk.mm_group(fns, r=[wt['r'], wt['in'], wt['i'], xnb[p]], w=[ps])
            sl = (slice(None), slice(rt * TC, (rt + 1) * TC))
            bk.do(bk.V, lambda e, sl=sl, rt=rt, ps=ps: e.tensor_scalar_add(
                bo_f.ap[sl], ps.ap[:], biasc.ap[:, rt:rt + 1]),
                r=[ps, biasc], w=[bo_f] if rt == 7 else (),
                after=bo_f.dep_write() if rt == 0 else ())
        for ft in range(PT):
            sl = (slice(None), slice(ft * TC, (ft + 1) * TC))
            bk.do(bk.V, lambda e, sl=sl: e.tensor_copy(bo_b.ap[sl],
                                                       bo_f.ap[sl]),
                  r=[bo_f], w=[bo_b] if ft == PT - 1 else (),
                  after=bo_b.dep_write() if ft == 0 else ())
        emit_ln_stats(bk, ln, ones_b, bo_b)
        emit_ln_apply(bk, ln, bo_f, tmp, bo_f, res=xc[p],
                      streams=(bk.V, bk.V, bk.V))
        out_toks.append(bk.do(bk.Y, lambda e, c0=c0: e.dma_start(
            out=chunk_view(x_out, c0, TC), in_=sb3(bo_f.ap)),
            r=[bo_f], dma=True))
    bk.end_scope()
    return out_toks





def load_cols(bk: BK, name, vec_hbm, n, q=None):
    dst = bk.buf(name, [128, n], F32)
    q = q or bk.Y
    for i in range(n):
        bk.do(q, lambda e, i=i: e.dma_start(
            out=dst.ap[:, i:i + 1],
            in_=vec_hbm[i * 128:(i + 1) * 128].unsqueeze(1)),
            w=[dst] if i == n - 1 else (), dma=True,
            after=dst.dep_write() if i == 0 else ())
    return dst


def load_conv_cols(bk: BK, name, w_hbm, b_hbm):
    dst = bk.buf(name, [128, 4 * PT], F32)
    first = True
    for k in range(3):
        for ft in range(PT):
            bk.do(bk.Y, lambda e, k=k, ft=ft: e.dma_start(
                out=dst.ap[:, k * PT + ft:k * PT + ft + 1],
                in_=w_hbm[ft * 128:(ft + 1) * 128, 0, k].unsqueeze(1),
                allow_slow_non_contiguous=True),
                dma=True, after=dst.dep_write() if first else ())
            first = False
    for ft in range(PT):
        bk.do(bk.Y, lambda e, ft=ft: e.dma_start(
            out=dst.ap[:, 3 * PT + ft:3 * PT + ft + 1],
            in_=b_hbm[ft * 128:(ft + 1) * 128].unsqueeze(1)),
            w=[dst] if ft == PT - 1 else (), dma=True)
    return dst


def load_lhsT(bk: BK, name, w_hbm, kt_n, n_cols):
    dst = bk.buf(name, [128, kt_n * n_cols], BF16)
    for i in range(kt_n):
        bk.do(bk.G, lambda e, i=i: e.dma_start(
            out=dst.ap[:, i * n_cols:(i + 1) * n_cols],
            in_=w_hbm[i * 128:(i + 1) * 128, :]),
            w=[dst] if i == kt_n - 1 else (), dma=True,
            after=dst.dep_write() if i == 0 else ())
    return dst


def load_halo(bk: BK, dst: Buf, src_hbm, c0, tcx):
    W2 = tcx + 2
    d3 = dst.ap.rearrange("p (ft t) -> p ft t", t=W2)
    a, b = max(c0 - 1, 0), min(c0 + tcx + 1, T)
    lo = 1 - (c0 - a)
    if c0 == 0:
        bk.do(bk.V, lambda e: e.memset(d3[:, :, 0:1], 0.0),
              w=(), after=dst.dep_write())
    if c0 + tcx == T:
        bk.do(bk.V, lambda e: e.memset(d3[:, :, W2 - 1:W2], 0.0),
              w=(), after=dst.dep_write())
    return bk.do(bk.Y, lambda e: e.dma_start(
        out=d3[:, :, lo:lo + (b - a)], in_=chunk_view(src_hbm, a, b - a)),
        w=[dst], dma=True)


def emit_l0_attn(bk: BK, W, consts, x_in, x_out, dbg=None):
    nc = bk.nc
    TCB = 512
    NCB = T // TCB
    H2 = TCB + 2
    bk.scope()
    ln = LNScratch(bk)
    ones_b = bk.buf("ones_m", [128, 128], BF16)
    bk.do(bk.V, lambda e: e.memset(ones_b.ap[:], 1.0), w=[ones_b])
    ident = bk.buf("identm", [128, 128], BF16)
    bk.do(bk.G, lambda e: e.dma_start(out=ident.ap[:], in_=consts['ident'][:]),
          w=[ident], dma=True)
    ind64 = bk.buf("ind64b", [128, 128], BF16)
    bk.do(bk.G, lambda e: e.dma_start(out=ind64.ap[:], in_=consts['ind64'][:]),
          w=[ind64], dma=True)
    kv_b = bk.buf("m_kvb", [128, 8 * 128], BF16)
    m2 = bk.buf("m_m2", [128, PT * 128], BF16)
    kmean = bk.buf("m_kmean", [128, PT], F32)
    def hb(n, dt=F32, key=None):
        if dbg and key in dbg:
            return Buf(dbg[key], n)
        return Buf(nc.dram_tensor(n, [C, T], dt, kind="Internal").ap(), n)
    xn_h, y_h = hb("m_xnh", key='xn'), hb("m_yh", key='y')
    xi0_h = hb("m_xi0h", key='xi0')
    act_h = hb("m_acth", BF16, key='act')
    q_h = Buf(xn_h.ap.bitcast(BF16)[:, 0:T], "m_qh")
    y2_h = Buf(xi0_h.ap, "m_y2h")
    if dbg and 'y2' in dbg:
        y2_h = Buf(dbg['y2'], "m_y2h")

    def conv_tile(dstb, dsl, srcb, sb, cw, ft, tcx, w1p=None, bias=True,
                  last=False):
        w1c = (w1p.ap[:, ft:ft + 1] if w1p is not None
               else cw.ap[:, PT + ft:PT + ft + 1])
        bc = cw.ap[:, 3 * PT + ft:3 * PT + ft + 1]
        rd = [srcb, cw] + ([w1p] if w1p is not None else [])
        bk.do(bk.V, lambda e: e.tensor_scalar(
            dstb.ap[:, dsl:dsl + tcx], srcb.ap[:, sb + 1:sb + 1 + tcx],
            w1c, bc, ALU.mult, ALU.add), r=rd,
            w=(), after=dstb.dep_write() if dsl == 0 else ())
        bk.do(bk.V, lambda e: e.scalar_tensor_tensor(
            dstb.ap[:, dsl:dsl + tcx], srcb.ap[:, sb:sb + tcx],
            cw.ap[:, 0 * PT + ft:0 * PT + ft + 1],
            dstb.ap[:, dsl:dsl + tcx], ALU.mult, ALU.add), r=rd)
        bk.do(bk.V, lambda e: e.scalar_tensor_tensor(
            dstb.ap[:, dsl:dsl + tcx], srcb.ap[:, sb + 2:sb + 2 + tcx],
            cw.ap[:, 2 * PT + ft:2 * PT + ft + 1],
            dstb.ap[:, dsl:dsl + tcx], ALU.mult, ALU.add), r=rd,
            w=[dstb] if last else ())

    bk.scope()
    xi_b = bk.buf("m_xib", [128, PT * (T + 2)], BF16)
    x3 = xi_b.ap.rearrange("p (ft t) -> p ft t", t=T + 2)
    bk.do(bk.V, lambda e: e.memset(x3[:, :, 0:1], 0.0), w=())
    bk.do(bk.V, lambda e: e.memset(x3[:, :, T + 1:T + 2], 0.0), w=())

    TCA, H2A = 256, 258
    NCA = T // TCA
    bk.scope()
    cpe1c = load_conv_cols(bk, "m_cpe1", W['L0_attn_cpe1_w'],
                           W['L0_attn_cpe1_b'])
    cpe1p = bk.buf("m_cpe1p", [128, PT], F32)
    bk.do(bk.V, lambda e: e.tensor_scalar_add(
        cpe1p.ap[:], cpe1c.ap[:, PT:2 * PT], 1.0), r=[cpe1c], w=[cpe1p])
    dwcc = load_conv_cols(bk, "m_dwc", W['L0_attn_dwc_w'], W['L0_attn_dwc_b'])
    n1g = load_cols(bk, "m_n1g", W['L0_attn_norm1_g'], PT)
    n1b = load_cols(bk, "m_n1b", W['L0_attn_norm1_b'], PT)
    actw = load_lhsT(bk, "m_actw", W['L0_attn_actp_w'], PT, 1024)
    actb = load_cols(bk, "m_actb", W['L0_attn_actp_b'], PT)
    inw = load_lhsT(bk, "m_inw", W['L0_attn_in_w'], PT, 1024)
    inb = load_cols(bk, "m_inb", W['L0_attn_in_b'], PT)
    xc = bk.buf("m_xc", [128, PT * TCA], F32)
    xcb = bk.buf("m_xcb", [128, PT * TCA], BF16)
    tmpA = bk.buf("m_tmpA", [128, PT * TCA], F32)
    xnf = bk.buf("m_xnf", [128, PT * TCA], F32)
    mmA = [bk.pbuf(f"m_mmA{i}", [128, TCA], F32) for i in range(2)]
    for ch in range(NCA):
        c0 = ch * TCA
        bk.do(bk.Y, lambda e, c0=c0: e.dma_start(
            out=sb3(xc.ap), in_=chunk_view(x_in, c0, TCA)), w=[xc], dma=True)
        for ft in range(PT):
            sl = (slice(None), slice(ft * TCA, (ft + 1) * TCA))
            bk.do(bk.V, lambda e, sl=sl: e.tensor_copy(xcb.ap[sl], xc.ap[sl]),
                  r=[xc], w=[xcb] if ft == PT - 1 else (),
                  after=xcb.dep_write() if ft == 0 else ())
        emit_ln_stats(bk, ln, ones_b, xcb, tc=TCA)
        emit_ln_apply(bk, ln, xc, tmpA, xnf, tc=TCA)
        bk.do(bk.Y, lambda e, c0=c0: e.dma_start(
            out=chunk_view(xn_h.ap, c0, TCA), in_=sb3(xnf.ap)),
            r=[xnf], w=[xn_h] if ch == NCA - 1 else (), dma=True,
            after=xn_h.dep_write() if ch == 0 else ())
    xh = bk.buf("m_xh", [128, PT * H2A], F32)
    yA = bk.buf("m_yA", [128, PT * TCA], F32)
    yb = bk.buf("m_yb", [128, PT * TCA], BF16)
    ynb = bk.buf("m_ynb", [128, PT * TCA], BF16)
    acb = bk.buf("m_acb", [128, PT * TCA], BF16)
    for ch in range(NCA):
        c0 = ch * TCA
        load_halo(bk, xh, xn_h.ap, c0, TCA)
        for ft in range(PT):
            conv_tile(yA, ft * TCA, xh, ft * H2A, cpe1c, ft, TCA,
                      w1p=cpe1p, last=(ft == PT - 1))
        bk.do(bk.Y, lambda e, c0=c0: e.dma_start(
            out=chunk_view(y_h.ap, c0, TCA), in_=sb3(yA.ap)),
            r=[yA], w=[y_h] if ch == NCA - 1 else (), dma=True,
            after=y_h.dep_write() if ch == 0 else ())
        for ft in range(PT):
            sl = (slice(None), slice(ft * TCA, (ft + 1) * TCA))
            bk.do(bk.V, lambda e, sl=sl: e.tensor_copy(yb.ap[sl], yA.ap[sl]),
                  r=[yA], w=[yb] if ft == PT - 1 else (),
                  after=yb.dep_write() if ft == 0 else ())
        emit_ln_stats(bk, ln, ones_b, yb, tc=TCA)
        for ft in range(PT):
            sl = (slice(None), slice(ft * TCA, (ft + 1) * TCA))
            bk.do(bk.V, lambda e, sl=sl: e.tensor_tensor(
                tmpA.ap[sl], yA.ap[sl], ln.r.ap[:, 0:TCA], ALU.mult),
                r=[yA, ln.r], w=[tmpA] if ft == PT - 1 else (),
                after=tmpA.dep_write() if ft == 0 else ())
        for ft in range(PT):
            sl = (slice(None), slice(ft * TCA, (ft + 1) * TCA))
            bk.do(bk.V, lambda e, sl=sl: e.tensor_tensor(
                tmpA.ap[sl], tmpA.ap[sl], ln.b2.ap[:, 0:TCA], ALU.add),
                r=[ln.b2, tmpA], w=[tmpA] if ft == PT - 1 else ())
        for ft in range(PT):
            sl = (slice(None), slice(ft * TCA, (ft + 1) * TCA))
            bk.do(bk.V, lambda e, sl=sl, ft=ft: e.tensor_scalar(
                ynb.ap[sl], tmpA.ap[sl], n1g.ap[:, ft:ft + 1],
                n1b.ap[:, ft:ft + 1], ALU.mult, ALU.add),
                r=[tmpA, n1g, n1b], w=[ynb] if ft == PT - 1 else (),
                after=ynb.dep_write() if ft == 0 else ())
        for fo in range(PT):
            ps = mmA[fo % 2]
            bk.mm_group([(lambda e, fo=fo, kt=kt, ps=ps: e.matmul(
                ps.ap[:], actw.ap[:, kt * 1024 + fo * 128:
                                 kt * 1024 + (fo + 1) * 128],
                ynb.ap[:, kt * TCA:(kt + 1) * TCA],
                start=(kt == 0), stop=(kt == PT - 1))) for kt in range(PT)],
                r=[actw, ynb], w=[ps])
            bk.do(bk.S, lambda e, fo=fo, ps=ps: e.activation(
                acb.ap[:, fo * TCA:(fo + 1) * TCA], ps.ap[:], AF.Silu,
                bias=actb.ap[:, fo:fo + 1]),
                r=[ps, actb], w=[acb] if fo == PT - 1 else (),
                after=acb.dep_write() if fo == 0 else ())
        bk.do(bk.Y, lambda e, c0=c0: e.dma_start(
            out=chunk_view(act_h.ap, c0, TCA), in_=sb3(acb.ap)),
            r=[acb], w=[act_h] if ch == NCA - 1 else (), dma=True,
            after=act_h.dep_write() if ch == 0 else ())
        for fo in range(PT):
            ps = mmA[fo % 2]
            bk.mm_group([(lambda e, fo=fo, kt=kt, ps=ps: e.matmul(
                ps.ap[:], inw.ap[:, kt * 1024 + fo * 128:
                                kt * 1024 + (fo + 1) * 128],
                ynb.ap[:, kt * TCA:(kt + 1) * TCA],
                start=(kt == 0), stop=(kt == PT - 1))) for kt in range(PT)],
                r=[inw, ynb], w=[ps])
            bk.do(bk.V, lambda e, fo=fo, ps=ps: e.tensor_scalar_add(
                xnf.ap[:, fo * TCA:(fo + 1) * TCA], ps.ap[:],
                inb.ap[:, fo:fo + 1]),
                r=[ps, inb], w=[xnf] if fo == PT - 1 else (),
                after=xnf.dep_write() if fo == 0 else ())
        bk.do(bk.Y, lambda e, c0=c0: e.dma_start(
            out=chunk_view(xi0_h.ap, c0, TCA), in_=sb3(xnf.ap)),
            r=[xnf], w=[xi0_h] if ch == NCA - 1 else (), dma=True,
            after=xi0_h.dep_write() if ch == 0 else ())
    for ch in range(NCA):
        c0 = ch * TCA
        load_halo(bk, xh, xi0_h.ap, c0, TCA)
        for ft in range(PT):
            conv_tile(tmpA, ft * TCA, xh, ft * H2A, dwcc, ft, TCA,
                      last=(ft == PT - 1))
        for ft in range(PT):
            bk.do(bk.S, lambda e, ft=ft, c0=c0: e.activation(
                xi_b.ap[:, ft * (T + 2) + 1 + c0:
                        ft * (T + 2) + 1 + c0 + TCA],
                tmpA.ap[:, ft * TCA:(ft + 1) * TCA], AF.Silu),
                r=[tmpA], w=[xi_b] if (ch, ft) == (NCA - 1, PT - 1) else (),
                after=xi_b.dep_write() if (ch, ft) == (0, 0) else ())
    bk.end_scope()

    bk.scope()
    qkw = load_lhsT(bk, "m_qkw", W['L0_attn_qk_w'], PT, 2048)
    qkb = load_cols(bk, "m_qkb", W['L0_attn_qk_b'], 16)
    q_ch = bk.buf("m_qch", [128, PT * TCB], BF16)
    k_ch = bk.buf("m_kch", [128, PT * TCB], BF16)
    e1 = bk.buf("m_e1", [128, TCB], F32)
    e2 = bk.buf("m_e2", [128, TCB], F32)
    e3 = bk.buf("m_e3", [128, TCB], F32)
    red = bk.buf("m_red", [128, 1], F32)
    ktm = bk.buf("m_ktm", [128, 4 * 1024], BF16)
    xitm = bk.buf("m_xitm", [128, 4 * 1024], BF16)
    mmB = [bk.pbuf(f"m_mmB{i}", [128, TCB], F32) for i in range(2)]
    kvps = bk.pbuf("m_kvps", [128, 512], F32)
    kvacc = bk.buf("m_kvacc", [128, 512], F32)
    tpsb = bk.pbuf("m_tpsb", [128, 128], BF16)
    bk.do(bk.V, lambda e: e.memset(kmean.ap[:], 0.0), w=[kmean])
    for ch in range(NCB):
        c0 = ch * TCB
        for fo2 in range(16):
            ps = mmB[fo2 % 2]
            bk.mm_group([(lambda e, fo2=fo2, kt=kt, ps=ps, c0=c0: e.matmul(
                ps.ap[:], qkw.ap[:, kt * 2048 + fo2 * 128:
                                kt * 2048 + (fo2 + 1) * 128],
                xi_b.ap[:, kt * (T + 2) + 1 + c0:
                        kt * (T + 2) + 1 + c0 + TCB],
                start=(kt == 0), stop=(kt == PT - 1))) for kt in range(PT)],
                r=[qkw, xi_b], w=[ps])
            bk.do(bk.V, lambda e, fo2=fo2, ps=ps: e.tensor_scalar(
                e1.ap[:], ps.ap[:], qkb.ap[:, fo2:fo2 + 1], 0.0,
                ALU.add, ALU.min), r=[ps, qkb], w=[e1])
            bk.do(bk.S, lambda e: e.activation(e2.ap[:], e1.ap[:], AF.Exp),
                  r=[e1], w=[e2])
            bk.do(bk.V, lambda e, fo2=fo2, ps=ps: e.tensor_scalar(
                e3.ap[:], ps.ap[:], qkb.ap[:, fo2:fo2 + 1], 0.0,
                ALU.add, ALU.max), r=[ps, qkb], w=[e3])
            dst, dc = (q_ch, fo2) if fo2 < 8 else (k_ch, fo2 - 8)
            bk.do(bk.V, lambda e, dst=dst, dc=dc: e.tensor_tensor(
                dst.ap[:, dc * TCB:(dc + 1) * TCB], e2.ap[:], e3.ap[:],
                ALU.add), r=[e2, e3], w=[dst] if dc == PT - 1 else (),
                after=dst.dep_write() if dc == 0 else ())
        bk.do(bk.Y, lambda e, c0=c0: e.dma_start(
            out=chunk_view(q_h.ap, c0, TCB), in_=sb3(q_ch.ap)),
            r=[q_ch], w=[q_h] if ch == NCB - 1 else (), dma=True,
            after=q_h.dep_write() if ch == 0 else ())
        if dbg and 'q' in dbg:
            bk.do(bk.Y, lambda e, c0=c0: e.dma_start(
                out=chunk_view(dbg['q'], c0, TCB), in_=sb3(q_ch.ap)),
                r=[q_ch], dma=True)
        for ft in range(PT):
            bk.do(bk.V, lambda e, ft=ft: e.reduce_sum(
                red.ap[:], k_ch.ap[:, ft * TCB:(ft + 1) * TCB], axis=AX.X),
                r=[k_ch], w=[red])
            bk.do(bk.V, lambda e, ft=ft: e.tensor_tensor(
                kmean.ap[:, ft:ft + 1], kmean.ap[:, ft:ft + 1], red.ap[:],
                ALU.add), r=[red], w=[kmean])
        for src, dst, ha in ((k_ch, ktm, False), (None, xitm, True)):
            for ft in range(PT):
                for tt in range(4):
                    if ha:
                        sap = xi_b.ap[:, ft * (T + 2) + 1 + c0 + tt * 128:
                                      ft * (T + 2) + 1 + c0 + (tt + 1) * 128]
                        rsrc = xi_b
                    else:
                        sap = src.ap[:, ft * TCB + tt * 128:
                                     ft * TCB + (tt + 1) * 128]
                        rsrc = src
                    bk.mm_group([lambda e, sap=sap: e.transpose(
                        tpsb.ap[:], sap, ident.ap[:])],
                        r=[rsrc, ident], w=[tpsb])
                    bk.do(bk.V, lambda e, ft=ft, tt=tt, dst=dst: e.tensor_copy(
                        dst.ap[:, tt * 1024 + ft * 128:
                               tt * 1024 + (ft + 1) * 128], tpsb.ap[:]),
                        r=[tpsb], w=[dst] if (ft, tt) == (PT - 1, 3) else (),
                        after=dst.dep_write() if (ft, tt) == (0, 0) else ())
        fns = []
        for h in range(16):
            pb, cb = 64 * (h % 2), 64 * (h // 2)
            for tt in range(4):
                fns.append(lambda e, h=h, tt=tt, pb=pb, cb=cb: e.matmul(
                    kvps.ap[pb:pb + 64, cb:cb + 64],
                    ktm.ap[:, tt * 1024 + h * 64:tt * 1024 + (h + 1) * 64],
                    xitm.ap[:, tt * 1024 + h * 64:tt * 1024 + (h + 1) * 64],
                    start=(h <= 1 and tt == 0), stop=(h == 15 and tt == 3),
                    skip_group_check=True))
        bk.mm_group(fns, r=[ktm, xitm], w=[kvps])
        if ch == 0:
            bk.do(bk.V, lambda e: e.tensor_copy(kvacc.ap[:], kvps.ap[:]),
                  r=[kvps], w=[kvacc])
        else:
            bk.do(bk.V, lambda e: e.tensor_tensor(
                kvacc.ap[:], kvacc.ap[:], kvps.ap[:], ALU.add),
                r=[kvps], w=[kvacc])
    bk.do(bk.V, lambda e: e.memset(kv_b.ap[:], 0.0), w=[kv_b])
    for h in range(16):
        pt, par = divmod(h, 2)
        pb, cb = 64 * par, 64 * (h // 2)
        db = pt * 128 + 64 * par
        bk.do(bk.V, lambda e, pb=pb, cb=cb, db=db: e.tensor_scalar_mul(
            kv_b.ap[pb:pb + 64, db:db + 64],
            kvacc.ap[pb:pb + 64, cb:cb + 64], 1.0 / T),
            r=[kvacc], w=[kv_b] if h == 15 else ())
    bk.do(bk.V, lambda e: e.tensor_scalar_mul(kmean.ap[:], kmean.ap[:],
                                              1.0 / T), w=[kmean])
    for ft in range(PT):
        bk.do(bk.V, lambda e, ft=ft: e.tensor_scalar_mul(
            m2.ap[:, ft * 128:(ft + 1) * 128], ind64.ap[:],
            kmean.ap[:, ft:ft + 1]), r=[ind64, kmean],
            w=[m2] if ft == PT - 1 else (),
            after=m2.dep_write() if ft == 0 else ())
    bk.end_scope()

    bk.scope()
    outw = load_lhsT(bk, "m_outw", W['L0_attn_out_w'], PT, 1024)
    outb = load_cols(bk, "m_outb", W['L0_attn_out_b'], PT)
    lepc = load_conv_cols(bk, "m_lep", W['L0_attn_lepe_w'],
                          W['L0_attn_lepe_b'])
    qC = bk.buf("m_qch2", [128, PT * TCB], BF16)
    a_ch = bk.buf("m_ach2", [128, PT * TCB], BF16)
    y_ch = bk.buf("m_ych", [128, PT * TCB], F32)
    zr = bk.buf("m_zr", [128, PT * TCB], F32)
    zden = bk.buf("m_zden", [128, TCB], F32)
    lt = bk.buf("m_lt", [128, TCB], F32)
    af = bk.buf("m_af", [128, TCB], F32)
    af0 = bk.buf("m_af0", [128, TCB], F32)
    gat = bk.buf("m_gat", [128, PT * TCB], BF16)
    y2 = bk.buf("m_y2", [128, PT * TCB], F32)
    mmC = [bk.pbuf(f"m_mmC{i}", [128, TCB], F32) for i in range(2)]
    for ch in range(NCB):
        c0 = ch * TCB
        bk.do(bk.Y, lambda e, c0=c0: e.dma_start(
            out=sb3(qC.ap), in_=chunk_view(q_h.ap, c0, TCB)),
            r=[q_h], w=[qC], dma=True)
        bk.do(bk.Y, lambda e, c0=c0: e.dma_start(
            out=sb3(a_ch.ap), in_=chunk_view(act_h.ap, c0, TCB)),
            r=[act_h], w=[a_ch], dma=True)
        bk.do(bk.Y, lambda e, c0=c0: e.dma_start(
            out=sb3(y_ch.ap), in_=chunk_view(y_h.ap, c0, TCB)),
            r=[y_h], w=[y_ch], dma=True)
        for ft in range(PT):
            ps = mmC[ft % 2]
            bk.mm_group([lambda e, ft=ft, ps=ps: e.matmul(
                ps.ap[:], m2.ap[:, ft * 128:(ft + 1) * 128],
                qC.ap[:, ft * TCB:(ft + 1) * TCB],
                start=True, stop=True)], r=[m2, qC], w=[ps])
            bk.do(bk.V, lambda e, ps=ps: e.tensor_scalar_add(
                zden.ap[:], ps.ap[:], 1e-6), r=[ps], w=[zden], drain=True)
            bk.do(bk.V, lambda e, ft=ft: e.reciprocal(
                zr.ap[:, ft * TCB:(ft + 1) * TCB], zden.ap[:]),
                r=[zden], w=[zr] if ft == PT - 1 else (), drain=True,
                after=zr.dep_write() if ft == 0 else ())
        bk.V.force_wait(zr.w)
        for ft in range(PT):
            ps = mmC[ft % 2]
            bk.mm_group([lambda e, ft=ft, ps=ps: e.matmul(
                ps.ap[:], kv_b.ap[:, ft * 128:(ft + 1) * 128],
                qC.ap[:, ft * TCB:(ft + 1) * TCB],
                start=True, stop=True)], r=[kv_b, qC], w=[ps])
            sb = ft * (T + 2) + c0
            bk.do(bk.V, lambda e, ft=ft, sb=sb: e.tensor_scalar(
                lt.ap[:], xi_b.ap[:, sb + 1:sb + 1 + TCB],
                lepc.ap[:, PT + ft:PT + ft + 1],
                lepc.ap[:, 3 * PT + ft:3 * PT + ft + 1], ALU.mult, ALU.add),
                r=[xi_b, lepc], w=[lt])
            bk.do(bk.V, lambda e, ft=ft, sb=sb: e.scalar_tensor_tensor(
                lt.ap[:], xi_b.ap[:, sb:sb + TCB],
                lepc.ap[:, ft:ft + 1], lt.ap[:], ALU.mult, ALU.add),
                r=[xi_b, lepc, lt], w=[lt])
            bk.do(bk.V, lambda e, ft=ft, sb=sb: e.scalar_tensor_tensor(
                lt.ap[:], xi_b.ap[:, sb + 2:sb + 2 + TCB],
                lepc.ap[:, 2 * PT + ft:2 * PT + ft + 1], lt.ap[:],
                ALU.mult, ALU.add), r=[xi_b, lepc, lt], w=[lt])
            bk.do(bk.S, lambda e, ps=ps: e.copy(af0.ap[:], ps.ap[:]),
                  r=[ps], w=[af0])
            bk.do(bk.V, lambda e, ft=ft: e.tensor_tensor(
                af.ap[:], af0.ap[:], zr.ap[:, ft * TCB:(ft + 1) * TCB],
                ALU.mult), r=[af0, zr], w=[af])
            bk.do(bk.V, lambda e: e.tensor_tensor(
                af.ap[:], af.ap[:], lt.ap[:], ALU.add), r=[lt, af], w=[af])
            if dbg and 'afd' in dbg:
                bk.do(bk.Y, lambda e, ch=ch, ft=ft: e.dma_start(
                    out=dbg['afd'][:, (ch * 8 + ft) * 512:
                                   (ch * 8 + ft + 1) * 512], in_=af.ap[:]),
                    r=[af], dma=True)
                bk.do(bk.Y, lambda e, ch=ch, ft=ft: e.dma_start(
                    out=dbg['af0d'][:, (ch * 8 + ft) * 512:
                                    (ch * 8 + ft + 1) * 512], in_=af0.ap[:]),
                    r=[af0], dma=True)
                bk.do(bk.Y, lambda e, ch=ch, ft=ft: e.dma_start(
                    out=dbg['ltd'][:, (ch * 8 + ft) * 512:
                                   (ch * 8 + ft + 1) * 512], in_=lt.ap[:]),
                    r=[lt], dma=True)
            bk.do(bk.V, lambda e, ft=ft: e.tensor_tensor(
                gat.ap[:, ft * TCB:(ft + 1) * TCB], af.ap[:],
                a_ch.ap[:, ft * TCB:(ft + 1) * TCB], ALU.mult),
                r=[af, a_ch], w=[gat] if ft == PT - 1 else (),
                after=gat.dep_write() if ft == 0 else ())
            if os.environ.get('MK_SERC3'):
                bk.barrier()
        for fo in range(PT):
            ps = mmC[fo % 2]
            bk.mm_group([(lambda e, fo=fo, kt=kt, ps=ps: e.matmul(
                ps.ap[:], outw.ap[:, kt * 1024 + fo * 128:
                                 kt * 1024 + (fo + 1) * 128],
                gat.ap[:, kt * TCB:(kt + 1) * TCB],
                start=(kt == 0), stop=(kt == PT - 1))) for kt in range(PT)],
                r=[outw, gat], w=[ps])
            bk.do(bk.V, lambda e, fo=fo, ps=ps: e.scalar_tensor_tensor(
                y2.ap[:, fo * TCB:(fo + 1) * TCB], ps.ap[:],
                outb.ap[:, fo:fo + 1], y_ch.ap[:, fo * TCB:(fo + 1) * TCB],
                ALU.add, ALU.add), r=[ps, outb, y_ch],
                w=[y2] if fo == PT - 1 else (),
                after=y2.dep_write() if fo == 0 else ())
        bk.do(bk.Y, lambda e, c0=c0: e.dma_start(
            out=chunk_view(y2_h.ap, c0, TCB), in_=sb3(y2.ap)),
            r=[y2], w=[y2_h] if ch == NCB - 1 else (), dma=True,
            after=y2_h.dep_write() if ch == 0 else ())
        if os.environ.get('MK_SERC'):
            bk.barrier()
        if dbg and 'zrd' in dbg:
            bk.do(bk.Y, lambda e, ch=ch: e.dma_start(
                out=dbg['zrd'][:, ch * 4096:(ch + 1) * 4096], in_=zr.ap[:]),
                r=[zr], dma=True)
        if dbg and 'gat' in dbg:
            bk.do(bk.Y, lambda e, ch=ch: e.dma_start(
                out=dbg['gat'][:, ch * 4096:(ch + 1) * 4096], in_=gat.ap[:]),
                r=[gat], dma=True)
    bk.end_scope()
    bk.end_scope()

    x_i = hb("m_xint")
    bk.scope()
    xcp = bk.buf("m_xcp", [128, PT * 512], F32)
    for ch in range(8):
        c0 = ch * 512
        bk.do(bk.Y, lambda e, c0=c0: e.dma_start(
            out=sb3(xcp.ap, 512), in_=chunk_view(x_in, c0, 512)),
            w=[xcp], dma=True)
        bk.do(bk.Y, lambda e, c0=c0: e.dma_start(
            out=chunk_view(x_i.ap, c0, 512), in_=sb3(xcp.ap, 512)),
            r=[xcp], w=[x_i] if ch == 7 else (), dma=True,
            after=x_i.dep_write() if ch == 0 else ())
    bk.end_scope(barrier=True)
    bk.scope()
    TCD = 256
    NCD = T // TCD
    H2D = TCD + 2
    fc1w = load_lhsT(bk, "m_fc1w", W['L0_attn_fc1_w'], PT, 4096)
    fc2w = load_lhsT(bk, "m_fc2w", W['L0_attn_fc2_w'], 32, 1024)
    fc1b = load_cols(bk, "m_fc1b", W['L0_attn_fc1_b'], 32)
    fc2b = load_cols(bk, "m_fc2b", W['L0_attn_fc2_b'], PT)
    cpe2c = load_conv_cols(bk, "m_cpe2", W['L0_attn_cpe2_w'],
                           W['L0_attn_cpe2_b'])
    cpe2p = bk.buf("m_cpe2p", [128, PT], F32)
    bk.do(bk.V, lambda e: e.tensor_scalar_add(
        cpe2p.ap[:], cpe2c.ap[:, PT:2 * PT], 1.0), r=[cpe2c], w=[cpe2p])
    n2g = load_cols(bk, "m_n2g", W['L0_attn_norm2_g'], PT)
    n2b = load_cols(bk, "m_n2b", W['L0_attn_norm2_b'], PT)
    y2h = bk.buf("m_y2hh", [128, PT * H2D], F32)
    y2c = bk.buf("m_y2c", [128, PT * TCD], F32)
    y2b = bk.buf("m_y2b", [128, PT * TCD], BF16)
    yn2 = bk.buf("m_yn2", [128, PT * TCD], BF16)
    hbuf = bk.buf("m_hbuf", [128, 32 * TCD], BF16)
    xcD = bk.buf("m_xcD", [128, PT * TCD], F32)
    tmpD = bk.buf("m_tmpD", [128, PT * TCD], F32)
    mmD = [bk.pbuf(f"m_mmD{i}", [128, TCD], F32) for i in range(2)]
    out_toks = []
    for ch in range(NCD):
        c0 = ch * TCD
        load_halo(bk, y2h, y2_h.ap, c0, TCD)
        for ft in range(PT):
            conv_tile(y2c, ft * TCD, y2h, ft * H2D, cpe2c, ft, TCD,
                      w1p=cpe2p, last=(ft == PT - 1))
        for ft in range(PT):
            sl = (slice(None), slice(ft * TCD, (ft + 1) * TCD))
            bk.do(bk.V, lambda e, sl=sl: e.tensor_copy(y2b.ap[sl],
                                                       y2c.ap[sl]),
                  r=[y2c], w=[y2b] if ft == PT - 1 else (),
                  after=y2b.dep_write() if ft == 0 else ())
        emit_ln_stats(bk, ln, ones_b, y2b, tc=TCD)
        for ft in range(PT):
            sl = (slice(None), slice(ft * TCD, (ft + 1) * TCD))
            bk.do(bk.V, lambda e, sl=sl: e.tensor_tensor(
                tmpD.ap[sl], y2c.ap[sl], ln.r.ap[:, 0:TCD], ALU.mult),
                r=[y2c, ln.r], w=[tmpD] if ft == PT - 1 else (),
                after=tmpD.dep_write() if ft == 0 else ())
        for ft in range(PT):
            sl = (slice(None), slice(ft * TCD, (ft + 1) * TCD))
            bk.do(bk.V, lambda e, sl=sl: e.tensor_tensor(
                tmpD.ap[sl], tmpD.ap[sl], ln.b2.ap[:, 0:TCD], ALU.add),
                r=[ln.b2, tmpD], w=[tmpD] if ft == PT - 1 else ())
        for ft in range(PT):
            sl = (slice(None), slice(ft * TCD, (ft + 1) * TCD))
            bk.do(bk.V, lambda e, sl=sl, ft=ft: e.tensor_scalar(
                yn2.ap[sl], tmpD.ap[sl], n2g.ap[:, ft:ft + 1],
                n2b.ap[:, ft:ft + 1], ALU.mult, ALU.add),
                r=[tmpD, n2g, n2b], w=[yn2] if ft == PT - 1 else (),
                after=yn2.dep_write() if ft == 0 else ())
        for fo in range(32):
            ps = mmD[fo % 2]
            bk.mm_group([(lambda e, fo=fo, kt=kt, ps=ps: e.matmul(
                ps.ap[:], fc1w.ap[:, kt * 4096 + fo * 128:
                                 kt * 4096 + (fo + 1) * 128],
                yn2.ap[:, kt * TCD:(kt + 1) * TCD],
                start=(kt == 0), stop=(kt == PT - 1))) for kt in range(PT)],
                r=[fc1w, yn2], w=[ps])
            bk.do(bk.S, lambda e, fo=fo, ps=ps: e.activation(
                hbuf.ap[:, fo * TCD:(fo + 1) * TCD], ps.ap[:], AF.Gelu,
                bias=fc1b.ap[:, fo:fo + 1]),
                r=[ps, fc1b], w=[hbuf] if fo == 31 else (),
                after=hbuf.dep_write() if fo == 0 else ())
        for fo in range(PT):
            ps = mmD[fo % 2]
            bk.mm_group([(lambda e, fo=fo, kt=kt, ps=ps: e.matmul(
                ps.ap[:], fc2w.ap[:, kt * 1024 + fo * 128:
                                 kt * 1024 + (fo + 1) * 128],
                hbuf.ap[:, kt * TCD:(kt + 1) * TCD],
                start=(kt == 0), stop=(kt == 31))) for kt in range(32)],
                r=[fc2w, hbuf], w=[ps])
            bk.do(bk.V, lambda e, fo=fo, ps=ps: e.scalar_tensor_tensor(
                y2c.ap[:, fo * TCD:(fo + 1) * TCD], ps.ap[:],
                fc2b.ap[:, fo:fo + 1], y2c.ap[:, fo * TCD:(fo + 1) * TCD],
                ALU.add, ALU.add), r=[ps, fc2b, y2c],
                w=[y2c] if fo == PT - 1 else ())
        for ft in range(PT):
            sl = (slice(None), slice(ft * TCD, (ft + 1) * TCD))
            bk.do(bk.V, lambda e, sl=sl: e.tensor_copy(y2b.ap[sl],
                                                       y2c.ap[sl]),
                  r=[y2c], w=[y2b] if ft == PT - 1 else (),
                  after=y2b.dep_write() if ft == 0 else ())
        emit_ln_stats(bk, ln, ones_b, y2b, tc=TCD)
        bk.do(bk.Y, lambda e, c0=c0: e.dma_start(
            out=sb3(xcD.ap, TCD), in_=chunk_view(x_i.ap, c0, TCD)),
            r=[x_i], w=[xcD], dma=True)
        emit_ln_apply(bk, ln, y2c, tmpD, tmpD, tc=TCD, res=xcD)
        out_toks.append(bk.do(bk.Y, lambda e, c0=c0: e.dma_start(
            out=chunk_view(x_out, c0, TCD), in_=sb3(tmpD.ap, TCD)),
            r=[tmpD, xcD], dma=True))
    bk.end_scope(barrier=bool(dbg and 'qq' in dbg))
    if dbg and 'qq' in dbg:
        bk.scope()
        bq = bk.buf("m_dbgq", [128, PT * 256], BF16)
        for ch2 in range(16):
            c02 = ch2 * 256
            bk.do(bk.Y, lambda e, c02=c02: e.dma_start(
                out=sb3(bq.ap, 256), in_=chunk_view(q_h.ap, c02, 256)),
                r=[q_h], w=[bq], dma=True)
            bk.do(bk.Y, lambda e, c02=c02: e.dma_start(
                out=dbg['qq'].rearrange("(ft p) t -> p ft t", p=128)
                [:, :, c02:c02 + 256], in_=sb3(bq.ap, 256)),
                r=[bq], dma=True)
        bk.end_scope(barrier=False)
    bk.end_scope(barrier=False)
    return out_toks




def emit_l0_ffn(bk: BK, W, consts, x_in, x_out, dbg=None):
    nc = bk.nc
    TCB = 512
    NCB = T // TCB
    bk.scope()
    ones_b = bk.buf("ones_f", [128, 128], BF16)
    bk.do(bk.V, lambda e: e.memset(ones_b.ap[:], 1.0), w=[ones_b])
    e8 = bk.buf("e8b", [8, 1024], BF16)
    bk.do(bk.G, lambda e: e.dma_start(out=e8.ap[:], in_=consts['e8'][:]),
          w=[e8], dma=True)
    gnb = bk.buf("f_gnb", [8, T], BF16)
    hb = lambda n, dt, shape: Buf(nc.dram_tensor(n, shape, dt,
                                                 kind="Internal").ap(), n)
    acc_h = hb("f_acch", F32, [C, T])
    acc_p = [Buf(acc_h.ap, f"f_acc{i}") for i in range(T // 256)]

    def accp(c0, tcx):
        return acc_p[c0 // 256:(c0 + tcx) // 256]
    ymid_h = hb("f_ymidh", BF16, [C, T])
    q_h = hb("f_qh", BF16, [C, T])
    k_h = hb("f_kh", BF16, [C, T])
    vt_h = hb("f_vth", BF16, [T, C])

    bk.scope()
    ln = LNScratch(bk, "f0")
    xn_b = bk.buf("f_xnb", [128, PT * T], BF16)
    xc = bk.buf("f_xc", [128, PT * TCB], F32)
    xcb = bk.buf("f_xcb", [128, PT * TCB], BF16)
    tmp = bk.buf("f_tmp", [128, PT * TCB], F32)
    xnf = bk.buf("f_xnf", [128, PT * TCB], F32)
    for ch in range(NCB):
        c0 = ch * TCB
        bk.do(bk.Y, lambda e, c0=c0: e.dma_start(
            out=sb3(xc.ap), in_=chunk_view(x_in, c0, TCB)), w=[xc], dma=True)
        for ft in range(PT):
            sl = (slice(None), slice(ft * TCB, (ft + 1) * TCB))
            bk.do(bk.V, lambda e, sl=sl: e.tensor_copy(xcb.ap[sl], xc.ap[sl]),
                  r=[xc], w=[xcb] if ft == PT - 1 else (),
                  after=xcb.dep_write() if ft == 0 else ())
        emit_ln_stats(bk, ln, ones_b, xcb)
        emit_ln_apply(bk, ln, xc, tmp, xnf)
        bk.do(bk.Y, lambda e, c0=c0: e.dma_start(
            out=chunk_view(acc_h.ap, c0, TCB), in_=sb3(xnf.ap)),
            r=[xnf], w=accp(c0, TCB), dma=True)
        for ft in range(PT):
            dsl = (slice(None), slice(ft * T + c0, ft * T + c0 + TCB))
            sl = (slice(None), slice(ft * TCB, (ft + 1) * TCB))
            bk.do(bk.V, lambda e, sl=sl, dsl=dsl: e.tensor_copy(
                xn_b.ap[dsl], xnf.ap[sl]),
                r=[xnf], w=[xn_b] if (ch, ft) == (NCB - 1, PT - 1) else (),
                after=xn_b.dep_write() if (ch, ft) == (0, 0) else ())
    wq = load_lhsT(bk, "f_wq", W['L0_ffn_Wq'], PT, 1024)
    wk = load_lhsT(bk, "f_wk", W['L0_ffn_Wk'], PT, 1024)
    wv = load_lhsT(bk, "f_wv", W['L0_ffn_Wv'], PT, 1024)
    bqc = load_cols(bk, "f_bq", W['L0_ffn_bq'], PT)
    bkc = load_cols(bk, "f_bk", W['L0_ffn_bk'], PT)
    prj = bk.buf("f_prj", [128, PT * TCB], BF16)
    mmF = [bk.pbuf(f"f_mmF{i}", [128, TCB], F32) for i in range(2)]
    for ch in range(NCB):
        c0 = ch * TCB
        for wmat, bc, dst_h in ((wq, bqc, q_h), (wk, bkc, k_h)):
            for fo in range(PT):
                ps = mmF[fo % 2]
                bk.mm_group([(lambda e, fo=fo, kt=kt, ps=ps, c0=c0,
                              wmat=wmat: e.matmul(
                    ps.ap[:], wmat.ap[:, kt * 1024 + fo * 128:
                                     kt * 1024 + (fo + 1) * 128],
                    xn_b.ap[:, kt * T + c0:kt * T + c0 + TCB],
                    start=(kt == 0), stop=(kt == PT - 1)))
                    for kt in range(PT)], r=[wmat, xn_b], w=[ps])
                bk.do(bk.V, lambda e, fo=fo, ps=ps, bc=bc: e.tensor_scalar(
                    prj.ap[:, fo * TCB:(fo + 1) * TCB], ps.ap[:],
                    bc.ap[:, fo:fo + 1], None, ALU.add),
                    r=[ps, bc], w=[prj] if fo == PT - 1 else (),
                    after=prj.dep_write() if fo == 0 else ())
            bk.do(bk.Y, lambda e, c0=c0, dst_h=dst_h: e.dma_start(
                out=chunk_view(dst_h.ap, c0, TCB), in_=sb3(prj.ap)),
                r=[prj], w=[dst_h] if ch == NCB - 1 else (), dma=True,
                after=dst_h.dep_write() if ch == 0 else ())
        for ts4 in range(4):
            for fo2 in range(2):
                ps = mmF[fo2 % 2]
                bk.mm_group([(lambda e, ts4=ts4, fo2=fo2, kt=kt, ps=ps,
                              c0=c0: e.matmul(
                    ps.ap[:], xn_b.ap[:, kt * T + c0 + ts4 * 128:
                                     kt * T + c0 + (ts4 + 1) * 128],
                    wv.ap[:, kt * 1024 + fo2 * 512:
                          kt * 1024 + (fo2 + 1) * 512],
                    start=(kt == 0), stop=(kt == PT - 1)))
                    for kt in range(PT)], r=[wv, xn_b], w=[ps])
                bk.do(bk.V, lambda e, ts4=ts4, fo2=fo2, ps=ps: e.tensor_copy(
                    prj.ap[:, ts4 * 1024 + fo2 * 512:
                           ts4 * 1024 + (fo2 + 1) * 512], ps.ap[:]),
                    r=[ps], w=[prj] if (ts4, fo2) == (3, 1) else (),
                    after=prj.dep_write() if (ts4, fo2) == (0, 0) else ())
        bk.do(bk.Y, lambda e, c0=c0: e.dma_start(
            out=vt_h.ap[c0:c0 + TCB, :].rearrange(
                "(s p) c -> p s c", p=128),
            in_=prj.ap.rearrange("p (s c) -> p s c", c=1024)),
            r=[prj], w=[vt_h] if ch == NCB - 1 else (), dma=True,
            after=vt_h.dep_write() if ch == 0 else ())
    bk.end_scope()

    QTC = 256
    NQB = T // QTC
    bk.scope()
    k_b = bk.buf("f_kb", [128, PT * T], BF16)
    v_tb = bk.buf("f_vtb", [128, 32 * 1024], BF16)
    bk.do(bk.Y, lambda e: e.dma_start(
        out=k_b.ap.rearrange("p (ft t) -> p ft t", t=T),
        in_=k_h.ap.rearrange("(ft p) t -> p ft t", p=128)),
        r=[k_h], w=[k_b], dma=True)
    bk.do(bk.Y, lambda e: e.dma_start(
        out=v_tb.ap.rearrange("p (s c) -> p s c", c=1024),
        in_=vt_h.ap.rearrange("(s p) c -> p s c", p=128)),
        r=[vt_h], w=[v_tb], dma=True)
    wo = load_lhsT(bk, "f_wo", W['L0_ffn_Wo'], PT, 1024)
    bvc = load_cols(bk, "f_bv", W['L0_ffn_bv'], PT)
    boc = load_cols(bk, "f_bo", W['L0_ffn_bo'], PT)
    bvb = bk.buf("f_bvb", [128, PT], BF16)
    for ft in range(PT):
        bk.do(bk.V, lambda e, ft=ft: e.tensor_copy(
            bvb.ap[:, ft:ft + 1], bvc.ap[:, ft:ft + 1]), r=[bvc],
            w=[bvb] if ft == PT - 1 else (),
            after=bvb.dep_write() if ft == 0 else ())
    bvps = bk.pbuf("f_bvps", [128, 8], F32)
    for fo in range(PT):
        bk.mm_group([(lambda e, fo=fo, kt=kt: e.matmul(
            bvps.ap[:, fo:fo + 1],
            wo.ap[:, kt * 1024 + fo * 128:kt * 1024 + (fo + 1) * 128],
            bvb.ap[:, kt:kt + 1],
            start=(kt == 0), stop=(kt == PT - 1))) for kt in range(PT)],
            r=[wo, bvb], w=() if fo < PT - 1 else [bvps],
            after=bvps.dep_write() if fo == 0 else ())
    bk.do(bk.V, lambda e: e.tensor_tensor(
        boc.ap[:], boc.ap[:], bvps.ap[:], ALU.add), r=[bvps], w=[boc])
    q_ch = bk.buf("f_qch", [128, PT * QTC], BF16)
    xnc = bk.buf("f_xnc", [128, PT * QTC], F32)
    p_t = bk.buf("f_pt", [128, 32 * QTC], BF16)
    zr = bk.buf("f_zr", [128, QTC], F32)
    a_b = bk.buf("f_ab", [128, PT * QTC], BF16)
    a_f = bk.buf("f_af", [128, PT * QTC], F32)
    mmS = [bk.pbuf(f"f_mmS{i}", [128, QTC], F32) for i in range(2)]
    mmP = [bk.pbuf(f"f_mmP{i}", [128, QTC], F32) for i in range(2)]
    zps = bk.pbuf("f_zps", [128, QTC], F32)
    SCALE = 1.0 / 32.0
    for ch in range(NQB):
        c0 = ch * QTC
        bk.do(bk.Y, lambda e, c0=c0: e.dma_start(
            out=sb3(q_ch.ap, QTC), in_=chunk_view(q_h.ap, c0, QTC)),
            r=[q_h], w=[q_ch], dma=True)
        z_tok = None
        for kt in range(32):
            ps = mmS[kt % 2]
            bk.mm_group([(lambda e, kt=kt, ft=ft, ps=ps: e.matmul(
                ps.ap[:], k_b.ap[:, ft * T + kt * 128:ft * T + (kt + 1) * 128],
                q_ch.ap[:, ft * QTC:(ft + 1) * QTC],
                start=(ft == 0), stop=(ft == PT - 1),
                skip_group_check=True)) for ft in range(PT)],
                r=[k_b, q_ch], w=[ps])
            e_tok = bk.do(bk.S, lambda e, kt=kt, ps=ps: e.activation(
                p_t.ap[:, kt * QTC:(kt + 1) * QTC], ps.ap[:], AF.Exp,
                scale=SCALE), r=[ps],
                w=[p_t] if kt == 31 else (),
                after=p_t.dep_write() if kt == 0 else ())
            z_tok = bk.mm_group([lambda e, kt=kt: e.matmul(
                zps.ap[:], ones_b.ap, p_t.ap[:, kt * QTC:(kt + 1) * QTC],
                start=(kt == 0), stop=(kt == 31), skip_group_check=True)],
                r=[ones_b], after=[e_tok] + (
                    zps.dep_write() if kt == 0 else []))
        zps.w = z_tok
        zps.rs = []
        bk.do(bk.V, lambda e: e.reciprocal(zr.ap[:], zps.ap[:]),
              r=[zps], w=[zr])
        for ft in range(PT):
            ps = mmP[ft % 2]
            bk.mm_group([(lambda e, ft=ft, kt=kt, ps=ps: e.matmul(
                ps.ap[:], v_tb.ap[:, kt * 1024 + ft * 128:
                                 kt * 1024 + (ft + 1) * 128],
                p_t.ap[:, kt * QTC:(kt + 1) * QTC],
                start=(kt == 0), stop=(kt == 31),
                skip_group_check=True)) for kt in range(32)],
                r=[v_tb, p_t], w=[ps])
            bk.do(bk.V, lambda e, ft=ft, ps=ps: e.tensor_tensor(
                a_b.ap[:, ft * QTC:(ft + 1) * QTC], ps.ap[:], zr.ap[:],
                ALU.mult), r=[ps, zr], w=[a_b] if ft == PT - 1 else (),
                after=a_b.dep_write() if ft == 0 else ())
        if dbg and 'ab' in dbg and ch == 0:
            bk.do(bk.Y, lambda e: e.dma_start(out=dbg['ab'], in_=a_b.ap[:]),
                  r=[a_b], dma=True)
            bk.do(bk.Y, lambda e: e.dma_start(out=dbg['zr'], in_=zr.ap[:]),
                  r=[zr], dma=True)
            bk.do(bk.Y, lambda e: e.dma_start(out=dbg['pt'], in_=p_t.ap[:]),
                  r=[p_t], dma=True)
        bk.do(bk.Y, lambda e, c0=c0: e.dma_start(
            out=sb3(xnc.ap, QTC), in_=chunk_view(acc_h.ap, c0, QTC)),
            r=accp(c0, QTC), w=[xnc], dma=True)
        for fo in range(PT):
            ps = mmS[fo % 2]
            bk.mm_group([(lambda e, fo=fo, kt=kt, ps=ps: e.matmul(
                ps.ap[:], wo.ap[:, kt * 1024 + fo * 128:
                               kt * 1024 + (fo + 1) * 128],
                a_b.ap[:, kt * QTC:(kt + 1) * QTC],
                start=(kt == 0), stop=(kt == PT - 1))) for kt in range(PT)],
                r=[wo, a_b], w=[ps])
            bk.do(bk.V, lambda e, fo=fo, ps=ps: e.tensor_scalar(
                a_f.ap[:, fo * QTC:(fo + 1) * QTC], ps.ap[:],
                boc.ap[:, fo:fo + 1], None, ALU.add),
                r=[ps, boc], w=[a_f] if fo == PT - 1 else (),
                after=a_f.dep_write() if fo == 0 else ())
        for ft in range(PT):
            sl = (slice(None), slice(ft * QTC, (ft + 1) * QTC))
            bk.do(bk.V, lambda e, sl=sl: e.tensor_tensor(
                q_ch.ap[sl], a_f.ap[sl], xnc.ap[sl], ALU.add),
                r=[a_f, xnc], w=[q_ch] if ft == PT - 1 else (),
                after=q_ch.dep_write() if ft == 0 else ())
        bk.do(bk.Y, lambda e, c0=c0: e.dma_start(
            out=chunk_view(ymid_h.ap, c0, QTC), in_=sb3(q_ch.ap, QTC)),
            r=[q_ch], w=[ymid_h] if ch == NQB - 1 else (), dma=True,
            after=ymid_h.dep_write() if ch == 0 else ())
        bk.do(bk.G, lambda e, c0=c0: e.dma_start(
            out=chunk_view(acc_h.ap, c0, QTC), in_=sb3(a_f.ap, QTC),
            accum_op=ALU.add), r=[a_f], w=accp(c0, QTC), dma=True)
    if dbg and 'ymid' in dbg:
        bk.do(bk.G, lambda e: e.dma_start(out=dbg['ymid'], in_=ymid_h.ap[:]),
              r=[ymid_h], dma=True)
    bk.end_scope()

    bk.scope()
    lng = LNScratch(bk, "fg")
    wg = load_lhsT(bk, "f_wg", W['L0_ffn_Wg'], PT, 8)
    bg8 = bk.buf("f_bg8", [8, 1], F32)
    bk.do(bk.Y, lambda e: e.dma_start(out=bg8.ap[:],
                                      in_=W['L0_ffn_bg'].unsqueeze(1)),
          w=[bg8], dma=True)
    ymc = bk.buf("f_ymc", [128, PT * TCB], BF16)
    ge = bk.buf("f_ge", [8, TCB], BF16)
    rzg = bk.buf("f_rzg", [128, TCB], F32)
    gps = bk.pbuf("f_gps", [8, TCB], F32)
    zgps = bk.pbuf("f_zgps", [128, TCB], F32)
    for ch in range(NCB):
        c0 = ch * TCB
        bk.do(bk.Y, lambda e, c0=c0: e.dma_start(
            out=sb3(ymc.ap), in_=chunk_view(ymid_h.ap, c0, TCB)),
            r=[ymid_h], w=[ymc], dma=True)
        bk.mm_group([(lambda e, kt=kt: e.matmul(
            gps.ap[:], wg.ap[:, kt * 8:(kt + 1) * 8],
            ymc.ap[:, kt * TCB:(kt + 1) * TCB],
            start=(kt == 0), stop=(kt == PT - 1))) for kt in range(PT)],
            r=[wg, ymc], w=[gps])
        bk.do(bk.S, lambda e: e.activation(
            ge.ap[:], gps.ap[:], AF.Exp, bias=bg8.ap[:, 0:1]),
            r=[gps, bg8], w=[ge])
        bk.mm_group([lambda e: e.matmul(
            zgps.ap[:], ones_b.ap[0:8, :], ge.ap[:],
            start=True, stop=True)], r=[ge, ones_b], w=[zgps])
        bk.do(bk.V, lambda e: e.reciprocal(rzg.ap[:], zgps.ap[:]),
              r=[zgps], w=[rzg])
        bk.do(bk.V, lambda e, c0=c0: e.tensor_tensor(
            gnb.ap[:, c0:c0 + TCB], ge.ap[:], rzg.ap[0:8, :], ALU.mult),
            r=[ge, rzg], w=[gnb] if ch == NCB - 1 else (),
            after=gnb.dep_write() if ch == 0 else ())
    if dbg and 'gnb' in dbg:
        bk.do(bk.G, lambda e: e.dma_start(out=dbg['gnb'], in_=gnb.ap[:]),
              r=[gnb], dma=True)
    bk.end_scope()

    TCM = 512
    NCM = T // TCM
    bk.scope()
    b2sb = bk.buf("f_b2sb", [8, 1024], BF16)
    bk.do(bk.G, lambda e: e.dma_start(out=b2sb.ap[:], in_=W['L0_ffn_b2'][:]),
          w=[b2sb], dma=True)
    ymc3 = bk.buf("f_ymc33", [128, PT * TCM], BF16)
    grep = bk.buf("f_grep", [128, TCM], BF16)
    hgb = bk.buf("f_hgb", [128, 32 * TCM], BF16)
    htmp = bk.buf("f_htmp", [128, TCM], BF16)
    stg = bk.buf("f_stg", [128, PT * TCM], F32)
    mmM = [bk.pbuf(f"f_mmM{i}", [128, TCM], F32) for i in range(2)]
    gbps = bk.pbuf("f_gbps", [128, TCM], F32)
    for ex in range(8):
        bk.scope()
        w1e = load_lhsT(bk, f"f_w1_{ex}", W['L0_ffn_W1'][ex], PT, 4096)
        w2e = load_lhsT(bk, f"f_w2_{ex}", W['L0_ffn_W2'][ex], 32, 1024)
        b1c = load_cols(bk, f"f_b1_{ex}", W['L0_ffn_b1'][ex], 32)
        for ch in range(NCM):
            c0 = ch * TCM
            bk.do(bk.Y, lambda e, c0=c0: e.dma_start(
                out=sb3(ymc3.ap, TCM), in_=chunk_view(ymid_h.ap, c0, TCM)),
                r=[ymid_h], w=[ymc3], dma=True)
            bk.mm_group([lambda e, ex=ex, c0=c0: e.matmul(
                gbps.ap[:], e8.ap[:, ex * 128:(ex + 1) * 128],
                gnb.ap[:, c0:c0 + TCM], start=True, stop=True)],
                r=[e8, gnb], w=[gbps])
            bk.do(bk.V, lambda e: e.tensor_copy(grep.ap[:], gbps.ap[:]),
                  r=[gbps], w=[grep])
            for fo in range(32):
                ps = mmM[fo % 2]
                bk.mm_group([(lambda e, fo=fo, kt=kt, ps=ps: e.matmul(
                    ps.ap[:], w1e.ap[:, kt * 4096 + fo * 128:
                                    kt * 4096 + (fo + 1) * 128],
                    ymc3.ap[:, kt * TCM:(kt + 1) * TCM],
                    start=(kt == 0), stop=(kt == PT - 1)))
                    for kt in range(PT)], r=[w1e, ymc3], w=[ps])
                bk.do(bk.S, lambda e, fo=fo, ps=ps: e.activation(
                    htmp.ap[:], ps.ap[:], AF.Relu,
                    bias=b1c.ap[:, fo:fo + 1]), r=[ps, b1c], w=[htmp])
                bk.do(bk.V, lambda e, fo=fo: e.tensor_tensor(
                    hgb.ap[:, fo * TCM:(fo + 1) * TCM], htmp.ap[:],
                    grep.ap[:], ALU.mult), r=[htmp, grep],
                    w=[hgb] if fo == 31 else (),
                    after=hgb.dep_write() if fo == 0 else ())
            for fo in range(PT):
                ps = mmM[fo % 2]
                fns = [(lambda e, fo=fo, kt=kt, ps=ps: e.matmul(
                    ps.ap[:], w2e.ap[:, kt * 1024 + fo * 128:
                                    kt * 1024 + (fo + 1) * 128],
                    hgb.ap[:, kt * TCM:(kt + 1) * TCM],
                    start=(kt == 0), stop=(kt == 31 and ex != 0)))
                    for kt in range(32)]
                if ex == 0:
                    fns.append(lambda e, fo=fo, ps=ps, c0=c0: e.matmul(
                        ps.ap[:], b2sb.ap[:, fo * 128:(fo + 1) * 128],
                        gnb.ap[:, c0:c0 + TCM], start=False, stop=True))
                bk.mm_group(fns, r=[w2e, hgb, b2sb, gnb], w=[ps])
                bk.do(bk.V, lambda e, fo=fo, ps=ps: e.tensor_copy(
                    stg.ap[:, fo * TCM:(fo + 1) * TCM], ps.ap[:]),
                    r=[ps], w=[stg] if fo == PT - 1 else (),
                    after=stg.dep_write() if fo == 0 else ())
            bk.do(bk.G, lambda e, c0=c0: e.dma_start(
                out=chunk_view(acc_h.ap, c0, TCM), in_=sb3(stg.ap, TCM),
                accum_op=ALU.add), r=[stg], w=accp(c0, TCM), dma=True)
        bk.end_scope()
    bk.end_scope()

    bk.scope()
    ln4 = LNScratch(bk, "f4")
    bo_f = bk.buf("f_bof", [128, PT * TCB], F32)
    bo_b = bk.buf("f_bob", [128, PT * TCB], BF16)
    xc4 = bk.buf("f_xc4", [128, PT * TCB], F32)
    tmp4 = bk.buf("f_tmp4", [128, PT * TCB], F32)
    out_toks = []
    for ch in range(NCB):
        c0 = ch * TCB
        bk.do(bk.Y, lambda e, c0=c0: e.dma_start(
            out=sb3(bo_f.ap), in_=chunk_view(acc_h.ap, c0, TCB)),
            r=accp(c0, TCB), w=[bo_f], dma=True)
        for ft in range(PT):
            sl = (slice(None), slice(ft * TCB, (ft + 1) * TCB))
            bk.do(bk.V, lambda e, sl=sl: e.tensor_copy(bo_b.ap[sl],
                                                       bo_f.ap[sl]),
                  r=[bo_f], w=[bo_b] if ft == PT - 1 else (),
                  after=bo_b.dep_write() if ft == 0 else ())
        emit_ln_stats(bk, ln4, ones_b, bo_b)
        bk.do(bk.Y, lambda e, c0=c0: e.dma_start(
            out=sb3(xc4.ap), in_=chunk_view(x_in, c0, TCB)),
            w=[xc4], dma=True)
        emit_ln_apply(bk, ln4, bo_f, tmp4, tmp4, res=xc4)
        out_toks.append(bk.do(bk.Y, lambda e, c0=c0: e.dma_start(
            out=chunk_view(x_out, c0, TCB), in_=sb3(tmp4.ap)),
            r=[tmp4, xc4], dma=True))
    bk.end_scope(barrier=False)
    bk.end_scope(barrier=False)
    return out_toks



def emit_wT(bk: BK, name, w_hbm, ident: Buf, nt=8):
    wt = bk.buf(name + "_t", [128, nt * nt * 128], BF16)
    bk.scope()
    wsb = bk.buf(name + "_w", [128, nt * nt * 128], BF16)
    tps = bk.pbuf(name + "_ps", [128, 128], BF16)
    for i in range(nt):
        bk.do(bk.G, lambda e, i=i: e.dma_start(
            out=wsb.ap[:, i * nt * 128:(i + 1) * nt * 128],
            in_=w_hbm[i * 128:(i + 1) * 128, :]),
            w=[wsb] if i == nt - 1 else (), dma=True,
            after=wsb.dep_write() if i == 0 else ())
    for i in range(nt):
        for j in range(nt):
            bk.mm_group([lambda e, i=i, j=j: e.transpose(
                tps.ap[:], wsb.ap[:, i * nt * 128 + j * 128:
                                  i * nt * 128 + (j + 1) * 128],
                ident.ap[:])], r=[wsb, ident], w=[tps])
            bk.do(bk.V, lambda e, i=i, j=j: e.tensor_copy(
                wt.ap[:, j * nt * 128 + i * 128: j * nt * 128 + (i + 1) * 128],
                tps.ap[:]), r=[tps],
                w=[wt] if (i, j) == (nt - 1, nt - 1) else (),
                after=wt.dep_write() if (i, j) == (0, 0) else ())
    bk.end_scope()
    return wt


def load_pcol(bk: BK, dst: Buf, vec_hbm, n=PT):
    for i in range(n):
        bk.do(bk.Y, lambda e, i=i: e.dma_start(
            out=dst.ap[:, i:i + 1],
            in_=vec_hbm[i * 128:(i + 1) * 128].unsqueeze(1)),
            w=[dst] if i == n - 1 else (), dma=True,
            after=dst.dep_write() if i == 0 else ())


def emit_l1_attn(bk: BK, W, consts, x_in, x_out, dbg=None):
    nc = bk.nc
    TCB = 256
    NCB = T // TCB
    TM = T // 4
    bk.scope()
    ln = LNScratch(bk)
    ones_b = bk.buf("ones_a", [128, 128], BF16)
    bk.do(bk.V, lambda e: e.memset(ones_b.ap[:], 1.0), w=[ones_b])
    ident = bk.buf("identa", [128, 128], BF16)
    bk.do(bk.G, lambda e: e.dma_start(out=ident.ap[:], in_=consts['ident'][:]),
          w=[ident], dma=True)
    xn_b = bk.buf("a_xnb", [128, PT * T], BF16)
    xc = bk.buf("a_xc", [128, PT * TCB], F32)
    xcb = bk.buf("a_xcb", [128, PT * TCB], BF16)
    tmp = bk.buf("a_tmp", [128, PT * TCB], F32)
    kv2 = bk.buf("a_kv2", [128, 8 * 64], BF16)
    k_tm = bk.buf("a_ktm", [128, 8 * 1024], BF16)
    v_tm = bk.buf("a_vtm", [128, 8 * 1024], BF16)
    mmps = [bk.pbuf(f"a_mm{i}", [128, TC], F32) for i in range(2)]

    for ch in range(NCB):
        c0 = ch * TCB
        bk.do(bk.Y, lambda e, c0=c0: e.dma_start(
            out=sb3(xc.ap, TCB), in_=chunk_view(x_in, c0, TCB)),
            w=[xc], dma=True)
        for ft in range(PT):
            sl = (slice(None), slice(ft * TCB, (ft + 1) * TCB))
            bk.do(bk.V, lambda e, sl=sl: e.tensor_copy(xcb.ap[sl], xc.ap[sl]),
                  r=[xc], w=[xcb] if ft == PT - 1 else (),
                  after=xcb.dep_write() if ft == 0 else ())
        emit_ln_stats(bk, ln, ones_b, xcb, tc=TCB)
        for ft in range(PT):
            sl = (slice(None), slice(ft * TCB, (ft + 1) * TCB))
            bk.do(bk.V, lambda e, sl=sl: e.tensor_tensor(
                tmp.ap[sl], xc.ap[sl], ln.r.ap[:, 0:TCB], ALU.mult),
                r=[xc, ln.r], w=[tmp] if ft == PT - 1 else (),
                after=tmp.dep_write() if ft == 0 else ())
        for ft in range(PT):
            dsl = (slice(None), slice(ft * T + c0, ft * T + c0 + TCB))
            sl = (slice(None), slice(ft * TCB, (ft + 1) * TCB))
            bk.do(bk.V, lambda e, sl=sl, dsl=dsl: e.tensor_tensor(
                xn_b.ap[dsl], tmp.ap[sl], ln.b2.ap[:, 0:TCB], ALU.add),
                r=[tmp, ln.b2], w=[xn_b] if (ch, ft) == (NCB - 1, PT - 1) else (),
                after=xn_b.dep_write() if (ch, ft) == (0, 0) else ())

    if dbg and 'xnb' in dbg:
        bk.do(bk.Y, lambda e: e.dma_start(out=dbg['xnb'], in_=xn_b.ap[:]),
              r=[xn_b], dma=True)
    bk.scope()
    xr = bk.buf("a_xr", [128, PT * TM], BF16)
    xr1 = bk.buf("a_xr1", [128, TM], F32)
    xr2 = bk.buf("a_xr2", [128, TM], F32)
    bnc = bk.buf("a_bn", [128, PT * 4], F32)
    for k, key in enumerate(('bn_gamma', 'bn_beta', 'bn_mean', 'bn_var')):
        for i in range(PT):
            bk.do(bk.Y, lambda e, i=i, k=k, key=key: e.dma_start(
                out=bnc.ap[:, k * PT + i:k * PT + i + 1],
                in_=W['L1_attn_' + key][i * 128:(i + 1) * 128].unsqueeze(1)),
                w=[bnc] if (k, i) == (3, PT - 1) else (), dma=True,
                after=bnc.dep_write() if (k, i) == (0, 0) else ())
    bns = bk.buf("a_bns", [128, PT], F32)
    bnb = bk.buf("a_bnb", [128, PT], F32)
    bnt = bk.buf("a_bnt", [128, PT], F32)
    epsb6 = bk.const_col(1e-6)
    bk.do(bk.S, lambda e: e.activation(
        bnt.ap[:], bnc.ap[:, 3 * PT:4 * PT], AF.Ln, bias=epsb6.ap[:, 0:1]),
        r=[bnc, epsb6], w=[bnt])
    bk.do(bk.S, lambda e: e.activation(
        bns.ap[:], bnt.ap[:], AF.Exp, scale=-0.5), r=[bnt], w=[bns],
        drain=True)
    bk.do(bk.V, lambda e: e.tensor_tensor(
        bns.ap[:], bns.ap[:], bnc.ap[:, 0:PT], ALU.mult), r=[bnc], w=[bns])
    bk.do(bk.V, lambda e: e.scalar_tensor_tensor(
        bnt.ap[:], bnc.ap[:, 2 * PT:3 * PT], -1.0, bns.ap[:],
        ALU.mult, ALU.mult), r=[bnc, bns], w=[bnt])
    bk.do(bk.V, lambda e: e.tensor_tensor(
        bnb.ap[:], bnt.ap[:], bnc.ap[:, PT:2 * PT], ALU.add),
        r=[bnt, bnc], w=[bnb])
    bk.do(bk.V, lambda e: e.tensor_scalar_mul(bns.ap[:], bns.ap[:], 0.25),
          w=[bns])
    for ft in range(PT):
        v4 = xn_b.ap[:, ft * T:(ft + 1) * T].rearrange("p (m f) -> p m f", f=4)
        bk.do(bk.V, lambda e, v4=v4: e.tensor_tensor(
            xr1.ap[:], v4[:, :, 0], v4[:, :, 1], ALU.add),
            r=[xn_b], w=[xr1])
        bk.do(bk.V, lambda e, v4=v4: e.tensor_tensor(
            xr2.ap[:], v4[:, :, 2], v4[:, :, 3], ALU.add),
            r=[xn_b], w=[xr2])
        bk.do(bk.V, lambda e: e.tensor_tensor(
            xr1.ap[:], xr1.ap[:], xr2.ap[:], ALU.add), r=[xr2], w=[xr1])
        if os.environ.get('MK_XR_RAW'):
            bk.do(bk.V, lambda e, ft=ft: e.tensor_copy(
                xr.ap[:, ft * TM:(ft + 1) * TM], xr1.ap[:]),
                r=[xr1], w=[xr] if ft == PT - 1 else (),
                after=xr.dep_write() if ft == 0 else ())
        else:
            bk.do(bk.V, lambda e, ft=ft: e.tensor_scalar(
                xr.ap[:, ft * TM:(ft + 1) * TM], xr1.ap[:],
                bns.ap[:, ft:ft + 1], bnb.ap[:, ft:ft + 1], ALU.mult, ALU.add),
                r=[xr1, bns, bnb], w=[xr] if ft == PT - 1 else (),
                after=xr.dep_write() if ft == 0 else ())

    if dbg and 'xr' in dbg:
        bk.do(bk.Y, lambda e: e.dma_start(out=dbg['xr'], in_=xr.ap[:]),
              r=[xr], dma=True)
    for name, key, dst in (("a_wk", 'L1_attn_Wk', k_tm),
                           ("a_wv", 'L1_attn_Wv', v_tm)):
        bk.scope()
        wt = emit_wT(bk, name, W[key], ident)
        for mt in range(8):
            for nc2 in range(2):
                ps = mmps[nc2 % 2]
                bk.mm_group([(lambda e, mt=mt, nc2=nc2, kt=kt, wt=wt, ps=ps:
                              e.matmul(ps.ap[:],
                                       xr.ap[:, kt * TM + mt * 128:
                                             kt * TM + (mt + 1) * 128],
                                       wt.ap[:, kt * 1024 + nc2 * 512:
                                             kt * 1024 + (nc2 + 1) * 512],
                                       start=(kt == 0), stop=(kt == PT - 1)))
                             for kt in range(PT)], r=[wt, xr], w=[ps])
                bk.do(bk.V, lambda e, mt=mt, nc2=nc2, dst=dst, ps=ps:
                      e.tensor_copy(dst.ap[:, mt * 1024 + nc2 * 512:
                                           mt * 1024 + (nc2 + 1) * 512],
                                    ps.ap[:]),
                      r=[ps], w=[dst] if (mt, nc2) == (7, 1) else (),
                      after=dst.dep_write() if (mt, nc2) == (0, 0) else ())
        bk.end_scope()

    if dbg and 'ktm' in dbg:
        bk.do(bk.Y, lambda e: e.dma_start(out=dbg['ktm'], in_=k_tm.ap[:]),
              r=[k_tm], dma=True)
    if dbg and 'vtm' in dbg:
        bk.do(bk.Y, lambda e: e.dma_start(out=dbg['vtm'], in_=v_tm.ap[:]),
              r=[v_tm], dma=True)
    kvp = bk.pbuf("a_kvp", [128, 64], F32)
    bk.do(bk.V, lambda e: e.memset(kv2.ap[:], 0.0), w=[kv2])
    SC = 32 ** -0.5
    for h in range(32):
        bk.mm_group([(lambda e, h=h, mt=mt: e.matmul(
            kvp.ap[0:32, 0:32],
            k_tm.ap[:, mt * 1024 + h * 32:mt * 1024 + (h + 1) * 32],
            v_tm.ap[:, mt * 1024 + h * 32:mt * 1024 + (h + 1) * 32],
            start=(mt == 0), stop=(mt == 7))) for mt in range(8)],
            r=[k_tm, v_tm], w=[kvp])
        hp, hh = divmod(h, 2)
        pb, cb = 64 * (hp % 2) + 32 * hh, 64 * (hp // 2) + 32 * hh
        bk.do(bk.V, lambda e, pb=pb, cb=cb: e.tensor_scalar_mul(
            kv2.ap[pb:pb + 32, cb:cb + 32], kvp.ap[0:32, 0:32], SC),
            r=[kvp], w=[kv2] if h == 31 else (),
            after=kv2.dep_write() if h == 0 else ())

    if dbg and 'kv2' in dbg:
        bk.do(bk.Y, lambda e: e.dma_start(out=dbg['kv2'], in_=kv2.ap[:]),
              r=[kv2], dma=True)
    bk.end_scope()

    bk.scope()
    wqT = emit_wT(bk, "a_wq", W['L1_attn_Wq'], ident)
    wpT = emit_wT(bk, "a_wp", W['L1_attn_Wp'], ident)
    bpc = bk.buf("a_bp", [128, PT], F32)
    load_pcol(bk, bpc, W['L1_attn_bp'])
    q_ch = bk.buf("a_qch", [128, PT * TCB], BF16)
    ao_ch = bk.buf("a_aoch", [128, PT * TCB], BF16)
    bo_f = bk.buf("a_bof", [128, PT * TCB], F32)
    bo_b = bk.buf("a_bob", [128, PT * TCB], BF16)
    out_toks = []
    for ch in range(NCB):
        c0 = ch * TCB
        for co in range(PT):
            ps = mmps[co % 2]
            bk.mm_group([(lambda e, co=co, kt=kt, ps=ps, c0=c0: e.matmul(
                ps.ap[:, 0:TCB], wqT.ap[:, kt * 1024 + co * 128:
                                        kt * 1024 + (co + 1) * 128],
                xn_b.ap[:, kt * T + c0: kt * T + c0 + TCB],
                start=(kt == 0), stop=(kt == PT - 1))) for kt in range(PT)],
                r=[wqT, xn_b], w=[ps])
            bk.do(bk.V, lambda e, co=co, ps=ps: e.tensor_copy(
                q_ch.ap[:, co * TCB:(co + 1) * TCB], ps.ap[:, 0:TCB]),
                r=[ps], w=[q_ch] if co == PT - 1 else (),
                after=q_ch.dep_write() if co == 0 else ())
        if dbg and 'qch' in dbg and ch == 0:
            bk.do(bk.Y, lambda e: e.dma_start(out=dbg['qch'], in_=q_ch.ap[:]),
                  r=[q_ch], dma=True)
        for pt in range(PT):
            ps = mmps[pt % 2]
            fns = []
            for half in (0, 1):
                hp = 2 * pt + half
                pb, cb = 64 * (hp % 2), 64 * (hp // 2)
                fns.append(lambda e, pt=pt, half=half, pb=pb, cb=cb, ps=ps:
                           e.matmul(ps.ap[64 * half:64 * half + 64, 0:TCB],
                                    kv2.ap[pb:pb + 64, cb:cb + 64],
                                    q_ch.ap[64 * half:64 * half + 64,
                                            pt * TCB:(pt + 1) * TCB],
                                    start=True, stop=True))
            bk.mm_group(fns, r=[kv2, q_ch], w=[ps])
            bk.do(bk.V, lambda e, pt=pt, ps=ps: e.tensor_copy(
                ao_ch.ap[:, pt * TCB:(pt + 1) * TCB], ps.ap[:, 0:TCB]),
                r=[ps], w=[ao_ch] if pt == PT - 1 else (),
                after=ao_ch.dep_write() if pt == 0 else ())
        if dbg and 'aoch' in dbg and ch == 0:
            bk.do(bk.Y, lambda e: e.dma_start(out=dbg['aoch'], in_=ao_ch.ap[:]),
                  r=[ao_ch], dma=True)
        for co in range(PT):
            ps = mmps[co % 2]
            bk.mm_group([(lambda e, co=co, kt=kt, ps=ps: e.matmul(
                ps.ap[:, 0:TCB], wpT.ap[:, kt * 1024 + co * 128:
                                        kt * 1024 + (co + 1) * 128],
                ao_ch.ap[:, kt * TCB:(kt + 1) * TCB],
                start=(kt == 0), stop=(kt == PT - 1))) for kt in range(PT)],
                r=[wpT, ao_ch], w=[ps])
            bk.do(bk.V, lambda e, co=co, ps=ps: e.tensor_scalar_add(
                bo_f.ap[:, co * TCB:(co + 1) * TCB], ps.ap[:, 0:TCB],
                bpc.ap[:, co:co + 1]),
                r=[ps, bpc], w=[bo_f] if co == PT - 1 else (),
                after=bo_f.dep_write() if co == 0 else ())
        for ft in range(PT):
            sl = (slice(None), slice(ft * TCB, (ft + 1) * TCB))
            bk.do(bk.V, lambda e, sl=sl: e.tensor_copy(bo_b.ap[sl],
                                                       bo_f.ap[sl]),
                  r=[bo_f], w=[bo_b] if ft == PT - 1 else (),
                  after=bo_b.dep_write() if ft == 0 else ())
        bk.do(bk.Y, lambda e, c0=c0: e.dma_start(
            out=sb3(xc.ap, TCB), in_=chunk_view(x_in, c0, TCB)),
            w=[xc], dma=True)
        emit_ln_stats(bk, ln, ones_b, bo_b, tc=TCB)
        emit_ln_apply(bk, ln, bo_f, tmp, tmp, tc=TCB, res=xc)
        out_toks.append(bk.do(bk.Y, lambda e, c0=c0: e.dma_start(
            out=chunk_view(x_out, c0, TCB), in_=sb3(tmp.ap, TCB)),
            r=[tmp, xc], dma=True))
    bk.end_scope(barrier=False)
    bk.end_scope(barrier=False)
    return out_toks



def hbm_inputs(nc, names_shapes):
    return {name: nc.dram_tensor(name, list(shape), F32,
                                 kind="ExternalInput").ap()
            for name, shape in names_shapes}


IN_SHAPES = {
    'L1_ffn_G1_R': (1, 32, 4, 32), 'L1_ffn_G2_R': (4, 16, 1, 16),
    'L1_ffn_G1_I': (1, 32, 4, 32), 'L1_ffn_G2_I': (4, 16, 1, 16),
    'L1_ffn_bias_real': (512,), 'L1_ffn_bias_imag': (512,),
}


L0A_SHAPES = {
    'L0_attn_cpe1_w': (1024, 1, 3), 'L0_attn_cpe1_b': (1024,),
    'L0_attn_norm1_g': (1024,), 'L0_attn_norm1_b': (1024,),
    'L0_attn_in_w': (1024, 1024), 'L0_attn_in_b': (1024,),
    'L0_attn_actp_w': (1024, 1024), 'L0_attn_actp_b': (1024,),
    'L0_attn_dwc_w': (1024, 1, 3), 'L0_attn_dwc_b': (1024,),
    'L0_attn_qk_w': (1024, 2048), 'L0_attn_qk_b': (2048,),
    'L0_attn_lepe_w': (1024, 1, 3), 'L0_attn_lepe_b': (1024,),
    'L0_attn_out_w': (1024, 1024), 'L0_attn_out_b': (1024,),
    'L0_attn_cpe2_w': (1024, 1, 3), 'L0_attn_cpe2_b': (1024,),
    'L0_attn_norm2_g': (1024,), 'L0_attn_norm2_b': (1024,),
    'L0_attn_fc1_w': (1024, 4096), 'L0_attn_fc1_b': (4096,),
    'L0_attn_fc2_w': (4096, 1024), 'L0_attn_fc2_b': (1024,),
}


L0F_SHAPES = {
    'L0_ffn_Wq': (1024, 1024), 'L0_ffn_bq': (1024,),
    'L0_ffn_Wk': (1024, 1024), 'L0_ffn_bk': (1024,),
    'L0_ffn_Wv': (1024, 1024), 'L0_ffn_bv': (1024,),
    'L0_ffn_Wo': (1024, 1024), 'L0_ffn_bo': (1024,),
    'L0_ffn_Wg': (1024, 8), 'L0_ffn_bg': (8,),
    'L0_ffn_W1': (8, 1024, 4096), 'L0_ffn_b1': (8, 4096),
    'L0_ffn_W2': (8, 4096, 1024), 'L0_ffn_b2': (8, 1024),
}


def build_l0f_only():
    bk = BK()
    nc = bk.nc
    x_in = nc.dram_tensor("x", [C, T], F32, kind="ExternalInput").ap()
    x_out = nc.dram_tensor("out", [C, T], F32, kind="ExternalOutput").ap()
    consts = {'e8': nc.dram_tensor("e8", [8, 1024], F32,
                                   kind="ExternalInput").ap()}
    W = hbm_inputs(nc, list(L0F_SHAPES.items()))
    toks = emit_l0_ffn(bk, W, consts, x_in, x_out)
    for t in toks:
        bk.Y.wait(t)
    bk.barrier()
    return bk.finalize()


def build_l0a_only():
    bk = BK()
    nc = bk.nc
    x_in = nc.dram_tensor("x", [C, T], F32, kind="ExternalInput").ap()
    x_out = nc.dram_tensor("out", [C, T], F32, kind="ExternalOutput").ap()
    consts = {'ident': nc.dram_tensor("ident", [128, 128], F32,
                                      kind="ExternalInput").ap(),
              'ind64': nc.dram_tensor("ind64", [128, 128], F32,
                                      kind="ExternalInput").ap()}
    W = hbm_inputs(nc, list(L0A_SHAPES.items()))
    import os as _os
    dbg = None
    if _os.environ.get('MK_L0A_DBG'):
        sel = _os.environ['MK_L0A_DBG'].split(',')
        if sel == ['1']:
            sel = ['xn', 'y', 'xi0', 'act', 'q', 'y2']
        dbg = {}
        for nm in sel:
            if nm == 'qq':
                dbg[nm] = nc.dram_tensor("dbg_qq", [C, T + 128], BF16,
                                         kind="ExternalOutput").ap()
            elif nm in ('afd', 'af0d', 'ltd'):
                dbg[nm] = nc.dram_tensor("dbg_" + nm, [128, 64 * 512 + 64],
                                         F32, kind="ExternalOutput").ap()
            elif nm == 'zrd':
                dbg[nm] = nc.dram_tensor("dbg_zrd", [128, 8 * T + 64], F32,
                                         kind="ExternalOutput").ap()
            elif nm == 'gat':
                dbg[nm] = nc.dram_tensor("dbg_gat", [128, 8 * T + 64], BF16,
                                         kind="ExternalOutput").ap()
            else:
                dbg[nm] = nc.dram_tensor("dbg_" + nm, [C, T],
                                         BF16 if nm in ('act',) else F32,
                                         kind="ExternalOutput").ap()
    toks = emit_l0_attn(bk, W, consts, x_in, x_out, dbg=dbg)
    for t in toks:
        bk.Y.wait(t)
    bk.barrier()
    return bk.finalize()


def build_l1a_only():
    bk = BK()
    nc = bk.nc
    x_in = nc.dram_tensor("x", [C, T], F32, kind="ExternalInput").ap()
    x_out = nc.dram_tensor("out", [C, T], F32, kind="ExternalOutput").ap()
    consts = {'ident': nc.dram_tensor("ident", [128, 128], F32,
                                      kind="ExternalInput").ap()}
    names = ['L1_attn_Wq', 'L1_attn_Wk', 'L1_attn_Wv', 'L1_attn_Wp',
             'L1_attn_bp', 'L1_attn_bn_gamma', 'L1_attn_bn_beta',
             'L1_attn_bn_mean', 'L1_attn_bn_var']
    shapes = {n: ((1024, 1024) if n.endswith(('Wq', 'Wk', 'Wv', 'Wp'))
                  else (1024,)) for n in names}
    W = hbm_inputs(nc, [(n, shapes[n]) for n in names])
    toks = emit_l1_attn(bk, W, consts, x_in, x_out)
    for t in toks:
        bk.Y.wait(t)
    bk.barrier()
    return bk.finalize()


def build_l1f_only():
    bk = BK()
    nc = bk.nc
    x_in = nc.dram_tensor("x", [C, T], F32, kind="ExternalInput").ap()
    x_out = nc.dram_tensor("out", [C, T], F32, kind="ExternalOutput").ap()
    consts = {'ident': nc.dram_tensor("ident", [128, 128], F32,
                                      kind="ExternalInput").ap()}
    W = hbm_inputs(nc, [(k, v) for k, v in IN_SHAPES.items()
                        if k.startswith('L1_ffn')])
    toks = emit_l1_ffn(bk, W, consts, x_in, x_out)
    for t in toks:
        bk.Y.wait(t)
    bk.barrier()
    return bk.finalize()


L1A_NAMES = ['L1_attn_Wq', 'L1_attn_Wk', 'L1_attn_Wv', 'L1_attn_Wp',
             'L1_attn_bp', 'L1_attn_bn_gamma', 'L1_attn_bn_beta',
             'L1_attn_bn_mean', 'L1_attn_bn_var']
L1F_NAMES = list(IN_SHAPES)


def host_consts():
    ident = np.eye(128, dtype=np.float32)
    ind64 = np.kron(np.eye(2, dtype=np.float32),
                    np.ones((64, 64), np.float32))
    e8 = np.zeros((8, 1024), np.float32)
    for e in range(8):
        e8[e, e * 128:(e + 1) * 128] = 1.0
    return ident, ind64, e8


L1A_SHAPES = {n: ((1024, 1024) if n.endswith(('Wq', 'Wk', 'Wv', 'Wp'))
                  else (1024,)) for n in L1A_NAMES}


def build_fused():
    bk = BK()
    nc = bk.nc
    x_in = nc.dram_tensor("x", [C, T], F32, kind="ExternalInput").ap()
    x_out = nc.dram_tensor("out", [C, T], F32, kind="ExternalOutput").ap()
    ident = nc.dram_tensor("ident", [128, 128], F32,
                           kind="ExternalInput").ap()
    ind64 = nc.dram_tensor("ind64", [128, 128], F32,
                           kind="ExternalInput").ap()
    e8 = nc.dram_tensor("e8", [8, 1024], F32, kind="ExternalInput").ap()
    W = hbm_inputs(nc, list(L0A_SHAPES.items()) + list(L0F_SHAPES.items())
                   + list(L1A_SHAPES.items()) + list(IN_SHAPES.items()))
    x1 = nc.dram_tensor("mk_x1", [C, T], F32, kind="Internal").ap()
    x2 = nc.dram_tensor("mk_x2", [C, T], F32, kind="Internal").ap()
    x3 = x1
    for emit, xi, xo, cn in (
            (emit_l0_attn, x_in, x1, {'ident': ident, 'ind64': ind64}),
            (emit_l0_ffn, x1, x2, {'e8': e8}),
            (emit_l1_attn, x2, x3, {'ident': ident}),
            (emit_l1_ffn, x3, x_out, {'ident': ident})):
        toks = emit(bk, W, cn, xi, xo)
        for t in toks:
            bk.Y.wait(t)
        bk.barrier()
    return bk.finalize()


ALL_NAMES = (list(L0A_SHAPES) + list(L0F_SHAPES) + list(L1A_SHAPES)
             + list(IN_SHAPES))


def kernel(**inputs):
    ident, ind64, e8 = host_consts()
    x = np.asarray(inputs['L0_attn_x'])
    cur = [np.ascontiguousarray(x[c].T) for c in range(N_CORES)]
    if os.environ.get('MK_UNFUSED'):
        stages = [
            (build_l0a_only, list(L0A_SHAPES),
             {'ident': ident, 'ind64': ind64}),
            (build_l0f_only, list(L0F_SHAPES), {'e8': e8}),
            (build_l1a_only, L1A_NAMES, {'ident': ident}),
            (build_l1f_only, L1F_NAMES, {'ident': ident}),
        ]
        for builder, names, consts in stages:
            nc = builder()
            wmap = {n: np.ascontiguousarray(np.asarray(inputs[n]))
                    for n in names}
            in_maps = [{'x': cur[c], **consts, **wmap}
                       for c in range(N_CORES)]
            res = run_bass_kernel_spmd(nc, in_maps,
                                       core_ids=list(range(N_CORES)))
            cur = [np.ascontiguousarray(res.results[c]['out'])
                   for c in range(N_CORES)]
        return np.stack([cur[c].T for c in range(N_CORES)], 0)
    nc = build_fused()
    wmap = {n: np.ascontiguousarray(np.asarray(inputs[n]))
            for n in ALL_NAMES}
    consts = {'ident': ident, 'ind64': ind64, 'e8': e8}
    in_maps = [{'x': cur[c], **consts, **wmap} for c in range(N_CORES)]
    res = run_bass_kernel_spmd(nc, in_maps, core_ids=list(range(N_CORES)))
    return np.stack([np.ascontiguousarray(res.results[c]['out']).T
                     for c in range(N_CORES)], 0)
